# version 03
# baseline (speedup 1.0000x reference)
import numpy as np
import ml_dtypes
import concourse.bass as bass
import concourse.mybir as mybir
from concourse.bass_utils import run_bass_kernel_spmd

F32 = mybir.dt.float32
BF16 = mybir.dt.bfloat16
AF = mybir.ActivationFunctionType
ALU = mybir.AluOpType
AX = mybir.AxisListType

B, T, D = 2, 4096, 2048
BT = B * T
CCH = 128
NCH = T // CCH
TSH = BT // 8
EPS = 1e-6
LNEPS = 1e-5
NEGINF = -30000.0
BF = ml_dtypes.bfloat16
LN128 = float(np.log(128.0))


def _hadamard(n):
    h = np.array([[1.0]])
    while h.shape[0] < n:
        h = np.block([[h, h], [h, -h]])
    return h


ENGS = ("sync", "scalar", "vector", "tensor", "gpsimd")


class Sched:

    NSLOT = 8

    def __init__(self, nc):
        self.nc = nc
        self.ops = {e: [] for e in ENGS}
        self.cnt = {e: 0 for e in ENGS}
        self.csem = {e: nc.alloc_semaphore(name=f"cs_{e}") for e in ENGS}
        self.waited = {e: {} for e in ENGS}
        self.dsem = {}
        self.dcnt = {}
        self.drr = {}
        for q in ("sync", "scalar", "vector", "gpsimd"):
            self.drr[q] = 0
            for i in range(self.NSLOT):
                self.dsem[(q, i)] = nc.alloc_semaphore(name=f"d_{q}{i}")
                self.dcnt[(q, i)] = 0
        self.n_cc = 0
        self.serial_last = ()
        self._grp = None

    SKIP_SAME = False
    PRUNE = False

    def _waits(self, eng, deps):
        need = {}
        own = self.csem[eng]
        for t in deps:
            if t is None:
                continue
            sem, val = t
            if self.SKIP_SAME and sem is own:
                continue
            k = sem.num
            if k not in need or need[k][1] < val:
                need[k] = (sem, val)
        out = []
        w = self.waited[eng]
        for k, (sem, val) in need.items():
            if self.PRUNE and w.get(k, -1) >= val:
                continue
            w[k] = val
            out.append((sem, val))
        return out

    def op(self, eng, fn, deps=()):
        waits = self._waits(eng, deps)
        self.cnt[eng] += 1
        tok = (self.csem[eng], self.cnt[eng])
        self.ops[eng].append((waits, fn, tok[0], 1))
        return tok

    def dma(self, q, fn, deps=()):
        slot = self.drr[q] % self.NSLOT
        self.drr[q] += 1
        sem = self.dsem[(q, slot)]
        prev = self.dcnt[(q, slot)]
        deps = list(deps)
        if prev:
            deps.append((sem, prev))
        waits = self._waits(q, deps)
        self.dcnt[(q, slot)] = prev + 16
        self.ops[q].append((waits, fn, sem, 16))
        return (sem, prev + 16)

    def cc(self, fn, deps=()):
        sem = self.nc.alloc_semaphore(name=f"ccs{self.n_cc}")
        self.n_cc += 1
        waits = self._waits("gpsimd", deps)
        self.ops["gpsimd"].append((waits, fn, sem, 1))
        return (sem, 1)

    def C(self, eng, fn, kind="comp"):
        deps = self.serial_last
        if kind == "dma":
            tok = self.dma(eng, fn, deps)
        elif kind == "cc":
            tok = self.cc(fn, deps)
        else:
            tok = self.op(eng, fn, deps)
        if self._grp is None:
            self.serial_last = (tok,)
        else:
            self._grp.append(tok)
        return tok

    def group(self):
        sched = self

        class G:
            def __enter__(self):
                sched._grp = []

            def __exit__(self, *a):
                if sched._grp:
                    sched.serial_last = tuple(sched._grp)
                sched._grp = None
                return False

        return G()

    def all_tokens(self):
        toks = [(self.csem[e], self.cnt[e]) for e in ENGS if self.cnt[e] > 0]
        for key, c in self.dcnt.items():
            if c > 0:
                toks.append((self.dsem[key], c))
        return tuple(toks)

    def emit(self, block, drain=()):
        fin = self._waits("sync", drain)
        for e in ENGS:
            ops = self.ops[e]
            tail = fin if e == "sync" else []

            def body(eng, ops=ops, tail=tail):
                for waits, fn, sem, inc in ops:
                    for s, v in waits:
                        eng.wait_ge(s, v)
                    inst = fn(eng)
                    if inst is not None and inc:
                        inst.then_inc(sem, inc)
                for s, v in tail:
                    eng.wait_ge(s, v)

            getattr(block, e)(body)


class Pool:

    def __init__(self, nc, name, nbytes):
        self.t = nc.alloc_sbuf_tensor(name, [128, nbytes], mybir.dt.uint8)
        self.off = 0
        self.peak = 0
        self.nbytes = nbytes

    def alloc(self, p, n, dtype):
        esz = mybir.dt.size(dtype)
        start = (self.off + 31) // 32 * 32
        self.off = start + n * esz
        self.peak = max(self.peak, self.off)
        assert self.off <= self.nbytes, f"pool overflow {self.off}"
        return self.t[0:p, start:start + n * esz].bitcast(dtype)

    def mark(self):
        return self.off

    def reset(self, m):
        self.off = m


def build(dbg=0):
    nc = bass.Bass("TRN2", target_bir_lowering=False)
    RG = [list(range(8))]

    x_sh = nc.dram_tensor("x_sh", [TSH, D], F32, kind="ExternalInput").ap()
    wq_sl = nc.dram_tensor("wq_sl", [D, 1024], F32, kind="ExternalInput").ap()
    wz_sl = nc.dram_tensor("wz_sl", [D, 512], F32, kind="ExternalInput").ap()
    wba_sl = nc.dram_tensor("wba_sl", [D, 8], F32, kind="ExternalInput").ap()
    cwT_sl = nc.dram_tensor("cwT_sl", [1024, 4], F32, kind="ExternalInput").ap()
    dta8 = nc.dram_tensor("dta8", [8, 1], F32, kind="ExternalInput").ap()
    alog8 = nc.dram_tensor("alog8", [8, 1], F32, kind="ExternalInput").ap()
    normw = nc.dram_tensor("normw", [128, 1], F32, kind="ExternalInput").ap()
    wo_sl = nc.dram_tensor("wo_sl", [512, D], F32, kind="ExternalInput").ap()
    w1_sl = nc.dram_tensor("w1_sl", [1024, D], F32, kind="ExternalInput").ap()
    w3_sl = nc.dram_tensor("w3_sl", [1024, D], F32, kind="ExternalInput").ap()
    w2_sl = nc.dram_tensor("w2_sl", [D, 1024], F32, kind="ExternalInput").ap()
    out_sh = nc.dram_tensor("out_sh", [TSH, D], F32, kind="ExternalOutput").ap()

    dram = nc.dram_tensor
    xln_sl = dram("xln_sl", [TSH, D], BF16).ap()
    ag_xln = dram("ag_xln", [BT, D], BF16).ap()
    w1b_sl = dram("w1b_sl", [1024, D], BF16).ap()
    w3b_sl = dram("w3b_sl", [1024, D], BF16).ap()
    w2b_sl = dram("w2b_sl", [D, 1024], BF16).ap()
    w2t_sl = dram("w2t_sl", [1024, D], BF16).ap()
    wob_sl = dram("wob_sl", [512, D], BF16).ap()
    ag_w1 = dram("ag_w1", [8192, D], BF16).ap()
    ag_w3 = dram("ag_w3", [8192, D], BF16).ap()
    ag_w2t = dram("ag_w2t", [8192, D], BF16).ap()

    qb_hbm = dram("qb_hbm", [256, BT], BF16).ap()
    kb_hbm = dram("kb_hbm", [256, BT], BF16).ap()
    vb_hbm = dram("vb_hbm", [512, BT], BF16).ap()
    zb_hbm = dram("zb_hbm", [512, BT], BF16).ap()
    lng_hbm = dram("lng_hbm", [8, BT], F32).ap()
    lr_hbm = dram("lr_hbm", [4, BT], F32).ap()
    glay = dram("glay", [256, 128], F32).ap()
    blay = dram("blay", [256, 128], F32).ap()
    lrklay = dram("lrklay", [256, 128], F32).ap()
    lrqlay = dram("lrqlay", [256, 128], F32).ap()
    evec_hbm = dram("evec_hbm", [3, 256, CCH], F32).ap()
    scl_hbm = dram("scl_hbm", [4, 256, CCH], BF16).ap()
    gcx_hbm = dram("gcx_hbm", [256, 1], F32).ap()
    gated_tm = dram("gated_tm", [BT, 512], BF16).ap()
    part_hbm = dram("part_hbm", [BT, D], BF16).ap()
    rs_out = dram("rs_out", [TSH, D], BF16).ap()
    x2_hbm = dram("x2_hbm", [TSH, D], F32).ap()
    x2ln_hbm = dram("x2ln_hbm", [TSH, D], BF16).ap()
    w1t_hbm = dram("w1t_hbm", [D, 8192], BF16).ap()
    w3t_hbm = dram("w3t_hbm", [D, 8192], BF16).ap()
    ybuf_hbm = dram("ybuf_hbm", [64, 128, TSH], BF16).ap()
    h2_hbm = dram("h2_hbm", [8, 2, 64, 64, 128], BF16).ap()

    dbg_aps = {}
    if dbg >= 1:
        dbg_aps["d_mix"] = dram("d_mix", [1024, BT], BF16, kind="ExternalOutput").ap()
        dbg_aps["d_lng"] = dram("d_lng", [8, BT], F32, kind="ExternalOutput").ap()
        dbg_aps["d_lr"] = dram("d_lr", [4, BT], F32, kind="ExternalOutput").ap()
        dbg_aps["d_xln"] = dram("d_xln", [BT, D], BF16, kind="ExternalOutput").ap()
        dbg_aps["d_z"] = dram("d_z", [512, BT], BF16, kind="ExternalOutput").ap()
    if dbg >= 2:
        dbg_aps["d_gat"] = dram("d_gat", [BT, 512], BF16, kind="ExternalOutput").ap()
    if dbg >= 3:
        dbg_aps["d_x2"] = dram("d_x2", [TSH, D], F32, kind="ExternalOutput").ap()

    ti = np.arange(CCH)
    maskL = np.where(ti[:, None] > ti[None, :], 0.0, NEGINF).astype(np.float32)
    maskU = maskL.T.copy()
    maskQ = np.where(ti[:, None] <= ti[None, :], -0.5 * LN128, NEGINF).astype(np.float32)
    c_maskL = nc.inline_tensor(maskL, name="c_maskL").ap()
    c_maskU = nc.inline_tensor(maskU, name="c_maskU").ap()
    c_maskQ = nc.inline_tensor(maskQ, name="c_maskQ").ap()
    c_idf = nc.inline_tensor(np.eye(128, dtype=np.float32), name="c_idf").ap()
    c_idb = nc.inline_tensor(np.eye(128, dtype=BF), name="c_idb").ap()
    c_onesb = nc.inline_tensor(np.ones((128, 1), dtype=BF), name="c_onesb").ap()
    c_onesf = nc.inline_tensor(np.ones((1, 128), dtype=np.float32), name="c_onesf").ap()
    c_sign8 = nc.inline_tensor(
        np.array([-1, -1, -1, -1, 1, 1, 1, 1], dtype=np.float32)[:, None], name="c_sign8").ap()
    c_h128 = nc.inline_tensor(_hadamard(128).astype(BF), name="c_h128").ap()
    c_h64 = nc.inline_tensor(_hadamard(64).astype(BF), name="c_h64").ap()
    c_bias = nc.inline_tensor(np.tile(np.array(
        [LNEPS, EPS, 1.0, -0.5 * LN128], dtype=np.float32), (128, 1)), name="c_bias").ap()

    sb = nc.alloc_sbuf_tensor
    cst = sb("cst", [128, 4 * 128 + 16 + 128 + 4], F32)
    idbt = sb("idbt", [128, 129], BF16)
    h128t = sb("h128t", [128, 128], BF16)
    h64t = sb("h64t", [64, 64], BF16)
    St = [sb(f"St{p}", [128, 128], F32) for p in range(8)]
    St2 = [sb(f"St2_{p}", [128, 128], F32) for p in range(8)]
    Sbt = [sb(f"Sbt{p}", [128, 128], BF16) for p in range(8)]
    sclT = sb("sclT", [128, 4 * 256], BF16)
    sclF = sb("sclF", [128, 4 * 256], F32)
    gcb = sb("gcb", [128, 256], F32)
    eops = sb("eops", [2, 5 * 1024], F32)

    MKL, MKU, MKQ = cst[:, 0:128], cst[:, 128:256], cst[:, 256:384]
    IDF = cst[:, 384:512]
    SGN8, DTA8, ALOG8 = cst[0:8, 512:513], cst[0:8, 513:514], cst[0:8, 514:515]
    NRMW = cst[0:128, 515:516]
    ONESF = cst[0:1, 528:656]
    BLN = cst[:, 656:657]
    BEP = cst[:, 657:658]
    BONE = cst[:, 658:659]
    BQM = cst[:, 659:660]
    IDB, ONEB = idbt[:, 0:128], idbt[:, 128:129]
    LA = eops[:, 0 * 1024:1 * 1024]
    LB = eops[:, 1 * 1024:2 * 1024]
    RB = eops[:, 2 * 1024:3 * 1024]
    RA = eops[:, 3 * 1024:4 * 1024]
    RQ = eops[:, 4 * 1024:5 * 1024]
    nrscT = sclF[:, 0:256]
    oscT = sclF[:, 256:512]
    dscT = sclF[:, 512:768]
    bexT = sclF[:, 768:1024]

    pool = Pool(nc, "pool", 167 * 1024)

    ps = nc.alloc_psum_tensor
    psBig = ps("psBig", [128, 1024], F32)
    psBig2 = ps("psBig2", [128, 1024], F32)
    psS1 = ps("psS1", [128, 512], F32)
    psA = ps("psA", [128, 512], F32)
    psB = ps("psB", [128, 512], F32)

    with nc.Block() as block:
        ch = Sched(nc)
        O = ch.C

        def ln_tile(xt, outb, scr, st, eps=None, out_f32=None, res=None):
            O("vector", lambda e: e.reduce_sum(out=st[:, 0:1], in_=xt, axis=AX.X))
            O("vector", lambda e: e.scalar_tensor_tensor(
                out=scr, in0=xt, scalar=1.0, in1=xt, op0=ALU.mult, op1=ALU.mult,
                accum_out=st[:, 1:2]))
            O("vector", lambda e: e.tensor_scalar(
                out=st[:, 2:3], in0=st[:, 0:1], scalar1=-1.0 / D, scalar2=None, op0=ALU.mult))
            O("vector", lambda e: e.tensor_tensor(
                out=st[:, 3:4], in0=st[:, 2:3], in1=st[:, 2:3], op=ALU.mult))
            O("vector", lambda e: e.scalar_tensor_tensor(
                out=st[:, 4:5], in0=st[:, 1:2], scalar=1.0 / D, in1=st[:, 3:4],
                op0=ALU.mult, op1=ALU.subtract))
            ba = eps if eps is not None else BLN
            O("scalar", lambda e, ba=ba: e.activation(out=st[:, 5:6], in_=st[:, 4:5], func=AF.Ln, bias=ba))
            O("scalar", lambda e: e.activation(out=st[:, 6:7], in_=st[:, 5:6], func=AF.Exp, scale=-0.5))
            if outb is not None:
                O("vector", lambda e: e.tensor_scalar(
                    out=outb, in0=xt, scalar1=st[:, 2:3], scalar2=st[:, 6:7],
                    op0=ALU.add, op1=ALU.mult))
            if res is not None:
                O("vector", lambda e: e.tensor_scalar(
                    out=scr, in0=xt, scalar1=st[:, 2:3], scalar2=st[:, 6:7],
                    op0=ALU.add, op1=ALU.mult))
                O("vector", lambda e: e.tensor_tensor(out=out_f32, in0=scr, in1=res, op=ALU.add))

        with ch.group():
            O("gpsimd", lambda e: e.dma_start(out=w1b_sl[:], in_=w1_sl[:]), "dma")
            O("gpsimd", lambda e: e.dma_start(out=w3b_sl[:], in_=w3_sl[:]), "dma")
            O("gpsimd", lambda e: e.dma_start(out=w2b_sl[:], in_=w2_sl[:]), "dma")
            O("sync", lambda e: e.dma_start(out=MKL, in_=c_maskL[:]), "dma")
            O("sync", lambda e: e.dma_start(out=MKU, in_=c_maskU[:]), "dma")
            O("sync", lambda e: e.dma_start(out=MKQ, in_=c_maskQ[:]), "dma")
            O("sync", lambda e: e.dma_start(out=IDF, in_=c_idf[:]), "dma")
            O("sync", lambda e: e.dma_start(out=SGN8, in_=c_sign8[:]), "dma")
            O("sync", lambda e: e.dma_start(out=DTA8, in_=dta8[:]), "dma")
            O("sync", lambda e: e.dma_start(out=ALOG8, in_=alog8[:]), "dma")
            O("sync", lambda e: e.dma_start(out=NRMW, in_=normw[:]), "dma")
            O("sync", lambda e: e.dma_start(out=ONESF, in_=c_onesf[:]), "dma")
            O("sync", lambda e: e.dma_start(out=idbt[:, 0:128], in_=c_idb[:]), "dma")
            O("sync", lambda e: e.dma_start(out=idbt[:, 128:129], in_=c_onesb[:]), "dma")
            O("sync", lambda e: e.dma_start(out=h128t[:], in_=c_h128[:]), "dma")
            O("sync", lambda e: e.dma_start(out=h64t[:], in_=c_h64[:]), "dma")
            O("sync", lambda e: e.dma_start(out=cst[:, 656:660], in_=c_bias[:]), "dma")

        m0 = pool.mark()
        coef8 = cst[0:8, 516:517]
        ncoef8 = cst[0:8, 517:518]
        O("scalar", lambda e: e.activation(out=coef8, in_=ALOG8, func=AF.Exp))
        O("vector", lambda e: e.tensor_scalar(
            out=ncoef8, in0=coef8, scalar1=-1.0, scalar2=None, op0=ALU.mult))

        w2tt = pool.alloc(128, D, BF16)
        for i in range(8):
            O("sync", lambda e, i=i: e.dma_start(
                out=w2tt, in_=w2b_sl[:, i * 128:(i + 1) * 128], transpose=True), "dma")
            O("sync", lambda e, i=i: e.dma_start(
                out=w2t_sl[i * 128:(i + 1) * 128, :], in_=w2tt), "dma")

        wot_f = pool.alloc(128, D, F32)
        wot_b = pool.alloc(128, D, BF16)
        for i in range(4):
            O("sync", lambda e, i=i: e.dma_start(out=wot_f, in_=wo_sl[i * 128:(i + 1) * 128, :]), "dma")
            O("vector", lambda e: e.tensor_scalar(
                out=wot_b, in0=wot_f, scalar1=NRMW, scalar2=None, op0=ALU.mult))
            O("sync", lambda e, i=i: e.dma_start(out=wob_sl[i * 128:(i + 1) * 128, :], in_=wot_b), "dma")

        xt0 = pool.alloc(128, D, F32)
        scr0 = pool.alloc(128, D, F32)
        st0 = pool.alloc(128, 8, F32)
        xlnb0 = pool.alloc(128, D, BF16)
        for i in range(8):
            O("sync", lambda e, i=i: e.dma_start(out=xt0, in_=x_sh[i * 128:(i + 1) * 128, :]), "dma")
            ln_tile(xt0, xlnb0, scr0, st0)
            O("sync", lambda e, i=i: e.dma_start(out=xln_sl[i * 128:(i + 1) * 128, :], in_=xlnb0), "dma")

        O("gpsimd", lambda e: e.collective_compute(
            "AllGather", ALU.bypass, replica_groups=RG, ins=[xln_sl[:]], outs=[ag_xln[:]]), "cc")
        O("gpsimd", lambda e: e.collective_compute(
            "AllGather", ALU.bypass, replica_groups=RG, ins=[w1b_sl[:]], outs=[ag_w1[:]]), "cc")
        O("gpsimd", lambda e: e.collective_compute(
            "AllGather", ALU.bypass, replica_groups=RG, ins=[w3b_sl[:]], outs=[ag_w3[:]]), "cc")
        O("gpsimd", lambda e: e.collective_compute(
            "AllGather", ALU.bypass, replica_groups=RG, ins=[w2t_sl[:]], outs=[ag_w2t[:]]), "cc")
        if dbg >= 1:
            O("gpsimd", lambda e: e.dma_start(out=dbg_aps["d_xln"][:], in_=ag_xln[:]), "dma")
        pool.reset(m0)

        batile = pool.alloc(8, BT, F32)
        m1b = pool.mark()
        wqb = pool.alloc(128, 16 * 1024, BF16)
        wzb = pool.alloc(128, 16 * 512, BF16)
        wbab = pool.alloc(128, 16 * 8, BF16)
        wst = pool.alloc(128, 1024, F32)
        cwt = pool.alloc(128, 32, F32)
        for k in range(16):
            O("sync", lambda e, k=k: e.dma_start(out=wst, in_=wq_sl[k * 128:(k + 1) * 128, :]), "dma")
            O("scalar", lambda e, k=k: e.copy(out=wqb[:, k * 1024:(k + 1) * 1024], in_=wst))
            O("sync", lambda e, k=k: e.dma_start(out=wst[:, 0:512], in_=wz_sl[k * 128:(k + 1) * 128, :]), "dma")
            O("scalar", lambda e, k=k: e.copy(out=wzb[:, k * 512:(k + 1) * 512], in_=wst[:, 0:512]))
            O("sync", lambda e, k=k: e.dma_start(out=wst[:, 0:8], in_=wba_sl[k * 128:(k + 1) * 128, :]), "dma")
            O("scalar", lambda e, k=k: e.copy(out=wbab[:, k * 8:(k + 1) * 8], in_=wst[:, 0:8]))
        for m in range(8):
            O("sync", lambda e, m=m: e.dma_start(
                out=cwt[:, m * 4:(m + 1) * 4], in_=cwT_sl[m * 128:(m + 1) * 128, :]), "dma")

        xlt = pool.alloc(128, 16 * 1024, BF16)
        mc = pool.alloc(128, 1027, F32)
        acc0 = pool.alloc(128, 1024, F32)
        acc1 = pool.alloc(128, 1024, F32)
        tails = pool.alloc(128, 24, F32)
        sqt = pool.alloc(128, 1024, BF16)
        mtmp = pool.alloc(128, 1024, BF16)
        lrrow = pool.alloc(1, 1024, F32)

        def lhsT_ph1(m, k):
            if m < 8:
                return wqb[:, k * 1024 + m * 128: k * 1024 + (m + 1) * 128]
            if m < 12:
                return wzb[:, k * 512 + (m - 8) * 128: k * 512 + (m - 7) * 128]
            return wbab[:, k * 8:(k + 1) * 8]

        for tc in range(8):
            tsl = slice(tc * 1024, (tc + 1) * 1024)
            with ch.group():
                for k in range(16):
                    O("sync", lambda e, k=k, tc=tc: e.dma_start(
                        out=xlt[:, k * 1024:(k + 1) * 1024],
                        in_=ag_xln[tc * 1024:(tc + 1) * 1024, k * 128:(k + 1) * 128],
                        transpose=True), "dma")
            for m in range(13):
                pst = psBig if (m % 2 == 0) else psBig2
                pm = pst[0:128, :] if m < 12 else pst[0:8, :]
                for k in range(16):
                    for h in range(2):
                        O("tensor", lambda e, m=m, k=k, h=h, pm=pm: e.matmul(
                            pm[:, h * 512:(h + 1) * 512], lhsT=lhsT_ph1(m, k),
                            rhs=xlt[:, k * 1024 + h * 512: k * 1024 + (h + 1) * 512],
                            start=(k == 0), stop=(k == 15)))
                if m < 8:
                    O("scalar", lambda e, pm=pm: e.copy(out=mc[:, 3:1027], in_=pm))
                    if tc in (0, 4):
                        O("vector", lambda e: e.memset(mc[:, 0:3], 0.0))
                    else:
                        O("scalar", lambda e, m=m: e.copy(out=mc[:, 0:3], in_=tails[:, m * 3:(m + 1) * 3]))
                    O("scalar", lambda e, m=m: e.copy(out=tails[:, m * 3:(m + 1) * 3], in_=mc[:, 1024:1027]))
                    O("vector", lambda e, m=m: e.tensor_scalar(
                        out=acc0, in0=mc[:, 3:1027], scalar1=cwt[:, m * 4 + 3:m * 4 + 4],
                        scalar2=None, op0=ALU.mult))
                    O("vector", lambda e, m=m: e.scalar_tensor_tensor(
                        out=acc1, in0=mc[:, 2:1026], scalar=cwt[:, m * 4 + 2:m * 4 + 3],
                        in1=acc0, op0=ALU.mult, op1=ALU.add))
                    O("vector", lambda e, m=m: e.scalar_tensor_tensor(
                        out=acc0, in0=mc[:, 1:1025], scalar=cwt[:, m * 4 + 1:m * 4 + 2],
                        in1=acc1, op0=ALU.mult, op1=ALU.add))
                    O("vector", lambda e, m=m: e.scalar_tensor_tensor(
                        out=acc1, in0=mc[:, 0:1024], scalar=cwt[:, m * 4 + 0:m * 4 + 1],
                        in1=acc0, op0=ALU.mult, op1=ALU.add))
                    O("scalar", lambda e: e.activation(out=mtmp, in_=acc1, func=AF.Silu))
                    if m < 2:
                        O("sync", lambda e, m=m, tsl=tsl: e.dma_start(
                            out=qb_hbm[m * 128:(m + 1) * 128, tsl], in_=mtmp), "dma")
                    elif m < 4:
                        O("sync", lambda e, m=m, tsl=tsl: e.dma_start(
                            out=kb_hbm[(m - 2) * 128:(m - 1) * 128, tsl], in_=mtmp), "dma")
                    else:
                        O("sync", lambda e, m=m, tsl=tsl: e.dma_start(
                            out=vb_hbm[(m - 4) * 128:(m - 3) * 128, tsl], in_=mtmp), "dma")
                    if m < 4:
                        O("scalar", lambda e: e.activation(out=sqt, in_=mtmp, func=AF.Square))
                        O("tensor", lambda e: e.matmul(
                            psS1[0:1, 0:512], lhsT=ONEB, rhs=sqt[:, 0:512], start=True, stop=True))
                        O("tensor", lambda e: e.matmul(
                            psA[0:1, 0:512], lhsT=ONEB, rhs=sqt[:, 512:1024], start=True, stop=True))
                        O("scalar", lambda e: e.activation(
                            out=lrrow[:, 0:512], in_=psS1[0:1, 0:512], func=AF.Ln, bias=BEP[0:1, :]))
                        O("scalar", lambda e: e.activation(
                            out=lrrow[:, 512:1024], in_=psA[0:1, 0:512], func=AF.Ln, bias=BEP[0:1, :]))
                        O("sync", lambda e, m=m, tsl=tsl: e.dma_start(
                            out=lr_hbm[m:m + 1, tsl], in_=lrrow), "dma")
                        lay = lrqlay if m < 2 else lrklay
                        hh = m % 2
                        bb, c0 = tc // 4, (tc % 4) * 8
                        for r in range(2):
                            base_ch = c0 * 8 + (2 * hh + r) * 2 + bb
                            O("sync", lambda e, lay=lay, base_ch=base_ch: e.dma_start(
                                out=lay[base_ch:base_ch + 57:8, :],
                                in_=lrrow.rearrange("o (c p) -> o c p", p=128)), "dma")
                    if dbg >= 1:
                        O("sync", lambda e, m=m, tsl=tsl: e.dma_start(
                            out=dbg_aps["d_mix"][m * 128:(m + 1) * 128, tsl], in_=mtmp), "dma")
                elif m < 12:
                    O("scalar", lambda e, pm=pm: e.copy(out=mtmp, in_=pm))
                    O("sync", lambda e, m=m, tsl=tsl: e.dma_start(
                        out=zb_hbm[(m - 8) * 128:(m - 7) * 128, tsl], in_=mtmp), "dma")
                    if dbg >= 1:
                        O("sync", lambda e, m=m, tsl=tsl: e.dma_start(
                            out=dbg_aps["d_z"][(m - 8) * 128:(m - 7) * 128, tsl], in_=mtmp), "dma")
                else:
                    O("scalar", lambda e, pm=pm, tc=tc: e.copy(
                        out=batile[:, tc * 1024:(tc + 1) * 1024], in_=pm))

        pool.reset(m1b)
        spA = pool.alloc(8, BT, F32)
        spB = pool.alloc(8, BT, F32)
        lngt = pool.alloc(8, BT, F32)
        O("vector", lambda e: e.tensor_scalar(
            out=spA, in0=batile, scalar1=SGN8, scalar2=DTA8, op0=ALU.mult, op1=ALU.add))
        O("scalar", lambda e: e.activation(out=spB, in_=spA, func=AF.Abs))
        O("scalar", lambda e: e.activation(out=spB, in_=spB, func=AF.Exp, scale=-1.0))
        O("scalar", lambda e: e.activation(out=spB, in_=spB, func=AF.Ln, bias=BONE[0:8, :]))
        O("vector", lambda e: e.tensor_scalar(
            out=spA, in0=spA, scalar1=0.0, scalar2=None, op0=ALU.max))
        O("vector", lambda e: e.tensor_tensor(out=spA, in0=spA, in1=spB, op=ALU.add))
        O("vector", lambda e: e.tensor_scalar(
            out=lngt, in0=spA, scalar1=ncoef8, scalar2=None, op0=ALU.mult))
        O("sync", lambda e: e.dma_start(out=lng_hbm[:], in_=lngt), "dma")
        O("sync", lambda e: e.dma_start(
            out=glay.rearrange("(c h b) p -> h b c p", h=4, b=2),
            in_=lngt[4:8, :].rearrange("h (b c p) -> h b c p", b=2, c=32)), "dma")
        O("sync", lambda e: e.dma_start(
            out=blay.rearrange("(c h b) p -> h b c p", h=4, b=2),
            in_=lngt[0:4, :].rearrange("h (b c p) -> h b c p", b=2, c=32)), "dma")
        if dbg >= 1:
            O("sync", lambda e: e.dma_start(out=dbg_aps["d_lng"][:], in_=lngt), "dma")
            O("sync", lambda e: e.dma_start(out=dbg_aps["d_lr"][:], in_=lr_hbm[:]), "dma")

        cgt = [pool.alloc(128, 128, F32) for _ in range(2)]
        cgt2 = [pool.alloc(128, 128, F32) for _ in range(2)]
        lnbt = [pool.alloc(128, 128, F32) for _ in range(2)]
        lrkt = [pool.alloc(128, 128, F32) for _ in range(2)]
        lrqt = [pool.alloc(128, 128, F32) for _ in range(2)]
        for t2 in range(2):
            cs = slice(t2 * 128, (t2 + 1) * 128)
            with ch.group():
                O("sync", lambda e, t2=t2, cs=cs: e.dma_start(out=cgt[t2][:, :], in_=glay[cs]), "dma")
                O("sync", lambda e, t2=t2, cs=cs: e.dma_start(out=lnbt[t2][:, :], in_=blay[cs]), "dma")
                O("sync", lambda e, t2=t2, cs=cs: e.dma_start(out=lrkt[t2][:, :], in_=lrklay[cs]), "dma")
                O("sync", lambda e, t2=t2, cs=cs: e.dma_start(out=lrqt[t2][:, :], in_=lrqlay[cs]), "dma")
        for t2 in range(2):
            src, dst = cgt[t2], cgt2[t2]
            s = 1
            while s < 128:
                O("vector", lambda e, src=src, dst=dst, s=s: e.tensor_tensor(
                    out=dst[:, s:128], in0=src[:, s:128], in1=src[:, 0:128 - s], op=ALU.add))
                O("vector", lambda e, src=src, dst=dst, s=s: e.tensor_copy(
                    out=dst[:, 0:s], in_=src[:, 0:s]))
                src, dst = dst, src
                s *= 2
            if src is not cgt[t2]:
                O("vector", lambda e, src=src, t2=t2: e.tensor_copy(out=cgt[t2], in_=src))

        av = [pool.alloc(128, 128, F32) for _ in range(2)]
        bv = [pool.alloc(128, 128, F32) for _ in range(2)]
        qv = [pool.alloc(128, 128, F32) for _ in range(2)]
        ncg = pool.alloc(128, 128, F32)
        sct = pool.alloc(128, 128, F32)
        scb = pool.alloc(128, 128, BF16)
        for t2 in range(2):
            O("vector", lambda e, t2=t2: e.scalar_tensor_tensor(
                out=av[t2], in0=lrkt[t2], scalar=-0.5, in1=cgt[t2], op0=ALU.mult, op1=ALU.add))
            O("vector", lambda e, t2=t2: e.tensor_tensor(
                out=av[t2], in0=av[t2], in1=lnbt[t2], op=ALU.add))
            O("vector", lambda e, t2=t2: e.tensor_scalar(
                out=ncg, in0=cgt[t2], scalar1=-1.0, scalar2=None, op0=ALU.mult))
            O("vector", lambda e, t2=t2: e.scalar_tensor_tensor(
                out=bv[t2], in0=lrkt[t2], scalar=-0.5, in1=ncg, op0=ALU.mult, op1=ALU.add))
            O("vector", lambda e, t2=t2: e.scalar_tensor_tensor(
                out=qv[t2], in0=lrqt[t2], scalar=-0.5, in1=cgt[t2], op0=ALU.mult, op1=ALU.add))
            O("scalar", lambda e, t2=t2: e.activation(out=sct, in_=av[t2], func=AF.Exp))
            O("vector", lambda e: e.tensor_scalar(
                out=scb, in0=sct, scalar1=-1.0, scalar2=None, op0=ALU.mult))
            O("sync", lambda e, t2=t2: e.dma_start(
                out=scl_hbm[0, t2 * 128:(t2 + 1) * 128, :], in_=scb), "dma")
            O("scalar", lambda e, t2=t2: e.activation(
                out=scb, in_=qv[t2], func=AF.Exp, bias=BQM))
            O("sync", lambda e, t2=t2: e.dma_start(
                out=scl_hbm[1, t2 * 128:(t2 + 1) * 128, :], in_=scb), "dma")
            O("vector", lambda e, t2=t2: e.tensor_scalar(
                out=sct, in0=bv[t2], scalar1=cgt[t2][:, 127:128], scalar2=None, op0=ALU.add))
            O("scalar", lambda e: e.activation(out=scb, in_=sct, func=AF.Exp))
            O("sync", lambda e, t2=t2: e.dma_start(
                out=scl_hbm[2, t2 * 128:(t2 + 1) * 128, :], in_=scb), "dma")
            O("scalar", lambda e, t2=t2: e.activation(out=scb, in_=lnbt[t2], func=AF.Exp))
            O("sync", lambda e, t2=t2: e.dma_start(
                out=scl_hbm[3, t2 * 128:(t2 + 1) * 128, :], in_=scb), "dma")
            O("sync", lambda e, t2=t2: e.dma_start(
                out=evec_hbm[0, t2 * 128:(t2 + 1) * 128, :], in_=av[t2]), "dma")
            O("sync", lambda e, t2=t2: e.dma_start(
                out=evec_hbm[1, t2 * 128:(t2 + 1) * 128, :], in_=bv[t2]), "dma")
            O("sync", lambda e, t2=t2: e.dma_start(
                out=evec_hbm[2, t2 * 128:(t2 + 1) * 128, :], in_=qv[t2]), "dma")
            O("sync", lambda e, t2=t2: e.dma_start(
                out=gcx_hbm[t2 * 128:(t2 + 1) * 128, :], in_=cgt[t2][:, 127:128]), "dma")
        for i in range(4):
            O("sync", lambda e, i=i: e.dma_start(
                out=sclT[:, i * 256:(i + 1) * 256], in_=scl_hbm[i], transpose=True), "dma")
            O("scalar", lambda e, i=i: e.copy(
                out=sclF[:, i * 256:(i + 1) * 256], in_=sclT[:, i * 256:(i + 1) * 256]))
        gcrow = pool.alloc(1, 256, F32)
        O("sync", lambda e: e.dma_start(
            out=gcrow, in_=gcx_hbm.rearrange("p o -> o p")), "dma")
        O("scalar", lambda e: e.activation(out=gcrow, in_=gcrow, func=AF.Exp))
        O("tensor", lambda e: e.matmul(psS1[:, 0:256], lhsT=ONESF, rhs=gcrow, start=True, stop=True))
        O("scalar", lambda e: e.copy(out=gcb[:], in_=psS1[:, 0:256]))
        pool.reset(0)

        for p in range(8):
            O("vector", lambda e, p=p: e.memset(St[p][:], 0.0))
            O("vector", lambda e, p=p: e.memset(Sbt[p][:], 0.0))
        O("vector", lambda e: e.memset(LA[:], 1.0))
        O("vector", lambda e: e.memset(LB[:], 1.0))
        O("vector", lambda e: e.memset(RB[:], 1.0))
        O("vector", lambda e: e.memset(RA[:], 1.0))
        O("vector", lambda e: e.memset(RQ[:], 1.0))

        m2 = pool.mark()
        Ktm = [pool.alloc(128, 256, BF16) for _ in range(2)]
        Vtm = [pool.alloc(128, 512, BF16) for _ in range(2)]
        kfm = pool.alloc(128, 256, BF16)
        qfm = pool.alloc(128, 256, BF16)
        Ztm = pool.alloc(128, 512, BF16)
        Elow = pool.alloc(128, 128, BF16)
        Eup = pool.alloc(128, 128, BF16)
        Eq = pool.alloc(128, 128, BF16)
        mKK = pool.alloc(128, 128, BF16)
        Ptl = pool.alloc(128, 128, BF16)
        Zp = [pool.alloc(128, 128, BF16) for _ in range(7)]
        Pp = [pool.alloc(128, 128, BF16) for _ in range(6)]
        bVt = pool.alloc(128, 128, BF16)
        Xb = [pool.alloc(128, 128, BF16) for _ in range(2)]
        qssb = pool.alloc(128, 128, BF16)
        Dbar = pool.alloc(128, 128, BF16)
        Obuf = [pool.alloc(128, 512, BF16) for _ in range(2)]
        sqg = pool.alloc(128, 512, BF16)
        gst = pool.alloc(128, 12, F32)
        zst = pool.alloc(128, 512, BF16)
        ggt = pool.alloc(128, 512, BF16)

        psE = psA[:, 0:128]
        psKK = psA[:, 128:256]
        psKQ = psA[:, 256:384]
        psKV = psA[:, 384:512]
        psQS = psB[:, 0:128]
        psO = psB[:, 128:256]
        psX = psB[:, 256:384]
        psS = psB[:, 384:512]

        for ic in range(NCH):
            Scur = St if ic % 2 == 0 else St2
            Snxt = St2 if ic % 2 == 0 else St
            with ch.group():
                ev = evec_hbm[:, ic * 8:(ic + 1) * 8, :]
                O("sync", lambda e, ev=ev: e.dma_start(
                    out=LA[1:2, :], in_=ev[0].rearrange("c p -> () (c p)")), "dma")
                O("sync", lambda e, ev=ev: e.dma_start(
                    out=RA[0:1, :], in_=ev[0].rearrange("c p -> () (c p)")), "dma")
                O("sync", lambda e, ev=ev: e.dma_start(
                    out=LB[1:2, :], in_=ev[1].rearrange("c p -> () (c p)")), "dma")
                O("sync", lambda e, ev=ev: e.dma_start(
                    out=RB[0:1, :], in_=ev[1].rearrange("c p -> () (c p)")), "dma")
                O("sync", lambda e, ev=ev: e.dma_start(
                    out=RQ[0:1, :], in_=ev[2].rearrange("c p -> () (c p)")), "dma")
            for b in range(2):
                off = b * T + ic * 128
                with ch.group():
                    O("sync", lambda e, b=b, off=off: e.dma_start(
                        out=Ktm[b], in_=kb_hbm[:, off:off + 128], transpose=True), "dma")
                    O("sync", lambda e, b=b, off=off: e.dma_start(
                        out=Vtm[b], in_=vb_hbm[:, off:off + 128], transpose=True), "dma")
                    for hh in range(2):
                        O("sync", lambda e, hh=hh, off=off: e.dma_start(
                            out=kfm[:, hh * 128:(hh + 1) * 128],
                            in_=kb_hbm[hh * 128:(hh + 1) * 128, off:off + 128]), "dma")
                        O("sync", lambda e, hh=hh, off=off: e.dma_start(
                            out=qfm[:, hh * 128:(hh + 1) * 128],
                            in_=qb_hbm[hh * 128:(hh + 1) * 128, off:off + 128]), "dma")
                for vh in range(4):
                    p = b * 4 + vh
                    pp = vh * 2 + b
                    kh = vh // 2
                    cc = ic * 8 + pp
                    K_fm = kfm[:, kh * 128:(kh + 1) * 128]
                    Q_fm = qfm[:, kh * 128:(kh + 1) * 128]
                    Ktm_p = Ktm[b][:, kh * 128:(kh + 1) * 128]
                    V_p = Vtm[b][:, vh * 128:(vh + 1) * 128]
                    esl = slice(pp * 128, (pp + 1) * 128)

                    O("tensor", lambda e, esl=esl: e.matmul(
                        psE, lhsT=LA[:, esl], rhs=RB[:, esl], start=True, stop=False))
                    O("tensor", lambda e: e.matmul(psE, lhsT=IDF, rhs=MKL, start=False, stop=True))
                    O("scalar", lambda e: e.activation(out=Elow, in_=psE, func=AF.Exp))
                    O("tensor", lambda e, esl=esl: e.matmul(
                        psE, lhsT=LB[:, esl], rhs=RA[:, esl], start=True, stop=False))
                    O("tensor", lambda e: e.matmul(psE, lhsT=IDF, rhs=MKU, start=False, stop=True))
                    O("scalar", lambda e: e.activation(out=Eup, in_=psE, func=AF.Exp))
                    O("tensor", lambda e, esl=esl: e.matmul(
                        psE, lhsT=LB[:, esl], rhs=RQ[:, esl], start=True, stop=False))
                    O("tensor", lambda e: e.matmul(psE, lhsT=IDF, rhs=MKQ, start=False, stop=True))
                    O("scalar", lambda e: e.activation(out=Eq, in_=psE, func=AF.Exp))
                    O("tensor", lambda e, K_fm=K_fm: e.matmul(
                        psKK, lhsT=K_fm, rhs=K_fm, start=True, stop=True))
                    O("scalar", lambda e: e.activation(out=mKK, in_=psKK, func=AF.Copy, scale=-1.0))
                    O("tensor", lambda e, K_fm=K_fm, Q_fm=Q_fm: e.matmul(
                        psKQ, lhsT=K_fm, rhs=Q_fm, start=True, stop=True))
                    O("vector", lambda e: e.tensor_tensor(out=Ptl, in0=psKQ, in1=Eq, op=ALU.mult))
                    O("vector", lambda e: e.tensor_tensor(out=Zp[0], in0=mKK, in1=Eup, op=ALU.mult))
                    O("vector", lambda e: e.tensor_tensor(out=Pp[0], in0=mKK, in1=Elow, op=ALU.mult))
                    for j in range(1, 7):
                        O("tensor", lambda e, j=j: e.matmul(
                            psX, lhsT=Pp[j - 1], rhs=Zp[j - 1], start=True, stop=True))
                        if j % 2 == 0:
                            O("scalar", lambda e, j=j: e.copy(out=Zp[j], in_=psX))
                        else:
                            O("vector", lambda e, j=j: e.tensor_copy(out=Zp[j], in_=psX))
                        if j < 6:
                            O("tensor", lambda e, j=j: e.matmul(
                                psX, lhsT=Zp[j - 1], rhs=Pp[j - 1], start=True, stop=True))
                            if j % 2 == 0:
                                O("vector", lambda e, j=j: e.tensor_copy(out=Pp[j], in_=psX))
                            else:
                                O("scalar", lambda e, j=j: e.copy(out=Pp[j], in_=psX))
                    O("tensor", lambda e, K_fm=K_fm, p=p: e.matmul(
                        psKV, lhsT=K_fm, rhs=Sbt[p][:], start=True, stop=True))
                    O("vector", lambda e, V_p=V_p, cc=cc: e.tensor_scalar(
                        out=bVt, in0=V_p, scalar1=bexT[:, cc:cc + 1], scalar2=None, op0=ALU.mult))
                    O("vector", lambda e, cc=cc: e.scalar_tensor_tensor(
                        out=Xb[0], in0=psKV, scalar=nrscT[:, cc:cc + 1], in1=bVt,
                        op0=ALU.mult, op1=ALU.add))
                    for j in range(7):
                        xin, xout = Xb[j % 2], Xb[(j + 1) % 2]
                        O("tensor", lambda e, j=j, xin=xin: e.matmul(
                            psX, lhsT=Zp[j], rhs=xin, start=True, stop=False))
                        O("tensor", lambda e, xin=xin: e.matmul(
                            psX, lhsT=IDB, rhs=xin, start=False, stop=True))
                        if j % 2 == 0:
                            O("scalar", lambda e, xout=xout: e.copy(out=xout, in_=psX))
                        else:
                            O("vector", lambda e, xout=xout: e.tensor_copy(out=xout, in_=psX))
                    Delta = Xb[1]
                    O("tensor", lambda e, Delta=Delta: e.matmul(
                        psO, lhsT=Ptl, rhs=Delta, start=True, stop=True))
                    O("tensor", lambda e, Q_fm=Q_fm, p=p: e.matmul(
                        psQS, lhsT=Q_fm, rhs=Sbt[p][:], start=True, stop=True))
                    O("scalar", lambda e: e.copy(out=qssb, in_=psQS))
                    O("vector", lambda e, b=b, vh=vh, cc=cc: e.scalar_tensor_tensor(
                        out=Obuf[b][:, vh * 128:(vh + 1) * 128], in0=qssb,
                        scalar=oscT[:, cc:cc + 1], in1=psO, op0=ALU.mult, op1=ALU.add))
                    O("vector", lambda e, Delta=Delta, cc=cc: e.tensor_scalar(
                        out=Dbar, in0=Delta, scalar1=dscT[:, cc:cc + 1], scalar2=None, op0=ALU.mult))
                    O("tensor", lambda e, Ktm_p=Ktm_p: e.matmul(
                        psS, lhsT=Ktm_p, rhs=Dbar, start=True, stop=True))
                    O("vector", lambda e, p=p, cc=cc, Scur=Scur, Snxt=Snxt: e.scalar_tensor_tensor(
                        out=Snxt[p][:], in0=Scur[p][:], scalar=gcb[:, cc:cc + 1], in1=psS,
                        op0=ALU.mult, op1=ALU.add))
                    O("scalar", lambda e, p=p, Snxt=Snxt: e.copy(out=Sbt[p][:], in_=Snxt[p][:]))
                O("sync", lambda e, off=off: e.dma_start(
                    out=Ztm, in_=zb_hbm[:, off:off + 128], transpose=True), "dma")
                O("scalar", lambda e, b=b: e.activation(out=sqg, in_=Obuf[b], func=AF.Square))
                O("vector", lambda e: e.tensor_reduce(
                    out=gst[:, 0:4].rearrange("p (h o) -> p h o", o=1),
                    in_=sqg.rearrange("p (h d) -> p h d", h=4), axis=AX.X, op=ALU.add))
                O("scalar", lambda e: e.activation(
                    out=gst[:, 4:8], in_=gst[:, 0:4], func=AF.Ln, scale=1.0 / 128, bias=BEP))
                O("scalar", lambda e: e.activation(
                    out=gst[:, 8:12], in_=gst[:, 4:8], func=AF.Exp, scale=-0.5))
                O("scalar", lambda e: e.activation(out=zst, in_=Ztm, func=AF.Silu))
                for vh in range(4):
                    O("vector", lambda e, b=b, vh=vh: e.tensor_scalar(
                        out=ggt[:, vh * 128:(vh + 1) * 128],
                        in0=Obuf[b][:, vh * 128:(vh + 1) * 128],
                        scalar1=gst[:, 8 + vh:9 + vh], scalar2=None, op0=ALU.mult))
                O("vector", lambda e: e.tensor_tensor(out=ggt, in0=ggt, in1=zst, op=ALU.mult))
                O("sync", lambda e, off=off: e.dma_start(
                    out=gated_tm[off:off + 128, :], in_=ggt), "dma")
        pool.reset(m2)

        if dbg >= 2:
            O("sync", lambda e: e.dma_start(out=dbg_aps["d_gat"][:], in_=gated_tm[:]), "dma")

        m3 = pool.mark()
        gfm = [pool.alloc(128, BT, BF16) for _ in range(4)]
        wobq = pool.alloc(128, 4 * 512, BF16)
        ptile = pool.alloc(128, 512, BF16)
        oab = pool.alloc(128, 8 * 2048, BF16)
        for vt in range(4):
            O("sync", lambda e, vt=vt: e.dma_start(
                out=gfm[vt], in_=gated_tm[:, vt * 128:(vt + 1) * 128], transpose=True), "dma")
        for dq in range(4):
            with ch.group():
                for vt in range(4):
                    O("sync", lambda e, dq=dq, vt=vt: e.dma_start(
                        out=wobq[:, vt * 512:(vt + 1) * 512],
                        in_=wob_sl[vt * 128:(vt + 1) * 128, dq * 512:(dq + 1) * 512]), "dma")
            for tt in range(64):
                for vt in range(4):
                    O("tensor", lambda e, tt=tt, vt=vt: e.matmul(
                        psS1[:, :], lhsT=gfm[vt][:, tt * 128:(tt + 1) * 128],
                        rhs=wobq[:, vt * 512:(vt + 1) * 512], start=(vt == 0), stop=(vt == 3)))
                O("scalar", lambda e: e.copy(out=ptile, in_=psS1[:, :]))
                O("sync", lambda e, dq=dq, tt=tt: e.dma_start(
                    out=part_hbm[tt * 128:(tt + 1) * 128, dq * 512:(dq + 1) * 512],
                    in_=ptile), "dma")
        O("gpsimd", lambda e: e.collective_compute(
            "ReduceScatter", ALU.add, replica_groups=RG, ins=[part_hbm[:]], outs=[rs_out[:]]), "cc")
        pool.reset(m3)
        gfm_unused = None
        oab = pool.alloc(128, 8 * 2048, BF16)
        with ch.group():
            for tt in range(8):
                O("sync", lambda e, tt=tt: e.dma_start(
                    out=oab[:, tt * 2048:(tt + 1) * 2048],
                    in_=rs_out[tt * 128:(tt + 1) * 128, :]), "dma")
        oaf = pool.alloc(128, D, F32)
        xt3 = pool.alloc(128, D, F32)
        x2t = pool.alloc(128, D, F32)
        scr3 = pool.alloc(128, D, F32)
        st3 = pool.alloc(128, 8, F32)
        x2lb = pool.alloc(128, D, BF16)
        for tt in range(8):
            O("scalar", lambda e, tt=tt: e.copy(out=oaf, in_=oab[:, tt * 2048:(tt + 1) * 2048]))
            O("sync", lambda e, tt=tt: e.dma_start(out=xt3, in_=x_sh[tt * 128:(tt + 1) * 128, :]), "dma")
            ln_tile(oaf, None, scr3, st3, out_f32=x2t, res=xt3)
            O("sync", lambda e, tt=tt: e.dma_start(out=x2_hbm[tt * 128:(tt + 1) * 128, :], in_=x2t), "dma")
            ln_tile(x2t, x2lb, scr3, st3)
            O("sync", lambda e, tt=tt: e.dma_start(out=x2ln_hbm[tt * 128:(tt + 1) * 128, :], in_=x2lb), "dma")
            if dbg >= 3:
                O("sync", lambda e, tt=tt: e.dma_start(
                    out=dbg_aps["d_x2"][tt * 128:(tt + 1) * 128, :], in_=x2t), "dma")
        pool.reset(m3)

        wtt = pool.alloc(128, 8192, BF16)
        for k in range(16):
            O("sync", lambda e, k=k: e.dma_start(
                out=wtt, in_=ag_w1[:, k * 128:(k + 1) * 128], transpose=True), "dma")
            O("sync", lambda e, k=k: e.dma_start(out=w1t_hbm[k * 128:(k + 1) * 128, :], in_=wtt), "dma")
            O("sync", lambda e, k=k: e.dma_start(
                out=wtt, in_=ag_w3[:, k * 128:(k + 1) * 128], transpose=True), "dma")
            O("sync", lambda e, k=k: e.dma_start(out=w3t_hbm[k * 128:(k + 1) * 128, :], in_=wtt), "dma")
        pool.reset(m3)

        x2lT = pool.alloc(128, 16 * 1024, BF16)
        with ch.group():
            for k in range(16):
                O("sync", lambda e, k=k: e.dma_start(
                    out=x2lT[:, k * 1024:(k + 1) * 1024],
                    in_=x2ln_hbm[:, k * 128:(k + 1) * 128], transpose=True), "dma")
        m4pre = pool.mark()
        blk1 = pool.alloc(128, 4096, BF16)
        blk3 = pool.alloc(128, 4096, BF16)
        sg4 = pool.alloc(128, 1024, BF16)
        hid4 = pool.alloc(128, 1024, BF16)
        yt4 = pool.alloc(128, 1024, BF16)
        w1v = w1t_hbm.rearrange("(k p) m -> p k m", p=128)
        w3v = w3t_hbm.rearrange("(k p) m -> p k m", p=128)
        for mp in range(32):
            msl = slice(mp * 256, (mp + 1) * 256)
            with ch.group():
                O("sync", lambda e, msl=msl: e.dma_start(
                    out=blk1.rearrange("p (k m) -> p k m", k=16), in_=w1v[:, :, msl]), "dma")
                O("sync", lambda e, msl=msl: e.dma_start(
                    out=blk3.rearrange("p (k m) -> p k m", k=16), in_=w3v[:, :, msl]), "dma")
            for mh in range(2):
                m = 2 * mp + mh
                moff = mh * 128
                for k in range(16):
                    for h in range(2):
                        O("tensor", lambda e, k=k, h=h, moff=moff: e.matmul(
                            psBig[:, h * 512:(h + 1) * 512],
                            lhsT=blk1[:, k * 256 + moff: k * 256 + moff + 128],
                            rhs=x2lT[:, k * 1024 + h * 512: k * 1024 + (h + 1) * 512],
                            start=(k == 0), stop=(k == 15)))
                for k in range(16):
                    for h in range(2):
                        O("tensor", lambda e, k=k, h=h, moff=moff: e.matmul(
                            psBig2[:, h * 512:(h + 1) * 512],
                            lhsT=blk3[:, k * 256 + moff: k * 256 + moff + 128],
                            rhs=x2lT[:, k * 1024 + h * 512: k * 1024 + (h + 1) * 512],
                            start=(k == 0), stop=(k == 15)))
                O("scalar", lambda e: e.activation(out=sg4, in_=psBig[:, :], func=AF.Silu))
                O("vector", lambda e: e.tensor_tensor(out=hid4, in0=sg4, in1=psBig2[:, :], op=ALU.mult))
                for h in range(2):
                    O("tensor", lambda e, h=h: e.matmul(
                        psBig[:, h * 512:(h + 1) * 512], lhsT=h128t[:],
                        rhs=hid4[:, h * 512:(h + 1) * 512], start=True, stop=True))
                O("scalar", lambda e: e.activation(out=yt4, in_=psBig[:, :], func=AF.Copy, scale=1.0 / 128))
                O("sync", lambda e, m=m: e.dma_start(out=ybuf_hbm[m], in_=yt4), "dma")
        pool.reset(m3)

        yr = pool.alloc(64, 1024, BF16)
        h2t = pool.alloc(64, 1024, BF16)
        for j in range(128):
            O("sync", lambda e, j=j: e.dma_start(out=yr, in_=ybuf_hbm[:, j, :]), "dma")
            for h in range(2):
                O("tensor", lambda e, h=h: e.matmul(
                    psBig[0:64, h * 512:(h + 1) * 512], lhsT=h64t[:],
                    rhs=yr[:, h * 512:(h + 1) * 512], start=True, stop=True))
            O("scalar", lambda e: e.activation(
                out=h2t, in_=psBig[0:64, 0:1024], func=AF.Copy, scale=float(np.sqrt(2.0))))
            O("sync", lambda e, j=j: e.dma_start(
                out=h2_hbm[:, j % 2, :, j // 2, :].rearrange("tt a tp -> a tt tp"),
                in_=h2t.rearrange("a (tt tp) -> a tt tp", tt=8)), "dma")
        pool.reset(m3)

        w2q = pool.alloc(128, 64 * 512, BF16)
        h2blk = pool.alloc(128, 8192, BF16)
        ffob = pool.alloc(128, 8 * 2048, BF16)
        w2v = ag_w2t.rearrange("(a b) d -> b a d", a=64)
        for dq in range(4):
            with ch.group():
                for k in range(64):
                    O("sync", lambda e, dq=dq, k=k: e.dma_start(
                        out=w2q[:, k * 512:(k + 1) * 512],
                        in_=w2v[2 * k:2 * k + 2, :, dq * 512:(dq + 1) * 512]), "dma")
            for tt in range(8):
                O("sync", lambda e, tt=tt: e.dma_start(
                    out=h2blk, in_=h2_hbm[tt].rearrange("h a m p -> (h a) (m p)")), "dma")
                for k in range(64):
                    O("tensor", lambda e, k=k: e.matmul(
                        psS1[:, :], lhsT=h2blk[:, k * 128:(k + 1) * 128],
                        rhs=w2q[:, k * 512:(k + 1) * 512], start=(k == 0), stop=(k == 63)))
                O("scalar", lambda e, dq=dq, tt=tt: e.copy(
                    out=ffob[:, tt * 2048 + dq * 512: tt * 2048 + (dq + 1) * 512], in_=psS1[:, :]))
        ff_f = pool.alloc(128, D, F32)
        x2t4 = pool.alloc(128, D, F32)
        outt = pool.alloc(128, D, F32)
        scr4 = pool.alloc(128, D, F32)
        st4 = pool.alloc(128, 8, F32)
        for tt in range(8):
            O("scalar", lambda e, tt=tt: e.copy(out=ff_f, in_=ffob[:, tt * 2048:(tt + 1) * 2048]))
            O("sync", lambda e, tt=tt: e.dma_start(out=x2t4, in_=x2_hbm[tt * 128:(tt + 1) * 128, :]), "dma")
            ln_tile(ff_f, None, scr4, st4, out_f32=outt, res=x2t4)
            O("sync", lambda e, tt=tt: e.dma_start(out=out_sh[tt * 128:(tt + 1) * 128, :], in_=outt), "dma")

        ch.emit(block, drain=ch.all_tokens())

    return nc, dbg_aps


def make_inputs(core, inp):
    c = core
    qc = slice(256 * c, 256 * (c + 1))
    kc = slice(2048 + 256 * c, 2048 + 256 * (c + 1))
    vc = slice(4096 + 512 * c, 4096 + 512 * (c + 1))
    wqkv = np.asarray(inp["L0_attn_w_qkv"], dtype=np.float32)
    conv = np.asarray(inp["L0_attn_conv_w"], dtype=np.float32)
    wq = np.concatenate([wqkv[:, qc], wqkv[:, kc], wqkv[:, vc]], axis=1)
    cw = np.concatenate([conv[:, qc], conv[:, kc], conv[:, vc]], axis=1)
    x = np.asarray(inp["L0_attn_x"], dtype=np.float32).reshape(BT, D)
    return {
        "x_sh": np.ascontiguousarray(x[c * TSH:(c + 1) * TSH]),
        "wq_sl": np.ascontiguousarray(wq),
        "wz_sl": np.ascontiguousarray(np.asarray(inp["L0_attn_w_z"], dtype=np.float32)[:, 512 * c:512 * (c + 1)]),
        "wba_sl": np.ascontiguousarray(np.concatenate(
            [np.asarray(inp["L0_attn_w_b"], dtype=np.float32)[:, 4 * c:4 * (c + 1)],
             np.asarray(inp["L0_attn_w_a"], dtype=np.float32)[:, 4 * c:4 * (c + 1)]], axis=1)),
        "cwT_sl": np.ascontiguousarray(cw.T),
        "dta8": np.ascontiguousarray(np.concatenate(
            [np.zeros(4, np.float32), np.asarray(inp["L0_attn_dt_bias"], dtype=np.float32)[4 * c:4 * (c + 1)]])[:, None]),
        "alog8": np.ascontiguousarray(np.concatenate(
            [np.zeros(4, np.float32), np.asarray(inp["L0_attn_A_log"], dtype=np.float32)[4 * c:4 * (c + 1)]])[:, None]),
        "normw": np.ascontiguousarray(np.asarray(inp["L0_attn_norm_w"], dtype=np.float32)[:, None]),
        "wo_sl": np.ascontiguousarray(np.asarray(inp["L0_attn_w_out"], dtype=np.float32)[512 * c:512 * (c + 1), :]),
        "w1_sl": np.ascontiguousarray(np.asarray(inp["L0_ffn_W1"], dtype=np.float32)[1024 * c:1024 * (c + 1), :]),
        "w3_sl": np.ascontiguousarray(np.asarray(inp["L0_ffn_W3"], dtype=np.float32)[1024 * c:1024 * (c + 1), :]),
        "w2_sl": np.ascontiguousarray(np.asarray(inp["L0_ffn_W2"], dtype=np.float32)[:, 1024 * c:1024 * (c + 1)]),
    }


def kernel(**inp):
    nc, _ = build(dbg=0)
    in_maps = [make_inputs(c, inp) for c in range(8)]
    res = run_bass_kernel_spmd(nc, in_maps, core_ids=list(range(8)))
    out = np.concatenate([res.results[c]["out_sh"] for c in range(8)], axis=0)
    return out.reshape(B, T, D).astype(np.float32)


# version 04
# speedup vs baseline: 1.0058x; 1.0058x over previous
import numpy as np
import ml_dtypes
import concourse.bass as bass
import concourse.mybir as mybir
from concourse.bass_utils import run_bass_kernel_spmd

F32 = mybir.dt.float32
BF16 = mybir.dt.bfloat16
AF = mybir.ActivationFunctionType
ALU = mybir.AluOpType
AX = mybir.AxisListType

B, T, D = 2, 4096, 2048
BT = B * T
CCH = 128
NCH = T // CCH
TSH = BT // 8
EPS = 1e-6
LNEPS = 1e-5
NEGINF = -30000.0
BF = ml_dtypes.bfloat16
LN128 = float(np.log(128.0))


def _hadamard(n):
    h = np.array([[1.0]])
    while h.shape[0] < n:
        h = np.block([[h, h], [h, -h]])
    return h


ENGS = ("sync", "scalar", "vector", "tensor", "gpsimd")


class Sched:

    NSLOT = 8

    def __init__(self, nc):
        self.nc = nc
        self.ops = {e: [] for e in ENGS}
        self.cnt = {e: 0 for e in ENGS}
        self.csem = {e: nc.alloc_semaphore(name=f"cs_{e}") for e in ENGS}
        self.waited = {e: {} for e in ENGS}
        self.dsem = {}
        self.dcnt = {}
        self.drr = {}
        for q in ("sync", "scalar", "vector", "gpsimd"):
            self.drr[q] = 0
            for i in range(self.NSLOT):
                self.dsem[(q, i)] = nc.alloc_semaphore(name=f"d_{q}{i}")
                self.dcnt[(q, i)] = 0
        self.n_cc = 0
        self.serial_last = ()
        self._grp = None

    SKIP_SAME = False
    PRUNE = True

    def _waits(self, eng, deps):
        need = {}
        own = self.csem[eng]
        for t in deps:
            if t is None:
                continue
            sem, val = t
            if self.SKIP_SAME and sem is own:
                continue
            k = sem.num
            if k not in need or need[k][1] < val:
                need[k] = (sem, val)
        out = []
        w = self.waited[eng]
        for k, (sem, val) in need.items():
            if self.PRUNE and w.get(k, -1) >= val:
                continue
            w[k] = val
            out.append((sem, val))
        return out

    def op(self, eng, fn, deps=()):
        waits = self._waits(eng, deps)
        self.cnt[eng] += 1
        tok = (self.csem[eng], self.cnt[eng])
        self.ops[eng].append((waits, fn, tok[0], 1))
        return tok

    def dma(self, q, fn, deps=()):
        slot = self.drr[q] % self.NSLOT
        self.drr[q] += 1
        sem = self.dsem[(q, slot)]
        prev = self.dcnt[(q, slot)]
        deps = list(deps)
        if prev:
            deps.append((sem, prev))
        waits = self._waits(q, deps)
        self.dcnt[(q, slot)] = prev + 16
        self.ops[q].append((waits, fn, sem, 16))
        return (sem, prev + 16)

    def cc(self, fn, deps=()):
        sem = self.nc.alloc_semaphore(name=f"ccs{self.n_cc}")
        self.n_cc += 1
        waits = self._waits("gpsimd", deps)
        self.ops["gpsimd"].append((waits, fn, sem, 1))
        return (sem, 1)

    def C(self, eng, fn, kind="comp"):
        deps = self.serial_last
        if kind == "dma":
            tok = self.dma(eng, fn, deps)
        elif kind == "cc":
            tok = self.cc(fn, deps)
        else:
            tok = self.op(eng, fn, deps)
        if self._grp is None:
            self.serial_last = (tok,)
        else:
            self._grp.append(tok)
        return tok

    def group(self):
        sched = self

        class G:
            def __enter__(self):
                sched._grp = []

            def __exit__(self, *a):
                if sched._grp:
                    sched.serial_last = tuple(sched._grp)
                sched._grp = None
                return False

        return G()

    def all_tokens(self):
        toks = [(self.csem[e], self.cnt[e]) for e in ENGS if self.cnt[e] > 0]
        for key, c in self.dcnt.items():
            if c > 0:
                toks.append((self.dsem[key], c))
        return tuple(toks)

    def emit(self, block, drain=()):
        fin = self._waits("sync", drain)
        for e in ENGS:
            ops = self.ops[e]
            tail = fin if e == "sync" else []

            def body(eng, ops=ops, tail=tail):
                for waits, fn, sem, inc in ops:
                    for s, v in waits:
                        eng.wait_ge(s, v)
                    inst = fn(eng)
                    if inst is not None and inc:
                        inst.then_inc(sem, inc)
                for s, v in tail:
                    eng.wait_ge(s, v)

            getattr(block, e)(body)


class Pool:

    def __init__(self, nc, name, nbytes):
        self.t = nc.alloc_sbuf_tensor(name, [128, nbytes], mybir.dt.uint8)
        self.off = 0
        self.peak = 0
        self.nbytes = nbytes

    def alloc(self, p, n, dtype):
        esz = mybir.dt.size(dtype)
        start = (self.off + 31) // 32 * 32
        self.off = start + n * esz
        self.peak = max(self.peak, self.off)
        assert self.off <= self.nbytes, f"pool overflow {self.off}"
        return self.t[0:p, start:start + n * esz].bitcast(dtype)

    def mark(self):
        return self.off

    def reset(self, m):
        self.off = m


def build(dbg=0):
    nc = bass.Bass("TRN2", target_bir_lowering=False)
    RG = [list(range(8))]

    x_sh = nc.dram_tensor("x_sh", [TSH, D], F32, kind="ExternalInput").ap()
    wq_sl = nc.dram_tensor("wq_sl", [D, 1024], F32, kind="ExternalInput").ap()
    wz_sl = nc.dram_tensor("wz_sl", [D, 512], F32, kind="ExternalInput").ap()
    wba_sl = nc.dram_tensor("wba_sl", [D, 8], F32, kind="ExternalInput").ap()
    cwT_sl = nc.dram_tensor("cwT_sl", [1024, 4], F32, kind="ExternalInput").ap()
    dta8 = nc.dram_tensor("dta8", [8, 1], F32, kind="ExternalInput").ap()
    alog8 = nc.dram_tensor("alog8", [8, 1], F32, kind="ExternalInput").ap()
    normw = nc.dram_tensor("normw", [128, 1], F32, kind="ExternalInput").ap()
    wo_sl = nc.dram_tensor("wo_sl", [512, D], F32, kind="ExternalInput").ap()
    w1_sl = nc.dram_tensor("w1_sl", [1024, D], F32, kind="ExternalInput").ap()
    w3_sl = nc.dram_tensor("w3_sl", [1024, D], F32, kind="ExternalInput").ap()
    w2_sl = nc.dram_tensor("w2_sl", [D, 1024], F32, kind="ExternalInput").ap()
    out_sh = nc.dram_tensor("out_sh", [TSH, D], F32, kind="ExternalOutput").ap()

    dram = nc.dram_tensor
    xln_sl = dram("xln_sl", [TSH, D], BF16).ap()
    ag_xln = dram("ag_xln", [BT, D], BF16).ap()
    w1b_sl = dram("w1b_sl", [1024, D], BF16).ap()
    w3b_sl = dram("w3b_sl", [1024, D], BF16).ap()
    w2b_sl = dram("w2b_sl", [D, 1024], BF16).ap()
    w2t_sl = dram("w2t_sl", [1024, D], BF16).ap()
    wob_sl = dram("wob_sl", [512, D], BF16).ap()
    ag_w1 = dram("ag_w1", [8192, D], BF16).ap()
    ag_w3 = dram("ag_w3", [8192, D], BF16).ap()
    ag_w2t = dram("ag_w2t", [8192, D], BF16).ap()

    qb_hbm = dram("qb_hbm", [256, BT], BF16).ap()
    kb_hbm = dram("kb_hbm", [256, BT], BF16).ap()
    vb_hbm = dram("vb_hbm", [512, BT], BF16).ap()
    zb_hbm = dram("zb_hbm", [512, BT], BF16).ap()
    lng_hbm = dram("lng_hbm", [8, BT], F32).ap()
    lr_hbm = dram("lr_hbm", [4, BT], F32).ap()
    glay = dram("glay", [256, 128], F32).ap()
    blay = dram("blay", [256, 128], F32).ap()
    lrklay = dram("lrklay", [256, 128], F32).ap()
    lrqlay = dram("lrqlay", [256, 128], F32).ap()
    evec_hbm = dram("evec_hbm", [3, 256, CCH], F32).ap()
    scl_hbm = dram("scl_hbm", [4, 256, CCH], BF16).ap()
    gcx_hbm = dram("gcx_hbm", [256, 1], F32).ap()
    gated_tm = dram("gated_tm", [BT, 512], BF16).ap()
    part_hbm = dram("part_hbm", [BT, D], BF16).ap()
    rs_out = dram("rs_out", [TSH, D], BF16).ap()
    x2_hbm = dram("x2_hbm", [TSH, D], F32).ap()
    x2ln_hbm = dram("x2ln_hbm", [TSH, D], BF16).ap()
    w1t_hbm = dram("w1t_hbm", [D, 8192], BF16).ap()
    w3t_hbm = dram("w3t_hbm", [D, 8192], BF16).ap()
    ybuf_hbm = dram("ybuf_hbm", [64, 128, TSH], BF16).ap()
    h2_hbm = dram("h2_hbm", [8, 2, 64, 64, 128], BF16).ap()

    dbg_aps = {}
    if dbg >= 1:
        dbg_aps["d_mix"] = dram("d_mix", [1024, BT], BF16, kind="ExternalOutput").ap()
        dbg_aps["d_lng"] = dram("d_lng", [8, BT], F32, kind="ExternalOutput").ap()
        dbg_aps["d_lr"] = dram("d_lr", [4, BT], F32, kind="ExternalOutput").ap()
        dbg_aps["d_xln"] = dram("d_xln", [BT, D], BF16, kind="ExternalOutput").ap()
        dbg_aps["d_z"] = dram("d_z", [512, BT], BF16, kind="ExternalOutput").ap()
    if dbg >= 2:
        dbg_aps["d_gat"] = dram("d_gat", [BT, 512], BF16, kind="ExternalOutput").ap()
    if dbg >= 3:
        dbg_aps["d_x2"] = dram("d_x2", [TSH, D], F32, kind="ExternalOutput").ap()

    ti = np.arange(CCH)
    maskL = np.where(ti[:, None] > ti[None, :], 0.0, NEGINF).astype(np.float32)
    maskU = maskL.T.copy()
    maskQ = np.where(ti[:, None] <= ti[None, :], -0.5 * LN128, NEGINF).astype(np.float32)
    c_maskL = nc.inline_tensor(maskL, name="c_maskL").ap()
    c_maskU = nc.inline_tensor(maskU, name="c_maskU").ap()
    c_maskQ = nc.inline_tensor(maskQ, name="c_maskQ").ap()
    c_idf = nc.inline_tensor(np.eye(128, dtype=np.float32), name="c_idf").ap()
    c_idb = nc.inline_tensor(np.eye(128, dtype=BF), name="c_idb").ap()
    c_onesb = nc.inline_tensor(np.ones((128, 1), dtype=BF), name="c_onesb").ap()
    c_onesf = nc.inline_tensor(np.ones((1, 128), dtype=np.float32), name="c_onesf").ap()
    c_sign8 = nc.inline_tensor(
        np.array([-1, -1, -1, -1, 1, 1, 1, 1], dtype=np.float32)[:, None], name="c_sign8").ap()
    c_h128 = nc.inline_tensor(_hadamard(128).astype(BF), name="c_h128").ap()
    c_h64 = nc.inline_tensor(_hadamard(64).astype(BF), name="c_h64").ap()
    c_bias = nc.inline_tensor(np.tile(np.array(
        [LNEPS, EPS, 1.0, -0.5 * LN128], dtype=np.float32), (128, 1)), name="c_bias").ap()

    sb = nc.alloc_sbuf_tensor
    cst = sb("cst", [128, 4 * 128 + 16 + 128 + 4], F32)
    idbt = sb("idbt", [128, 129], BF16)
    h128t = sb("h128t", [128, 128], BF16)
    h64t = sb("h64t", [64, 64], BF16)
    St = [sb(f"St{p}", [128, 128], F32) for p in range(8)]
    St2 = [sb(f"St2_{p}", [128, 128], F32) for p in range(8)]
    Sbt = [sb(f"Sbt{p}", [128, 128], BF16) for p in range(8)]
    sclT = sb("sclT", [128, 4 * 256], BF16)
    sclF = sb("sclF", [128, 4 * 256], F32)
    gcb = sb("gcb", [128, 256], F32)
    eops = sb("eops", [2, 5 * 1024], F32)

    MKL, MKU, MKQ = cst[:, 0:128], cst[:, 128:256], cst[:, 256:384]
    IDF = cst[:, 384:512]
    SGN8, DTA8, ALOG8 = cst[0:8, 512:513], cst[0:8, 513:514], cst[0:8, 514:515]
    NRMW = cst[0:128, 515:516]
    ONESF = cst[0:1, 528:656]
    BLN = cst[:, 656:657]
    BEP = cst[:, 657:658]
    BONE = cst[:, 658:659]
    BQM = cst[:, 659:660]
    IDB, ONEB = idbt[:, 0:128], idbt[:, 128:129]
    LA = eops[:, 0 * 1024:1 * 1024]
    LB = eops[:, 1 * 1024:2 * 1024]
    RB = eops[:, 2 * 1024:3 * 1024]
    RA = eops[:, 3 * 1024:4 * 1024]
    RQ = eops[:, 4 * 1024:5 * 1024]
    nrscT = sclF[:, 0:256]
    oscT = sclF[:, 256:512]
    dscT = sclF[:, 512:768]
    bexT = sclF[:, 768:1024]

    pool = Pool(nc, "pool", 167 * 1024)

    ps = nc.alloc_psum_tensor
    psBig = ps("psBig", [128, 1024], F32)
    psBig2 = ps("psBig2", [128, 1024], F32)
    psS1 = ps("psS1", [128, 512], F32)
    psA = ps("psA", [128, 512], F32)
    psB = ps("psB", [128, 512], F32)

    with nc.Block() as block:
        ch = Sched(nc)
        O = ch.C

        def ln_tile(xt, outb, scr, st, eps=None, out_f32=None, res=None):
            O("vector", lambda e: e.reduce_sum(out=st[:, 0:1], in_=xt, axis=AX.X))
            O("vector", lambda e: e.scalar_tensor_tensor(
                out=scr, in0=xt, scalar=1.0, in1=xt, op0=ALU.mult, op1=ALU.mult,
                accum_out=st[:, 1:2]))
            O("vector", lambda e: e.tensor_scalar(
                out=st[:, 2:3], in0=st[:, 0:1], scalar1=-1.0 / D, scalar2=None, op0=ALU.mult))
            O("vector", lambda e: e.tensor_tensor(
                out=st[:, 3:4], in0=st[:, 2:3], in1=st[:, 2:3], op=ALU.mult))
            O("vector", lambda e: e.scalar_tensor_tensor(
                out=st[:, 4:5], in0=st[:, 1:2], scalar=1.0 / D, in1=st[:, 3:4],
                op0=ALU.mult, op1=ALU.subtract))
            ba = eps if eps is not None else BLN
            O("scalar", lambda e, ba=ba: e.activation(out=st[:, 5:6], in_=st[:, 4:5], func=AF.Ln, bias=ba))
            O("scalar", lambda e: e.activation(out=st[:, 6:7], in_=st[:, 5:6], func=AF.Exp, scale=-0.5))
            if outb is not None:
                O("vector", lambda e: e.tensor_scalar(
                    out=outb, in0=xt, scalar1=st[:, 2:3], scalar2=st[:, 6:7],
                    op0=ALU.add, op1=ALU.mult))
            if res is not None:
                O("vector", lambda e: e.tensor_scalar(
                    out=scr, in0=xt, scalar1=st[:, 2:3], scalar2=st[:, 6:7],
                    op0=ALU.add, op1=ALU.mult))
                O("vector", lambda e: e.tensor_tensor(out=out_f32, in0=scr, in1=res, op=ALU.add))

        with ch.group():
            O("gpsimd", lambda e: e.dma_start(out=w1b_sl[:], in_=w1_sl[:]), "dma")
            O("gpsimd", lambda e: e.dma_start(out=w3b_sl[:], in_=w3_sl[:]), "dma")
            O("gpsimd", lambda e: e.dma_start(out=w2b_sl[:], in_=w2_sl[:]), "dma")
            O("sync", lambda e: e.dma_start(out=MKL, in_=c_maskL[:]), "dma")
            O("sync", lambda e: e.dma_start(out=MKU, in_=c_maskU[:]), "dma")
            O("sync", lambda e: e.dma_start(out=MKQ, in_=c_maskQ[:]), "dma")
            O("sync", lambda e: e.dma_start(out=IDF, in_=c_idf[:]), "dma")
            O("sync", lambda e: e.dma_start(out=SGN8, in_=c_sign8[:]), "dma")
            O("sync", lambda e: e.dma_start(out=DTA8, in_=dta8[:]), "dma")
            O("sync", lambda e: e.dma_start(out=ALOG8, in_=alog8[:]), "dma")
            O("sync", lambda e: e.dma_start(out=NRMW, in_=normw[:]), "dma")
            O("sync", lambda e: e.dma_start(out=ONESF, in_=c_onesf[:]), "dma")
            O("sync", lambda e: e.dma_start(out=idbt[:, 0:128], in_=c_idb[:]), "dma")
            O("sync", lambda e: e.dma_start(out=idbt[:, 128:129], in_=c_onesb[:]), "dma")
            O("sync", lambda e: e.dma_start(out=h128t[:], in_=c_h128[:]), "dma")
            O("sync", lambda e: e.dma_start(out=h64t[:], in_=c_h64[:]), "dma")
            O("sync", lambda e: e.dma_start(out=cst[:, 656:660], in_=c_bias[:]), "dma")

        m0 = pool.mark()
        coef8 = cst[0:8, 516:517]
        ncoef8 = cst[0:8, 517:518]
        O("scalar", lambda e: e.activation(out=coef8, in_=ALOG8, func=AF.Exp))
        O("vector", lambda e: e.tensor_scalar(
            out=ncoef8, in0=coef8, scalar1=-1.0, scalar2=None, op0=ALU.mult))

        w2tt = pool.alloc(128, D, BF16)
        for i in range(8):
            O("sync", lambda e, i=i: e.dma_start(
                out=w2tt, in_=w2b_sl[:, i * 128:(i + 1) * 128], transpose=True), "dma")
            O("sync", lambda e, i=i: e.dma_start(
                out=w2t_sl[i * 128:(i + 1) * 128, :], in_=w2tt), "dma")

        wot_f = pool.alloc(128, D, F32)
        wot_b = pool.alloc(128, D, BF16)
        for i in range(4):
            O("sync", lambda e, i=i: e.dma_start(out=wot_f, in_=wo_sl[i * 128:(i + 1) * 128, :]), "dma")
            O("vector", lambda e: e.tensor_scalar(
                out=wot_b, in0=wot_f, scalar1=NRMW, scalar2=None, op0=ALU.mult))
            O("sync", lambda e, i=i: e.dma_start(out=wob_sl[i * 128:(i + 1) * 128, :], in_=wot_b), "dma")

        xt0 = pool.alloc(128, D, F32)
        scr0 = pool.alloc(128, D, F32)
        st0 = pool.alloc(128, 8, F32)
        xlnb0 = pool.alloc(128, D, BF16)
        for i in range(8):
            O("sync", lambda e, i=i: e.dma_start(out=xt0, in_=x_sh[i * 128:(i + 1) * 128, :]), "dma")
            ln_tile(xt0, xlnb0, scr0, st0)
            O("sync", lambda e, i=i: e.dma_start(out=xln_sl[i * 128:(i + 1) * 128, :], in_=xlnb0), "dma")

        O("gpsimd", lambda e: e.collective_compute(
            "AllGather", ALU.bypass, replica_groups=RG, ins=[xln_sl[:]], outs=[ag_xln[:]]), "cc")
        O("gpsimd", lambda e: e.collective_compute(
            "AllGather", ALU.bypass, replica_groups=RG, ins=[w1b_sl[:]], outs=[ag_w1[:]]), "cc")
        O("gpsimd", lambda e: e.collective_compute(
            "AllGather", ALU.bypass, replica_groups=RG, ins=[w3b_sl[:]], outs=[ag_w3[:]]), "cc")
        O("gpsimd", lambda e: e.collective_compute(
            "AllGather", ALU.bypass, replica_groups=RG, ins=[w2t_sl[:]], outs=[ag_w2t[:]]), "cc")
        if dbg >= 1:
            O("gpsimd", lambda e: e.dma_start(out=dbg_aps["d_xln"][:], in_=ag_xln[:]), "dma")
        pool.reset(m0)

        batile = pool.alloc(8, BT, F32)
        m1b = pool.mark()
        wqb = pool.alloc(128, 16 * 1024, BF16)
        wzb = pool.alloc(128, 16 * 512, BF16)
        wbab = pool.alloc(128, 16 * 8, BF16)
        wst = pool.alloc(128, 1024, F32)
        cwt = pool.alloc(128, 32, F32)
        for k in range(16):
            O("sync", lambda e, k=k: e.dma_start(out=wst, in_=wq_sl[k * 128:(k + 1) * 128, :]), "dma")
            O("scalar", lambda e, k=k: e.copy(out=wqb[:, k * 1024:(k + 1) * 1024], in_=wst))
            O("sync", lambda e, k=k: e.dma_start(out=wst[:, 0:512], in_=wz_sl[k * 128:(k + 1) * 128, :]), "dma")
            O("scalar", lambda e, k=k: e.copy(out=wzb[:, k * 512:(k + 1) * 512], in_=wst[:, 0:512]))
            O("sync", lambda e, k=k: e.dma_start(out=wst[:, 0:8], in_=wba_sl[k * 128:(k + 1) * 128, :]), "dma")
            O("scalar", lambda e, k=k: e.copy(out=wbab[:, k * 8:(k + 1) * 8], in_=wst[:, 0:8]))
        for m in range(8):
            O("sync", lambda e, m=m: e.dma_start(
                out=cwt[:, m * 4:(m + 1) * 4], in_=cwT_sl[m * 128:(m + 1) * 128, :]), "dma")

        xlt = pool.alloc(128, 16 * 1024, BF16)
        mc = pool.alloc(128, 1027, F32)
        acc0 = pool.alloc(128, 1024, F32)
        acc1 = pool.alloc(128, 1024, F32)
        tails = pool.alloc(128, 24, F32)
        sqt = pool.alloc(128, 1024, BF16)
        mtmp = pool.alloc(128, 1024, BF16)
        lrrow = pool.alloc(1, 1024, F32)

        def lhsT_ph1(m, k):
            if m < 8:
                return wqb[:, k * 1024 + m * 128: k * 1024 + (m + 1) * 128]
            if m < 12:
                return wzb[:, k * 512 + (m - 8) * 128: k * 512 + (m - 7) * 128]
            return wbab[:, k * 8:(k + 1) * 8]

        for tc in range(8):
            tsl = slice(tc * 1024, (tc + 1) * 1024)
            with ch.group():
                for k in range(16):
                    O("sync", lambda e, k=k, tc=tc: e.dma_start(
                        out=xlt[:, k * 1024:(k + 1) * 1024],
                        in_=ag_xln[tc * 1024:(tc + 1) * 1024, k * 128:(k + 1) * 128],
                        transpose=True), "dma")
            for m in range(13):
                pst = psBig if (m % 2 == 0) else psBig2
                pm = pst[0:128, :] if m < 12 else pst[0:8, :]
                for k in range(16):
                    for h in range(2):
                        O("tensor", lambda e, m=m, k=k, h=h, pm=pm: e.matmul(
                            pm[:, h * 512:(h + 1) * 512], lhsT=lhsT_ph1(m, k),
                            rhs=xlt[:, k * 1024 + h * 512: k * 1024 + (h + 1) * 512],
                            start=(k == 0), stop=(k == 15)))
                if m < 8:
                    O("scalar", lambda e, pm=pm: e.copy(out=mc[:, 3:1027], in_=pm))
                    if tc in (0, 4):
                        O("vector", lambda e: e.memset(mc[:, 0:3], 0.0))
                    else:
                        O("scalar", lambda e, m=m: e.copy(out=mc[:, 0:3], in_=tails[:, m * 3:(m + 1) * 3]))
                    O("scalar", lambda e, m=m: e.copy(out=tails[:, m * 3:(m + 1) * 3], in_=mc[:, 1024:1027]))
                    O("vector", lambda e, m=m: e.tensor_scalar(
                        out=acc0, in0=mc[:, 3:1027], scalar1=cwt[:, m * 4 + 3:m * 4 + 4],
                        scalar2=None, op0=ALU.mult))
                    O("vector", lambda e, m=m: e.scalar_tensor_tensor(
                        out=acc1, in0=mc[:, 2:1026], scalar=cwt[:, m * 4 + 2:m * 4 + 3],
                        in1=acc0, op0=ALU.mult, op1=ALU.add))
                    O("vector", lambda e, m=m: e.scalar_tensor_tensor(
                        out=acc0, in0=mc[:, 1:1025], scalar=cwt[:, m * 4 + 1:m * 4 + 2],
                        in1=acc1, op0=ALU.mult, op1=ALU.add))
                    O("vector", lambda e, m=m: e.scalar_tensor_tensor(
                        out=acc1, in0=mc[:, 0:1024], scalar=cwt[:, m * 4 + 0:m * 4 + 1],
                        in1=acc0, op0=ALU.mult, op1=ALU.add))
                    O("scalar", lambda e: e.activation(out=mtmp, in_=acc1, func=AF.Silu))
                    if m < 2:
                        O("sync", lambda e, m=m, tsl=tsl: e.dma_start(
                            out=qb_hbm[m * 128:(m + 1) * 128, tsl], in_=mtmp), "dma")
                    elif m < 4:
                        O("sync", lambda e, m=m, tsl=tsl: e.dma_start(
                            out=kb_hbm[(m - 2) * 128:(m - 1) * 128, tsl], in_=mtmp), "dma")
                    else:
                        O("sync", lambda e, m=m, tsl=tsl: e.dma_start(
                            out=vb_hbm[(m - 4) * 128:(m - 3) * 128, tsl], in_=mtmp), "dma")
                    if m < 4:
                        O("scalar", lambda e: e.activation(out=sqt, in_=mtmp, func=AF.Square))
                        O("tensor", lambda e: e.matmul(
                            psS1[0:1, 0:512], lhsT=ONEB, rhs=sqt[:, 0:512], start=True, stop=True))
                        O("tensor", lambda e: e.matmul(
                            psA[0:1, 0:512], lhsT=ONEB, rhs=sqt[:, 512:1024], start=True, stop=True))
                        O("scalar", lambda e: e.activation(
                            out=lrrow[:, 0:512], in_=psS1[0:1, 0:512], func=AF.Ln, bias=BEP[0:1, :]))
                        O("scalar", lambda e: e.activation(
                            out=lrrow[:, 512:1024], in_=psA[0:1, 0:512], func=AF.Ln, bias=BEP[0:1, :]))
                        O("sync", lambda e, m=m, tsl=tsl: e.dma_start(
                            out=lr_hbm[m:m + 1, tsl], in_=lrrow), "dma")
                        lay = lrqlay if m < 2 else lrklay
                        hh = m % 2
                        bb, c0 = tc // 4, (tc % 4) * 8
                        for r in range(2):
                            base_ch = c0 * 8 + (2 * hh + r) * 2 + bb
                            O("sync", lambda e, lay=lay, base_ch=base_ch: e.dma_start(
                                out=lay[base_ch:base_ch + 57:8, :],
                                in_=lrrow.rearrange("o (c p) -> o c p", p=128)), "dma")
                    if dbg >= 1:
                        O("sync", lambda e, m=m, tsl=tsl: e.dma_start(
                            out=dbg_aps["d_mix"][m * 128:(m + 1) * 128, tsl], in_=mtmp), "dma")
                elif m < 12:
                    O("scalar", lambda e, pm=pm: e.copy(out=mtmp, in_=pm))
                    O("sync", lambda e, m=m, tsl=tsl: e.dma_start(
                        out=zb_hbm[(m - 8) * 128:(m - 7) * 128, tsl], in_=mtmp), "dma")
                    if dbg >= 1:
                        O("sync", lambda e, m=m, tsl=tsl: e.dma_start(
                            out=dbg_aps["d_z"][(m - 8) * 128:(m - 7) * 128, tsl], in_=mtmp), "dma")
                else:
                    O("scalar", lambda e, pm=pm, tc=tc: e.copy(
                        out=batile[:, tc * 1024:(tc + 1) * 1024], in_=pm))

        pool.reset(m1b)
        spA = pool.alloc(8, BT, F32)
        spB = pool.alloc(8, BT, F32)
        lngt = pool.alloc(8, BT, F32)
        O("vector", lambda e: e.tensor_scalar(
            out=spA, in0=batile, scalar1=SGN8, scalar2=DTA8, op0=ALU.mult, op1=ALU.add))
        O("scalar", lambda e: e.activation(out=spB, in_=spA, func=AF.Abs))
        O("scalar", lambda e: e.activation(out=spB, in_=spB, func=AF.Exp, scale=-1.0))
        O("scalar", lambda e: e.activation(out=spB, in_=spB, func=AF.Ln, bias=BONE[0:8, :]))
        O("vector", lambda e: e.tensor_scalar(
            out=spA, in0=spA, scalar1=0.0, scalar2=None, op0=ALU.max))
        O("vector", lambda e: e.tensor_tensor(out=spA, in0=spA, in1=spB, op=ALU.add))
        O("vector", lambda e: e.tensor_scalar(
            out=lngt, in0=spA, scalar1=ncoef8, scalar2=None, op0=ALU.mult))
        O("sync", lambda e: e.dma_start(out=lng_hbm[:], in_=lngt), "dma")
        O("sync", lambda e: e.dma_start(
            out=glay.rearrange("(c h b) p -> h b c p", h=4, b=2),
            in_=lngt[4:8, :].rearrange("h (b c p) -> h b c p", b=2, c=32)), "dma")
        O("sync", lambda e: e.dma_start(
            out=blay.rearrange("(c h b) p -> h b c p", h=4, b=2),
            in_=lngt[0:4, :].rearrange("h (b c p) -> h b c p", b=2, c=32)), "dma")
        if dbg >= 1:
            O("sync", lambda e: e.dma_start(out=dbg_aps["d_lng"][:], in_=lngt), "dma")
            O("sync", lambda e: e.dma_start(out=dbg_aps["d_lr"][:], in_=lr_hbm[:]), "dma")

        cgt = [pool.alloc(128, 128, F32) for _ in range(2)]
        cgt2 = [pool.alloc(128, 128, F32) for _ in range(2)]
        lnbt = [pool.alloc(128, 128, F32) for _ in range(2)]
        lrkt = [pool.alloc(128, 128, F32) for _ in range(2)]
        lrqt = [pool.alloc(128, 128, F32) for _ in range(2)]
        for t2 in range(2):
            cs = slice(t2 * 128, (t2 + 1) * 128)
            with ch.group():
                O("sync", lambda e, t2=t2, cs=cs: e.dma_start(out=cgt[t2][:, :], in_=glay[cs]), "dma")
                O("sync", lambda e, t2=t2, cs=cs: e.dma_start(out=lnbt[t2][:, :], in_=blay[cs]), "dma")
                O("sync", lambda e, t2=t2, cs=cs: e.dma_start(out=lrkt[t2][:, :], in_=lrklay[cs]), "dma")
                O("sync", lambda e, t2=t2, cs=cs: e.dma_start(out=lrqt[t2][:, :], in_=lrqlay[cs]), "dma")
        for t2 in range(2):
            src, dst = cgt[t2], cgt2[t2]
            s = 1
            while s < 128:
                O("vector", lambda e, src=src, dst=dst, s=s: e.tensor_tensor(
                    out=dst[:, s:128], in0=src[:, s:128], in1=src[:, 0:128 - s], op=ALU.add))
                O("vector", lambda e, src=src, dst=dst, s=s: e.tensor_copy(
                    out=dst[:, 0:s], in_=src[:, 0:s]))
                src, dst = dst, src
                s *= 2
            if src is not cgt[t2]:
                O("vector", lambda e, src=src, t2=t2: e.tensor_copy(out=cgt[t2], in_=src))

        av = [pool.alloc(128, 128, F32) for _ in range(2)]
        bv = [pool.alloc(128, 128, F32) for _ in range(2)]
        qv = [pool.alloc(128, 128, F32) for _ in range(2)]
        ncg = pool.alloc(128, 128, F32)
        sct = pool.alloc(128, 128, F32)
        scb = pool.alloc(128, 128, BF16)
        for t2 in range(2):
            O("vector", lambda e, t2=t2: e.scalar_tensor_tensor(
                out=av[t2], in0=lrkt[t2], scalar=-0.5, in1=cgt[t2], op0=ALU.mult, op1=ALU.add))
            O("vector", lambda e, t2=t2: e.tensor_tensor(
                out=av[t2], in0=av[t2], in1=lnbt[t2], op=ALU.add))
            O("vector", lambda e, t2=t2: e.tensor_scalar(
                out=ncg, in0=cgt[t2], scalar1=-1.0, scalar2=None, op0=ALU.mult))
            O("vector", lambda e, t2=t2: e.scalar_tensor_tensor(
                out=bv[t2], in0=lrkt[t2], scalar=-0.5, in1=ncg, op0=ALU.mult, op1=ALU.add))
            O("vector", lambda e, t2=t2: e.scalar_tensor_tensor(
                out=qv[t2], in0=lrqt[t2], scalar=-0.5, in1=cgt[t2], op0=ALU.mult, op1=ALU.add))
            O("scalar", lambda e, t2=t2: e.activation(out=sct, in_=av[t2], func=AF.Exp))
            O("vector", lambda e: e.tensor_scalar(
                out=scb, in0=sct, scalar1=-1.0, scalar2=None, op0=ALU.mult))
            O("sync", lambda e, t2=t2: e.dma_start(
                out=scl_hbm[0, t2 * 128:(t2 + 1) * 128, :], in_=scb), "dma")
            O("scalar", lambda e, t2=t2: e.activation(
                out=scb, in_=qv[t2], func=AF.Exp, bias=BQM))
            O("sync", lambda e, t2=t2: e.dma_start(
                out=scl_hbm[1, t2 * 128:(t2 + 1) * 128, :], in_=scb), "dma")
            O("vector", lambda e, t2=t2: e.tensor_scalar(
                out=sct, in0=bv[t2], scalar1=cgt[t2][:, 127:128], scalar2=None, op0=ALU.add))
            O("scalar", lambda e: e.activation(out=scb, in_=sct, func=AF.Exp))
            O("sync", lambda e, t2=t2: e.dma_start(
                out=scl_hbm[2, t2 * 128:(t2 + 1) * 128, :], in_=scb), "dma")
            O("scalar", lambda e, t2=t2: e.activation(out=scb, in_=lnbt[t2], func=AF.Exp))
            O("sync", lambda e, t2=t2: e.dma_start(
                out=scl_hbm[3, t2 * 128:(t2 + 1) * 128, :], in_=scb), "dma")
            O("sync", lambda e, t2=t2: e.dma_start(
                out=evec_hbm[0, t2 * 128:(t2 + 1) * 128, :], in_=av[t2]), "dma")
            O("sync", lambda e, t2=t2: e.dma_start(
                out=evec_hbm[1, t2 * 128:(t2 + 1) * 128, :], in_=bv[t2]), "dma")
            O("sync", lambda e, t2=t2: e.dma_start(
                out=evec_hbm[2, t2 * 128:(t2 + 1) * 128, :], in_=qv[t2]), "dma")
            O("sync", lambda e, t2=t2: e.dma_start(
                out=gcx_hbm[t2 * 128:(t2 + 1) * 128, :], in_=cgt[t2][:, 127:128]), "dma")
        for i in range(4):
            O("sync", lambda e, i=i: e.dma_start(
                out=sclT[:, i * 256:(i + 1) * 256], in_=scl_hbm[i], transpose=True), "dma")
            O("scalar", lambda e, i=i: e.copy(
                out=sclF[:, i * 256:(i + 1) * 256], in_=sclT[:, i * 256:(i + 1) * 256]))
        gcrow = pool.alloc(1, 256, F32)
        O("sync", lambda e: e.dma_start(
            out=gcrow, in_=gcx_hbm.rearrange("p o -> o p")), "dma")
        O("scalar", lambda e: e.activation(out=gcrow, in_=gcrow, func=AF.Exp))
        O("tensor", lambda e: e.matmul(psS1[:, 0:256], lhsT=ONESF, rhs=gcrow, start=True, stop=True))
        O("scalar", lambda e: e.copy(out=gcb[:], in_=psS1[:, 0:256]))
        pool.reset(0)

        for p in range(8):
            O("vector", lambda e, p=p: e.memset(St[p][:], 0.0))
            O("vector", lambda e, p=p: e.memset(Sbt[p][:], 0.0))
        O("vector", lambda e: e.memset(LA[:], 1.0))
        O("vector", lambda e: e.memset(LB[:], 1.0))
        O("vector", lambda e: e.memset(RB[:], 1.0))
        O("vector", lambda e: e.memset(RA[:], 1.0))
        O("vector", lambda e: e.memset(RQ[:], 1.0))

        m2 = pool.mark()
        Ktm = [pool.alloc(128, 256, BF16) for _ in range(2)]
        Vtm = [pool.alloc(128, 512, BF16) for _ in range(2)]
        kfm = pool.alloc(128, 256, BF16)
        qfm = pool.alloc(128, 256, BF16)
        Ztm = pool.alloc(128, 512, BF16)
        Elow = pool.alloc(128, 128, BF16)
        Eup = pool.alloc(128, 128, BF16)
        Eq = pool.alloc(128, 128, BF16)
        mKK = pool.alloc(128, 128, BF16)
        Ptl = pool.alloc(128, 128, BF16)
        Zp = [pool.alloc(128, 128, BF16) for _ in range(7)]
        Pp = [pool.alloc(128, 128, BF16) for _ in range(6)]
        bVt = pool.alloc(128, 128, BF16)
        Xb = [pool.alloc(128, 128, BF16) for _ in range(2)]
        qssb = pool.alloc(128, 128, BF16)
        Dbar = pool.alloc(128, 128, BF16)
        Obuf = [pool.alloc(128, 512, BF16) for _ in range(2)]
        sqg = pool.alloc(128, 512, BF16)
        gst = pool.alloc(128, 12, F32)
        zst = pool.alloc(128, 512, BF16)
        ggt = pool.alloc(128, 512, BF16)

        psE = psA[:, 0:128]
        psKK = psA[:, 128:256]
        psKQ = psA[:, 256:384]
        psKV = psA[:, 384:512]
        psQS = psB[:, 0:128]
        psO = psB[:, 128:256]
        psX = psB[:, 256:384]
        psS = psB[:, 384:512]

        for ic in range(NCH):
            Scur = St if ic % 2 == 0 else St2
            Snxt = St2 if ic % 2 == 0 else St
            with ch.group():
                ev = evec_hbm[:, ic * 8:(ic + 1) * 8, :]
                O("sync", lambda e, ev=ev: e.dma_start(
                    out=LA[1:2, :], in_=ev[0].rearrange("c p -> () (c p)")), "dma")
                O("sync", lambda e, ev=ev: e.dma_start(
                    out=RA[0:1, :], in_=ev[0].rearrange("c p -> () (c p)")), "dma")
                O("sync", lambda e, ev=ev: e.dma_start(
                    out=LB[1:2, :], in_=ev[1].rearrange("c p -> () (c p)")), "dma")
                O("sync", lambda e, ev=ev: e.dma_start(
                    out=RB[0:1, :], in_=ev[1].rearrange("c p -> () (c p)")), "dma")
                O("sync", lambda e, ev=ev: e.dma_start(
                    out=RQ[0:1, :], in_=ev[2].rearrange("c p -> () (c p)")), "dma")
            for b in range(2):
                off = b * T + ic * 128
                with ch.group():
                    O("sync", lambda e, b=b, off=off: e.dma_start(
                        out=Ktm[b], in_=kb_hbm[:, off:off + 128], transpose=True), "dma")
                    O("sync", lambda e, b=b, off=off: e.dma_start(
                        out=Vtm[b], in_=vb_hbm[:, off:off + 128], transpose=True), "dma")
                    for hh in range(2):
                        O("sync", lambda e, hh=hh, off=off: e.dma_start(
                            out=kfm[:, hh * 128:(hh + 1) * 128],
                            in_=kb_hbm[hh * 128:(hh + 1) * 128, off:off + 128]), "dma")
                        O("sync", lambda e, hh=hh, off=off: e.dma_start(
                            out=qfm[:, hh * 128:(hh + 1) * 128],
                            in_=qb_hbm[hh * 128:(hh + 1) * 128, off:off + 128]), "dma")
                for vh in range(4):
                    p = b * 4 + vh
                    pp = vh * 2 + b
                    kh = vh // 2
                    cc = ic * 8 + pp
                    K_fm = kfm[:, kh * 128:(kh + 1) * 128]
                    Q_fm = qfm[:, kh * 128:(kh + 1) * 128]
                    Ktm_p = Ktm[b][:, kh * 128:(kh + 1) * 128]
                    V_p = Vtm[b][:, vh * 128:(vh + 1) * 128]
                    esl = slice(pp * 128, (pp + 1) * 128)

                    O("tensor", lambda e, esl=esl: e.matmul(
                        psE, lhsT=LA[:, esl], rhs=RB[:, esl], start=True, stop=False))
                    O("tensor", lambda e: e.matmul(psE, lhsT=IDF, rhs=MKL, start=False, stop=True))
                    O("scalar", lambda e: e.activation(out=Elow, in_=psE, func=AF.Exp))
                    O("tensor", lambda e, esl=esl: e.matmul(
                        psE, lhsT=LB[:, esl], rhs=RA[:, esl], start=True, stop=False))
                    O("tensor", lambda e: e.matmul(psE, lhsT=IDF, rhs=MKU, start=False, stop=True))
                    O("scalar", lambda e: e.activation(out=Eup, in_=psE, func=AF.Exp))
                    O("tensor", lambda e, esl=esl: e.matmul(
                        psE, lhsT=LB[:, esl], rhs=RQ[:, esl], start=True, stop=False))
                    O("tensor", lambda e: e.matmul(psE, lhsT=IDF, rhs=MKQ, start=False, stop=True))
                    O("scalar", lambda e: e.activation(out=Eq, in_=psE, func=AF.Exp))
                    O("tensor", lambda e, K_fm=K_fm: e.matmul(
                        psKK, lhsT=K_fm, rhs=K_fm, start=True, stop=True))
                    O("scalar", lambda e: e.activation(out=mKK, in_=psKK, func=AF.Copy, scale=-1.0))
                    O("tensor", lambda e, K_fm=K_fm, Q_fm=Q_fm: e.matmul(
                        psKQ, lhsT=K_fm, rhs=Q_fm, start=True, stop=True))
                    O("vector", lambda e: e.tensor_tensor(out=Ptl, in0=psKQ, in1=Eq, op=ALU.mult))
                    O("vector", lambda e: e.tensor_tensor(out=Zp[0], in0=mKK, in1=Eup, op=ALU.mult))
                    O("vector", lambda e: e.tensor_tensor(out=Pp[0], in0=mKK, in1=Elow, op=ALU.mult))
                    for j in range(1, 7):
                        O("tensor", lambda e, j=j: e.matmul(
                            psX, lhsT=Pp[j - 1], rhs=Zp[j - 1], start=True, stop=True))
                        if j % 2 == 0:
                            O("scalar", lambda e, j=j: e.copy(out=Zp[j], in_=psX))
                        else:
                            O("vector", lambda e, j=j: e.tensor_copy(out=Zp[j], in_=psX))
                        if j < 6:
                            O("tensor", lambda e, j=j: e.matmul(
                                psX, lhsT=Zp[j - 1], rhs=Pp[j - 1], start=True, stop=True))
                            if j % 2 == 0:
                                O("vector", lambda e, j=j: e.tensor_copy(out=Pp[j], in_=psX))
                            else:
                                O("scalar", lambda e, j=j: e.copy(out=Pp[j], in_=psX))
                    O("tensor", lambda e, K_fm=K_fm, p=p: e.matmul(
                        psKV, lhsT=K_fm, rhs=Sbt[p][:], start=True, stop=True))
                    O("vector", lambda e, V_p=V_p, cc=cc: e.tensor_scalar(
                        out=bVt, in0=V_p, scalar1=bexT[:, cc:cc + 1], scalar2=None, op0=ALU.mult))
                    O("vector", lambda e, cc=cc: e.scalar_tensor_tensor(
                        out=Xb[0], in0=psKV, scalar=nrscT[:, cc:cc + 1], in1=bVt,
                        op0=ALU.mult, op1=ALU.add))
                    for j in range(7):
                        xin, xout = Xb[j % 2], Xb[(j + 1) % 2]
                        O("tensor", lambda e, j=j, xin=xin: e.matmul(
                            psX, lhsT=Zp[j], rhs=xin, start=True, stop=False))
                        O("tensor", lambda e, xin=xin: e.matmul(
                            psX, lhsT=IDB, rhs=xin, start=False, stop=True))
                        if j % 2 == 0:
                            O("scalar", lambda e, xout=xout: e.copy(out=xout, in_=psX))
                        else:
                            O("vector", lambda e, xout=xout: e.tensor_copy(out=xout, in_=psX))
                    Delta = Xb[1]
                    O("tensor", lambda e, Delta=Delta: e.matmul(
                        psO, lhsT=Ptl, rhs=Delta, start=True, stop=True))
                    O("tensor", lambda e, Q_fm=Q_fm, p=p: e.matmul(
                        psQS, lhsT=Q_fm, rhs=Sbt[p][:], start=True, stop=True))
                    O("scalar", lambda e: e.copy(out=qssb, in_=psQS))
                    O("vector", lambda e, b=b, vh=vh, cc=cc: e.scalar_tensor_tensor(
                        out=Obuf[b][:, vh * 128:(vh + 1) * 128], in0=qssb,
                        scalar=oscT[:, cc:cc + 1], in1=psO, op0=ALU.mult, op1=ALU.add))
                    O("vector", lambda e, Delta=Delta, cc=cc: e.tensor_scalar(
                        out=Dbar, in0=Delta, scalar1=dscT[:, cc:cc + 1], scalar2=None, op0=ALU.mult))
                    O("tensor", lambda e, Ktm_p=Ktm_p: e.matmul(
                        psS, lhsT=Ktm_p, rhs=Dbar, start=True, stop=True))
                    O("vector", lambda e, p=p, cc=cc, Scur=Scur, Snxt=Snxt: e.scalar_tensor_tensor(
                        out=Snxt[p][:], in0=Scur[p][:], scalar=gcb[:, cc:cc + 1], in1=psS,
                        op0=ALU.mult, op1=ALU.add))
                    O("scalar", lambda e, p=p, Snxt=Snxt: e.copy(out=Sbt[p][:], in_=Snxt[p][:]))
                O("sync", lambda e, off=off: e.dma_start(
                    out=Ztm, in_=zb_hbm[:, off:off + 128], transpose=True), "dma")
                O("scalar", lambda e, b=b: e.activation(out=sqg, in_=Obuf[b], func=AF.Square))
                O("vector", lambda e: e.tensor_reduce(
                    out=gst[:, 0:4].rearrange("p (h o) -> p h o", o=1),
                    in_=sqg.rearrange("p (h d) -> p h d", h=4), axis=AX.X, op=ALU.add))
                O("scalar", lambda e: e.activation(
                    out=gst[:, 4:8], in_=gst[:, 0:4], func=AF.Ln, scale=1.0 / 128, bias=BEP))
                O("scalar", lambda e: e.activation(
                    out=gst[:, 8:12], in_=gst[:, 4:8], func=AF.Exp, scale=-0.5))
                O("scalar", lambda e: e.activation(out=zst, in_=Ztm, func=AF.Silu))
                for vh in range(4):
                    O("vector", lambda e, b=b, vh=vh: e.tensor_scalar(
                        out=ggt[:, vh * 128:(vh + 1) * 128],
                        in0=Obuf[b][:, vh * 128:(vh + 1) * 128],
                        scalar1=gst[:, 8 + vh:9 + vh], scalar2=None, op0=ALU.mult))
                O("vector", lambda e: e.tensor_tensor(out=ggt, in0=ggt, in1=zst, op=ALU.mult))
                O("sync", lambda e, off=off: e.dma_start(
                    out=gated_tm[off:off + 128, :], in_=ggt), "dma")
        pool.reset(m2)

        if dbg >= 2:
            O("sync", lambda e: e.dma_start(out=dbg_aps["d_gat"][:], in_=gated_tm[:]), "dma")

        m3 = pool.mark()
        gfm = [pool.alloc(128, BT, BF16) for _ in range(4)]
        wobq = pool.alloc(128, 4 * 512, BF16)
        ptile = pool.alloc(128, 512, BF16)
        oab = pool.alloc(128, 8 * 2048, BF16)
        for vt in range(4):
            O("sync", lambda e, vt=vt: e.dma_start(
                out=gfm[vt], in_=gated_tm[:, vt * 128:(vt + 1) * 128], transpose=True), "dma")
        for dq in range(4):
            with ch.group():
                for vt in range(4):
                    O("sync", lambda e, dq=dq, vt=vt: e.dma_start(
                        out=wobq[:, vt * 512:(vt + 1) * 512],
                        in_=wob_sl[vt * 128:(vt + 1) * 128, dq * 512:(dq + 1) * 512]), "dma")
            for tt in range(64):
                for vt in range(4):
                    O("tensor", lambda e, tt=tt, vt=vt: e.matmul(
                        psS1[:, :], lhsT=gfm[vt][:, tt * 128:(tt + 1) * 128],
                        rhs=wobq[:, vt * 512:(vt + 1) * 512], start=(vt == 0), stop=(vt == 3)))
                O("scalar", lambda e: e.copy(out=ptile, in_=psS1[:, :]))
                O("sync", lambda e, dq=dq, tt=tt: e.dma_start(
                    out=part_hbm[tt * 128:(tt + 1) * 128, dq * 512:(dq + 1) * 512],
                    in_=ptile), "dma")
        O("gpsimd", lambda e: e.collective_compute(
            "ReduceScatter", ALU.add, replica_groups=RG, ins=[part_hbm[:]], outs=[rs_out[:]]), "cc")
        pool.reset(m3)
        gfm_unused = None
        oab = pool.alloc(128, 8 * 2048, BF16)
        with ch.group():
            for tt in range(8):
                O("sync", lambda e, tt=tt: e.dma_start(
                    out=oab[:, tt * 2048:(tt + 1) * 2048],
                    in_=rs_out[tt * 128:(tt + 1) * 128, :]), "dma")
        oaf = pool.alloc(128, D, F32)
        xt3 = pool.alloc(128, D, F32)
        x2t = pool.alloc(128, D, F32)
        scr3 = pool.alloc(128, D, F32)
        st3 = pool.alloc(128, 8, F32)
        x2lb = pool.alloc(128, D, BF16)
        for tt in range(8):
            O("scalar", lambda e, tt=tt: e.copy(out=oaf, in_=oab[:, tt * 2048:(tt + 1) * 2048]))
            O("sync", lambda e, tt=tt: e.dma_start(out=xt3, in_=x_sh[tt * 128:(tt + 1) * 128, :]), "dma")
            ln_tile(oaf, None, scr3, st3, out_f32=x2t, res=xt3)
            O("sync", lambda e, tt=tt: e.dma_start(out=x2_hbm[tt * 128:(tt + 1) * 128, :], in_=x2t), "dma")
            ln_tile(x2t, x2lb, scr3, st3)
            O("sync", lambda e, tt=tt: e.dma_start(out=x2ln_hbm[tt * 128:(tt + 1) * 128, :], in_=x2lb), "dma")
            if dbg >= 3:
                O("sync", lambda e, tt=tt: e.dma_start(
                    out=dbg_aps["d_x2"][tt * 128:(tt + 1) * 128, :], in_=x2t), "dma")
        pool.reset(m3)

        wtt = pool.alloc(128, 8192, BF16)
        for k in range(16):
            O("sync", lambda e, k=k: e.dma_start(
                out=wtt, in_=ag_w1[:, k * 128:(k + 1) * 128], transpose=True), "dma")
            O("sync", lambda e, k=k: e.dma_start(out=w1t_hbm[k * 128:(k + 1) * 128, :], in_=wtt), "dma")
            O("sync", lambda e, k=k: e.dma_start(
                out=wtt, in_=ag_w3[:, k * 128:(k + 1) * 128], transpose=True), "dma")
            O("sync", lambda e, k=k: e.dma_start(out=w3t_hbm[k * 128:(k + 1) * 128, :], in_=wtt), "dma")
        pool.reset(m3)

        x2lT = pool.alloc(128, 16 * 1024, BF16)
        with ch.group():
            for k in range(16):
                O("sync", lambda e, k=k: e.dma_start(
                    out=x2lT[:, k * 1024:(k + 1) * 1024],
                    in_=x2ln_hbm[:, k * 128:(k + 1) * 128], transpose=True), "dma")
        m4pre = pool.mark()
        blk1 = pool.alloc(128, 4096, BF16)
        blk3 = pool.alloc(128, 4096, BF16)
        sg4 = pool.alloc(128, 1024, BF16)
        hid4 = pool.alloc(128, 1024, BF16)
        yt4 = pool.alloc(128, 1024, BF16)
        w1v = w1t_hbm.rearrange("(k p) m -> p k m", p=128)
        w3v = w3t_hbm.rearrange("(k p) m -> p k m", p=128)
        for mp in range(32):
            msl = slice(mp * 256, (mp + 1) * 256)
            with ch.group():
                O("sync", lambda e, msl=msl: e.dma_start(
                    out=blk1.rearrange("p (k m) -> p k m", k=16), in_=w1v[:, :, msl]), "dma")
                O("sync", lambda e, msl=msl: e.dma_start(
                    out=blk3.rearrange("p (k m) -> p k m", k=16), in_=w3v[:, :, msl]), "dma")
            for mh in range(2):
                m = 2 * mp + mh
                moff = mh * 128
                for k in range(16):
                    for h in range(2):
                        O("tensor", lambda e, k=k, h=h, moff=moff: e.matmul(
                            psBig[:, h * 512:(h + 1) * 512],
                            lhsT=blk1[:, k * 256 + moff: k * 256 + moff + 128],
                            rhs=x2lT[:, k * 1024 + h * 512: k * 1024 + (h + 1) * 512],
                            start=(k == 0), stop=(k == 15)))
                for k in range(16):
                    for h in range(2):
                        O("tensor", lambda e, k=k, h=h, moff=moff: e.matmul(
                            psBig2[:, h * 512:(h + 1) * 512],
                            lhsT=blk3[:, k * 256 + moff: k * 256 + moff + 128],
                            rhs=x2lT[:, k * 1024 + h * 512: k * 1024 + (h + 1) * 512],
                            start=(k == 0), stop=(k == 15)))
                O("scalar", lambda e: e.activation(out=sg4, in_=psBig[:, :], func=AF.Silu))
                O("vector", lambda e: e.tensor_tensor(out=hid4, in0=sg4, in1=psBig2[:, :], op=ALU.mult))
                for h in range(2):
                    O("tensor", lambda e, h=h: e.matmul(
                        psBig[:, h * 512:(h + 1) * 512], lhsT=h128t[:],
                        rhs=hid4[:, h * 512:(h + 1) * 512], start=True, stop=True))
                O("scalar", lambda e: e.activation(out=yt4, in_=psBig[:, :], func=AF.Copy, scale=1.0 / 128))
                O("sync", lambda e, m=m: e.dma_start(out=ybuf_hbm[m], in_=yt4), "dma")
        pool.reset(m3)

        yr = pool.alloc(64, 1024, BF16)
        h2t = pool.alloc(64, 1024, BF16)
        for j in range(128):
            O("sync", lambda e, j=j: e.dma_start(out=yr, in_=ybuf_hbm[:, j, :]), "dma")
            for h in range(2):
                O("tensor", lambda e, h=h: e.matmul(
                    psBig[0:64, h * 512:(h + 1) * 512], lhsT=h64t[:],
                    rhs=yr[:, h * 512:(h + 1) * 512], start=True, stop=True))
            O("scalar", lambda e: e.activation(
                out=h2t, in_=psBig[0:64, 0:1024], func=AF.Copy, scale=float(np.sqrt(2.0))))
            O("sync", lambda e, j=j: e.dma_start(
                out=h2_hbm[:, j % 2, :, j // 2, :].rearrange("tt a tp -> a tt tp"),
                in_=h2t.rearrange("a (tt tp) -> a tt tp", tt=8)), "dma")
        pool.reset(m3)

        w2q = pool.alloc(128, 64 * 512, BF16)
        h2blk = pool.alloc(128, 8192, BF16)
        ffob = pool.alloc(128, 8 * 2048, BF16)
        w2v = ag_w2t.rearrange("(a b) d -> b a d", a=64)
        for dq in range(4):
            with ch.group():
                for k in range(64):
                    O("sync", lambda e, dq=dq, k=k: e.dma_start(
                        out=w2q[:, k * 512:(k + 1) * 512],
                        in_=w2v[2 * k:2 * k + 2, :, dq * 512:(dq + 1) * 512]), "dma")
            for tt in range(8):
                O("sync", lambda e, tt=tt: e.dma_start(
                    out=h2blk, in_=h2_hbm[tt].rearrange("h a m p -> (h a) (m p)")), "dma")
                for k in range(64):
                    O("tensor", lambda e, k=k: e.matmul(
                        psS1[:, :], lhsT=h2blk[:, k * 128:(k + 1) * 128],
                        rhs=w2q[:, k * 512:(k + 1) * 512], start=(k == 0), stop=(k == 63)))
                O("scalar", lambda e, dq=dq, tt=tt: e.copy(
                    out=ffob[:, tt * 2048 + dq * 512: tt * 2048 + (dq + 1) * 512], in_=psS1[:, :]))
        ff_f = pool.alloc(128, D, F32)
        x2t4 = pool.alloc(128, D, F32)
        outt = pool.alloc(128, D, F32)
        scr4 = pool.alloc(128, D, F32)
        st4 = pool.alloc(128, 8, F32)
        for tt in range(8):
            O("scalar", lambda e, tt=tt: e.copy(out=ff_f, in_=ffob[:, tt * 2048:(tt + 1) * 2048]))
            O("sync", lambda e, tt=tt: e.dma_start(out=x2t4, in_=x2_hbm[tt * 128:(tt + 1) * 128, :]), "dma")
            ln_tile(ff_f, None, scr4, st4, out_f32=outt, res=x2t4)
            O("sync", lambda e, tt=tt: e.dma_start(out=out_sh[tt * 128:(tt + 1) * 128, :], in_=outt), "dma")

        ch.emit(block, drain=ch.all_tokens())

    return nc, dbg_aps


def make_inputs(core, inp):
    c = core
    qc = slice(256 * c, 256 * (c + 1))
    kc = slice(2048 + 256 * c, 2048 + 256 * (c + 1))
    vc = slice(4096 + 512 * c, 4096 + 512 * (c + 1))
    wqkv = np.asarray(inp["L0_attn_w_qkv"], dtype=np.float32)
    conv = np.asarray(inp["L0_attn_conv_w"], dtype=np.float32)
    wq = np.concatenate([wqkv[:, qc], wqkv[:, kc], wqkv[:, vc]], axis=1)
    cw = np.concatenate([conv[:, qc], conv[:, kc], conv[:, vc]], axis=1)
    x = np.asarray(inp["L0_attn_x"], dtype=np.float32).reshape(BT, D)
    return {
        "x_sh": np.ascontiguousarray(x[c * TSH:(c + 1) * TSH]),
        "wq_sl": np.ascontiguousarray(wq),
        "wz_sl": np.ascontiguousarray(np.asarray(inp["L0_attn_w_z"], dtype=np.float32)[:, 512 * c:512 * (c + 1)]),
        "wba_sl": np.ascontiguousarray(np.concatenate(
            [np.asarray(inp["L0_attn_w_b"], dtype=np.float32)[:, 4 * c:4 * (c + 1)],
             np.asarray(inp["L0_attn_w_a"], dtype=np.float32)[:, 4 * c:4 * (c + 1)]], axis=1)),
        "cwT_sl": np.ascontiguousarray(cw.T),
        "dta8": np.ascontiguousarray(np.concatenate(
            [np.zeros(4, np.float32), np.asarray(inp["L0_attn_dt_bias"], dtype=np.float32)[4 * c:4 * (c + 1)]])[:, None]),
        "alog8": np.ascontiguousarray(np.concatenate(
            [np.zeros(4, np.float32), np.asarray(inp["L0_attn_A_log"], dtype=np.float32)[4 * c:4 * (c + 1)]])[:, None]),
        "normw": np.ascontiguousarray(np.asarray(inp["L0_attn_norm_w"], dtype=np.float32)[:, None]),
        "wo_sl": np.ascontiguousarray(np.asarray(inp["L0_attn_w_out"], dtype=np.float32)[512 * c:512 * (c + 1), :]),
        "w1_sl": np.ascontiguousarray(np.asarray(inp["L0_ffn_W1"], dtype=np.float32)[1024 * c:1024 * (c + 1), :]),
        "w3_sl": np.ascontiguousarray(np.asarray(inp["L0_ffn_W3"], dtype=np.float32)[1024 * c:1024 * (c + 1), :]),
        "w2_sl": np.ascontiguousarray(np.asarray(inp["L0_ffn_W2"], dtype=np.float32)[:, 1024 * c:1024 * (c + 1)]),
    }


def kernel(**inp):
    nc, _ = build(dbg=0)
    in_maps = [make_inputs(c, inp) for c in range(8)]
    res = run_bass_kernel_spmd(nc, in_maps, core_ids=list(range(8)))
    out = np.concatenate([res.results[c]["out_sh"] for c in range(8)], axis=0)
    return out.reshape(B, T, D).astype(np.float32)


# version 05
# speedup vs baseline: 1.3779x; 1.3700x over previous
import numpy as np
import ml_dtypes
import concourse.bass as bass
import concourse.mybir as mybir
from concourse.bass_utils import run_bass_kernel_spmd

F32 = mybir.dt.float32
BF16 = mybir.dt.bfloat16
AF = mybir.ActivationFunctionType
ALU = mybir.AluOpType
AX = mybir.AxisListType

B, T, D = 2, 4096, 2048
BT = B * T
CCH = 128
NCH = T // CCH
TSH = BT // 8
EPS = 1e-6
LNEPS = 1e-5
NEGINF = -30000.0
BF = ml_dtypes.bfloat16
LN128 = float(np.log(128.0))


def _hadamard(n):
    h = np.array([[1.0]])
    while h.shape[0] < n:
        h = np.block([[h, h], [h, -h]])
    return h


ENGS = ("sync", "scalar", "vector", "tensor", "gpsimd")


class Sched:

    NSLOT = 8

    def __init__(self, nc):
        self.nc = nc
        self.ops = {e: [] for e in ENGS}
        self.cnt = {e: 0 for e in ENGS}
        self.csem = {e: nc.alloc_semaphore(name=f"cs_{e}") for e in ENGS}
        self.waited = {e: {} for e in ENGS}
        self.dsem = {}
        self.dcnt = {}
        self.drr = {}
        for q in ("sync", "scalar", "vector", "gpsimd"):
            self.drr[q] = 0
            for i in range(self.NSLOT):
                self.dsem[(q, i)] = nc.alloc_semaphore(name=f"d_{q}{i}")
                self.dcnt[(q, i)] = 0
        self.n_cc = 0
        self.serial_last = ()
        self._grp = None

    SKIP_SAME = False
    PRUNE = True

    def _waits(self, eng, deps, nosame=False):
        need = {}
        own = self.csem[eng]
        for t in deps:
            if t is None:
                continue
            sem, val = t
            if (nosame or self.SKIP_SAME) and sem is own:
                continue
            k = sem.num
            if k not in need or need[k][1] < val:
                need[k] = (sem, val)
        out = []
        w = self.waited[eng]
        for k, (sem, val) in need.items():
            if self.PRUNE and w.get(k, -1) >= val:
                continue
            w[k] = val
            out.append((sem, val))
        return out

    def op(self, eng, fn, deps=(), nosame=False):
        waits = self._waits(eng, deps, nosame)
        self.cnt[eng] += 1
        tok = (self.csem[eng], self.cnt[eng])
        self.ops[eng].append((waits, fn, tok[0], 1))
        return tok

    def dma(self, q, fn, deps=(), nosame=False):
        slot = self.drr[q] % self.NSLOT
        self.drr[q] += 1
        sem = self.dsem[(q, slot)]
        prev = self.dcnt[(q, slot)]
        deps = list(deps)
        if prev:
            deps.append((sem, prev))
        waits = self._waits(q, deps, nosame)
        self.dcnt[(q, slot)] = prev + 16
        self.ops[q].append((waits, fn, sem, 16))
        return (sem, prev + 16)

    def cc(self, fn, deps=()):
        sem = self.nc.alloc_semaphore(name=f"ccs{self.n_cc}")
        self.n_cc += 1
        waits = self._waits("gpsimd", deps)
        self.ops["gpsimd"].append((waits, fn, sem, 1))
        return (sem, 1)

    def C(self, eng, fn, kind="comp"):
        deps = self.serial_last
        if kind == "dma":
            tok = self.dma(eng, fn, deps)
        elif kind == "cc":
            tok = self.cc(fn, deps)
        else:
            tok = self.op(eng, fn, deps)
        if self._grp is None:
            self.serial_last = (tok,)
        else:
            self._grp.append(tok)
        return tok

    def group(self):
        sched = self

        class G:
            def __enter__(self):
                sched._grp = []

            def __exit__(self, *a):
                if sched._grp:
                    sched.serial_last = tuple(sched._grp)
                sched._grp = None
                return False

        return G()

    def all_tokens(self):
        toks = [(self.csem[e], self.cnt[e]) for e in ENGS if self.cnt[e] > 0]
        for key, c in self.dcnt.items():
            if c > 0:
                toks.append((self.dsem[key], c))
        return tuple(toks)

    def emit(self, block, drain=()):
        fin = self._waits("sync", drain)
        for e in ENGS:
            ops = self.ops[e]
            tail = fin if e == "sync" else []

            def body(eng, ops=ops, tail=tail):
                for waits, fn, sem, inc in ops:
                    for s, v in waits:
                        eng.wait_ge(s, v)
                    inst = fn(eng)
                    if inst is not None and inc:
                        inst.then_inc(sem, inc)
                for s, v in tail:
                    eng.wait_ge(s, v)

            getattr(block, e)(body)


class Pool:

    def __init__(self, nc, name, nbytes):
        self.t = nc.alloc_sbuf_tensor(name, [128, nbytes], mybir.dt.uint8)
        self.off = 0
        self.peak = 0
        self.nbytes = nbytes

    def alloc(self, p, n, dtype):
        esz = mybir.dt.size(dtype)
        start = (self.off + 31) // 32 * 32
        self.off = start + n * esz
        self.peak = max(self.peak, self.off)
        assert self.off <= self.nbytes, f"pool overflow {self.off}"
        return self.t[0:p, start:start + n * esz].bitcast(dtype)

    def mark(self):
        return self.off

    def reset(self, m):
        self.off = m


def build(dbg=0):
    nc = bass.Bass("TRN2", target_bir_lowering=False)
    RG = [list(range(8))]

    x_sh = nc.dram_tensor("x_sh", [TSH, D], F32, kind="ExternalInput").ap()
    wq_sl = nc.dram_tensor("wq_sl", [D, 1024], F32, kind="ExternalInput").ap()
    wz_sl = nc.dram_tensor("wz_sl", [D, 512], F32, kind="ExternalInput").ap()
    wba_sl = nc.dram_tensor("wba_sl", [D, 8], F32, kind="ExternalInput").ap()
    cwT_sl = nc.dram_tensor("cwT_sl", [1024, 4], F32, kind="ExternalInput").ap()
    dta8 = nc.dram_tensor("dta8", [8, 1], F32, kind="ExternalInput").ap()
    alog8 = nc.dram_tensor("alog8", [8, 1], F32, kind="ExternalInput").ap()
    normw = nc.dram_tensor("normw", [128, 1], F32, kind="ExternalInput").ap()
    wo_sl = nc.dram_tensor("wo_sl", [512, D], F32, kind="ExternalInput").ap()
    w1_sl = nc.dram_tensor("w1_sl", [1024, D], F32, kind="ExternalInput").ap()
    w3_sl = nc.dram_tensor("w3_sl", [1024, D], F32, kind="ExternalInput").ap()
    w2_sl = nc.dram_tensor("w2_sl", [D, 1024], F32, kind="ExternalInput").ap()
    out_sh = nc.dram_tensor("out_sh", [TSH, D], F32, kind="ExternalOutput").ap()

    dram = nc.dram_tensor
    xln_sl = dram("xln_sl", [TSH, D], BF16).ap()
    ag_xln = dram("ag_xln", [BT, D], BF16).ap()
    w1b_sl = dram("w1b_sl", [1024, D], BF16).ap()
    w3b_sl = dram("w3b_sl", [1024, D], BF16).ap()
    w2b_sl = dram("w2b_sl", [D, 1024], BF16).ap()
    w2t_sl = dram("w2t_sl", [1024, D], BF16).ap()
    wob_sl = dram("wob_sl", [512, D], BF16).ap()
    ag_w1 = dram("ag_w1", [8192, D], BF16).ap()
    ag_w3 = dram("ag_w3", [8192, D], BF16).ap()
    ag_w2t = dram("ag_w2t", [8192, D], BF16).ap()

    qb_hbm = dram("qb_hbm", [256, BT], BF16).ap()
    kb_hbm = dram("kb_hbm", [256, BT], BF16).ap()
    vb_hbm = dram("vb_hbm", [512, BT], BF16).ap()
    zb_hbm = dram("zb_hbm", [512, BT], BF16).ap()
    lng_hbm = dram("lng_hbm", [8, BT], F32).ap()
    lr_hbm = dram("lr_hbm", [4, BT], F32).ap()
    glay = dram("glay", [256, 128], F32).ap()
    blay = dram("blay", [256, 128], F32).ap()
    lrklay = dram("lrklay", [256, 128], F32).ap()
    lrqlay = dram("lrqlay", [256, 128], F32).ap()
    evec_hbm = dram("evec_hbm", [3, 256, CCH], F32).ap()
    scl_hbm = dram("scl_hbm", [4, 256, CCH], BF16).ap()
    gcx_hbm = dram("gcx_hbm", [256, 1], F32).ap()
    gated_tm = dram("gated_tm", [BT, 512], BF16).ap()
    part_hbm = dram("part_hbm", [BT, D], BF16).ap()
    rs_out = dram("rs_out", [TSH, D], BF16).ap()
    x2_hbm = dram("x2_hbm", [TSH, D], F32).ap()
    x2ln_hbm = dram("x2ln_hbm", [TSH, D], BF16).ap()
    w1t_hbm = dram("w1t_hbm", [D, 8192], BF16).ap()
    w3t_hbm = dram("w3t_hbm", [D, 8192], BF16).ap()
    ybuf_hbm = dram("ybuf_hbm", [64, 128, TSH], BF16).ap()
    h2_hbm = dram("h2_hbm", [8, 2, 64, 64, 128], BF16).ap()

    dbg_aps = {}
    if dbg >= 1:
        dbg_aps["d_mix"] = dram("d_mix", [1024, BT], BF16, kind="ExternalOutput").ap()
        dbg_aps["d_lng"] = dram("d_lng", [8, BT], F32, kind="ExternalOutput").ap()
        dbg_aps["d_lr"] = dram("d_lr", [4, BT], F32, kind="ExternalOutput").ap()
        dbg_aps["d_xln"] = dram("d_xln", [BT, D], BF16, kind="ExternalOutput").ap()
        dbg_aps["d_z"] = dram("d_z", [512, BT], BF16, kind="ExternalOutput").ap()
    if dbg >= 2:
        dbg_aps["d_gat"] = dram("d_gat", [BT, 512], BF16, kind="ExternalOutput").ap()
    if dbg >= 3:
        dbg_aps["d_x2"] = dram("d_x2", [TSH, D], F32, kind="ExternalOutput").ap()

    ti = np.arange(CCH)
    maskL = np.where(ti[:, None] > ti[None, :], 0.0, NEGINF).astype(np.float32)
    maskU = maskL.T.copy()
    maskQ = np.where(ti[:, None] <= ti[None, :], -0.5 * LN128, NEGINF).astype(np.float32)
    c_maskL = nc.inline_tensor(maskL, name="c_maskL").ap()
    c_maskU = nc.inline_tensor(maskU, name="c_maskU").ap()
    c_maskQ = nc.inline_tensor(maskQ, name="c_maskQ").ap()
    c_idf = nc.inline_tensor(np.eye(128, dtype=np.float32), name="c_idf").ap()
    c_idb = nc.inline_tensor(np.eye(128, dtype=BF), name="c_idb").ap()
    c_onesb = nc.inline_tensor(np.ones((128, 1), dtype=BF), name="c_onesb").ap()
    c_onesf = nc.inline_tensor(np.ones((1, 128), dtype=np.float32), name="c_onesf").ap()
    c_sign8 = nc.inline_tensor(
        np.array([-1, -1, -1, -1, 1, 1, 1, 1], dtype=np.float32)[:, None], name="c_sign8").ap()
    c_h128 = nc.inline_tensor(_hadamard(128).astype(BF), name="c_h128").ap()
    c_h64 = nc.inline_tensor(_hadamard(64).astype(BF), name="c_h64").ap()
    mL4 = np.tile(np.where(ti[:, None] > ti[None, :], 0.0, NEGINF), (1, 4)).astype(BF)
    mQ4 = np.tile(np.where(ti[:, None] <= ti[None, :], 0.0, NEGINF), (1, 4)).astype(BF)
    c_mskL4 = nc.inline_tensor(mL4, name="c_mskL4").ap()
    c_mskQ4 = nc.inline_tensor(mQ4, name="c_mskQ4").ap()
    c_bias = nc.inline_tensor(np.tile(np.array(
        [LNEPS, EPS, 1.0, 128.0 * EPS], dtype=np.float32), (128, 1)), name="c_bias").ap()

    sb = nc.alloc_sbuf_tensor
    cst = sb("cst", [128, 4 * 128 + 16 + 128 + 4], F32)
    idbt = sb("idbt", [128, 129], BF16)
    h128t = sb("h128t", [128, 128], BF16)
    h64t = sb("h64t", [64, 64], BF16)
    St = [sb(f"St{p}", [128, 128], F32) for p in range(8)]
    St2 = [sb(f"St2_{p}", [128, 128], F32) for p in range(8)]
    Sbt = [sb(f"Sbt{p}", [128, 128], BF16) for p in range(8)]
    sclT = sb("sclT", [128, 4 * 256], BF16)
    sclF = sb("sclF", [128, 4 * 256], F32)
    gcb = sb("gcb", [128, 256], F32)
    mskL4 = sb("mskL4", [128, 512], BF16)
    mskQ4 = sb("mskQ4", [128, 512], BF16)

    MKL, MKU, MKQ = cst[:, 0:128], cst[:, 128:256], cst[:, 256:384]
    IDF = cst[:, 384:512]
    SGN8, DTA8, ALOG8 = cst[0:8, 512:513], cst[0:8, 513:514], cst[0:8, 514:515]
    NRMW = cst[0:128, 515:516]
    ONESF = cst[0:1, 528:656]
    BLN = cst[:, 656:657]
    BEP = cst[:, 657:658]
    BONE = cst[:, 658:659]
    BQ128 = cst[:, 659:660]
    IDB, ONEB = idbt[:, 0:128], idbt[:, 128:129]
    nrscT = sclF[:, 0:256]
    oscT = sclF[:, 256:512]
    dscT = sclF[:, 512:768]
    bexT = sclF[:, 768:1024]

    pool = Pool(nc, "pool", 167 * 1024)

    ps = nc.alloc_psum_tensor
    psU = ps("psU", [128, 2048], F32)
    psV = ps("psV", [128, 2048], F32)
    psBig = psU[:, 0:1024]
    psBig2 = psU[:, 1024:2048]
    psS1 = psV[:, 0:512]
    psA = psV[:, 512:1024]
    psB = psV[:, 1024:1536]

    with nc.Block() as block:
        ch = Sched(nc)
        O = ch.C

        def ln_tile(xt, outb, scr, st, eps=None, out_f32=None, res=None):
            O("vector", lambda e: e.reduce_sum(out=st[:, 0:1], in_=xt, axis=AX.X))
            O("vector", lambda e: e.scalar_tensor_tensor(
                out=scr, in0=xt, scalar=1.0, in1=xt, op0=ALU.mult, op1=ALU.mult,
                accum_out=st[:, 1:2]))
            O("vector", lambda e: e.tensor_scalar(
                out=st[:, 2:3], in0=st[:, 0:1], scalar1=-1.0 / D, scalar2=None, op0=ALU.mult))
            O("vector", lambda e: e.tensor_tensor(
                out=st[:, 3:4], in0=st[:, 2:3], in1=st[:, 2:3], op=ALU.mult))
            O("vector", lambda e: e.scalar_tensor_tensor(
                out=st[:, 4:5], in0=st[:, 1:2], scalar=1.0 / D, in1=st[:, 3:4],
                op0=ALU.mult, op1=ALU.subtract))
            ba = eps if eps is not None else BLN
            O("scalar", lambda e, ba=ba: e.activation(out=st[:, 5:6], in_=st[:, 4:5], func=AF.Ln, bias=ba))
            O("scalar", lambda e: e.activation(out=st[:, 6:7], in_=st[:, 5:6], func=AF.Exp, scale=-0.5))
            if outb is not None:
                O("vector", lambda e: e.tensor_scalar(
                    out=outb, in0=xt, scalar1=st[:, 2:3], scalar2=st[:, 6:7],
                    op0=ALU.add, op1=ALU.mult))
            if res is not None:
                O("vector", lambda e: e.tensor_scalar(
                    out=scr, in0=xt, scalar1=st[:, 2:3], scalar2=st[:, 6:7],
                    op0=ALU.add, op1=ALU.mult))
                O("vector", lambda e: e.tensor_tensor(out=out_f32, in0=scr, in1=res, op=ALU.add))

        with ch.group():
            O("gpsimd", lambda e: e.dma_start(out=w1b_sl[:], in_=w1_sl[:]), "dma")
            O("gpsimd", lambda e: e.dma_start(out=w3b_sl[:], in_=w3_sl[:]), "dma")
            O("gpsimd", lambda e: e.dma_start(out=w2b_sl[:], in_=w2_sl[:]), "dma")
            O("sync", lambda e: e.dma_start(out=MKL, in_=c_maskL[:]), "dma")
            O("sync", lambda e: e.dma_start(out=MKU, in_=c_maskU[:]), "dma")
            O("sync", lambda e: e.dma_start(out=MKQ, in_=c_maskQ[:]), "dma")
            O("sync", lambda e: e.dma_start(out=IDF, in_=c_idf[:]), "dma")
            O("sync", lambda e: e.dma_start(out=SGN8, in_=c_sign8[:]), "dma")
            O("sync", lambda e: e.dma_start(out=DTA8, in_=dta8[:]), "dma")
            O("sync", lambda e: e.dma_start(out=ALOG8, in_=alog8[:]), "dma")
            O("sync", lambda e: e.dma_start(out=NRMW, in_=normw[:]), "dma")
            O("sync", lambda e: e.dma_start(out=ONESF, in_=c_onesf[:]), "dma")
            O("sync", lambda e: e.dma_start(out=idbt[:, 0:128], in_=c_idb[:]), "dma")
            O("sync", lambda e: e.dma_start(out=idbt[:, 128:129], in_=c_onesb[:]), "dma")
            O("sync", lambda e: e.dma_start(out=h128t[:], in_=c_h128[:]), "dma")
            O("sync", lambda e: e.dma_start(out=h64t[:], in_=c_h64[:]), "dma")
            O("sync", lambda e: e.dma_start(out=cst[:, 656:660], in_=c_bias[:]), "dma")
            O("sync", lambda e: e.dma_start(out=mskL4[:], in_=c_mskL4[:]), "dma")
            O("sync", lambda e: e.dma_start(out=mskQ4[:], in_=c_mskQ4[:]), "dma")

        m0 = pool.mark()
        coef8 = cst[0:8, 516:517]
        ncoef8 = cst[0:8, 517:518]
        O("scalar", lambda e: e.activation(out=coef8, in_=ALOG8, func=AF.Exp))
        O("vector", lambda e: e.tensor_scalar(
            out=ncoef8, in0=coef8, scalar1=-1.0, scalar2=None, op0=ALU.mult))

        w2tt = pool.alloc(128, D, BF16)
        for i in range(8):
            O("sync", lambda e, i=i: e.dma_start(
                out=w2tt, in_=w2b_sl[:, i * 128:(i + 1) * 128], transpose=True), "dma")
            O("sync", lambda e, i=i: e.dma_start(
                out=w2t_sl[i * 128:(i + 1) * 128, :], in_=w2tt), "dma")

        wot_f = pool.alloc(128, D, F32)
        wot_b = pool.alloc(128, D, BF16)
        for i in range(4):
            O("sync", lambda e, i=i: e.dma_start(out=wot_f, in_=wo_sl[i * 128:(i + 1) * 128, :]), "dma")
            O("vector", lambda e: e.tensor_scalar(
                out=wot_b, in0=wot_f, scalar1=NRMW, scalar2=None, op0=ALU.mult))
            O("sync", lambda e, i=i: e.dma_start(out=wob_sl[i * 128:(i + 1) * 128, :], in_=wot_b), "dma")

        xt0 = pool.alloc(128, D, F32)
        scr0 = pool.alloc(128, D, F32)
        st0 = pool.alloc(128, 8, F32)
        xlnb0 = pool.alloc(128, D, BF16)
        for i in range(8):
            O("sync", lambda e, i=i: e.dma_start(out=xt0, in_=x_sh[i * 128:(i + 1) * 128, :]), "dma")
            ln_tile(xt0, xlnb0, scr0, st0)
            O("sync", lambda e, i=i: e.dma_start(out=xln_sl[i * 128:(i + 1) * 128, :], in_=xlnb0), "dma")

        O("gpsimd", lambda e: e.collective_compute(
            "AllGather", ALU.bypass, replica_groups=RG, ins=[xln_sl[:]], outs=[ag_xln[:]]), "cc")
        O("gpsimd", lambda e: e.collective_compute(
            "AllGather", ALU.bypass, replica_groups=RG, ins=[w1b_sl[:]], outs=[ag_w1[:]]), "cc")
        O("gpsimd", lambda e: e.collective_compute(
            "AllGather", ALU.bypass, replica_groups=RG, ins=[w3b_sl[:]], outs=[ag_w3[:]]), "cc")
        O("gpsimd", lambda e: e.collective_compute(
            "AllGather", ALU.bypass, replica_groups=RG, ins=[w2t_sl[:]], outs=[ag_w2t[:]]), "cc")
        if dbg >= 1:
            O("gpsimd", lambda e: e.dma_start(out=dbg_aps["d_xln"][:], in_=ag_xln[:]), "dma")
        pool.reset(m0)

        batile = pool.alloc(8, BT, F32)
        m1b = pool.mark()
        wqb = pool.alloc(128, 16 * 1024, BF16)
        wzb = pool.alloc(128, 16 * 512, BF16)
        wbab = pool.alloc(128, 16 * 8, BF16)
        wst = pool.alloc(128, 1024, F32)
        cwt = pool.alloc(128, 32, F32)
        for k in range(16):
            O("sync", lambda e, k=k: e.dma_start(out=wst, in_=wq_sl[k * 128:(k + 1) * 128, :]), "dma")
            O("scalar", lambda e, k=k: e.copy(out=wqb[:, k * 1024:(k + 1) * 1024], in_=wst))
            O("sync", lambda e, k=k: e.dma_start(out=wst[:, 0:512], in_=wz_sl[k * 128:(k + 1) * 128, :]), "dma")
            O("scalar", lambda e, k=k: e.copy(out=wzb[:, k * 512:(k + 1) * 512], in_=wst[:, 0:512]))
            O("sync", lambda e, k=k: e.dma_start(out=wst[:, 0:8], in_=wba_sl[k * 128:(k + 1) * 128, :]), "dma")
            O("scalar", lambda e, k=k: e.copy(out=wbab[:, k * 8:(k + 1) * 8], in_=wst[:, 0:8]))
        for m in range(8):
            O("sync", lambda e, m=m: e.dma_start(
                out=cwt[:, m * 4:(m + 1) * 4], in_=cwT_sl[m * 128:(m + 1) * 128, :]), "dma")

        xlt = pool.alloc(128, 16 * 1024, BF16)
        mc = pool.alloc(128, 1027, F32)
        acc0 = pool.alloc(128, 1024, F32)
        acc1 = pool.alloc(128, 1024, F32)
        tails = pool.alloc(128, 24, F32)
        sqt = pool.alloc(128, 1024, BF16)
        mtmp = pool.alloc(128, 1024, BF16)
        lrrow = pool.alloc(1, 1024, F32)

        def lhsT_ph1(m, k):
            if m < 8:
                return wqb[:, k * 1024 + m * 128: k * 1024 + (m + 1) * 128]
            if m < 12:
                return wzb[:, k * 512 + (m - 8) * 128: k * 512 + (m - 7) * 128]
            return wbab[:, k * 8:(k + 1) * 8]

        for tc in range(8):
            tsl = slice(tc * 1024, (tc + 1) * 1024)
            with ch.group():
                for k in range(16):
                    O("sync", lambda e, k=k, tc=tc: e.dma_start(
                        out=xlt[:, k * 1024:(k + 1) * 1024],
                        in_=ag_xln[tc * 1024:(tc + 1) * 1024, k * 128:(k + 1) * 128],
                        transpose=True), "dma")
            for m in range(13):
                pst = psBig if (m % 2 == 0) else psBig2
                pm = pst[0:128, :] if m < 12 else pst[0:8, :]
                for k in range(16):
                    for h in range(2):
                        O("tensor", lambda e, m=m, k=k, h=h, pm=pm: e.matmul(
                            pm[:, h * 512:(h + 1) * 512], lhsT=lhsT_ph1(m, k),
                            rhs=xlt[:, k * 1024 + h * 512: k * 1024 + (h + 1) * 512],
                            start=(k == 0), stop=(k == 15)))
                if m < 8:
                    O("scalar", lambda e, pm=pm: e.copy(out=mc[:, 3:1027], in_=pm))
                    if tc in (0, 4):
                        O("vector", lambda e: e.memset(mc[:, 0:3], 0.0))
                    else:
                        O("scalar", lambda e, m=m: e.copy(out=mc[:, 0:3], in_=tails[:, m * 3:(m + 1) * 3]))
                    O("scalar", lambda e, m=m: e.copy(out=tails[:, m * 3:(m + 1) * 3], in_=mc[:, 1024:1027]))
                    O("vector", lambda e, m=m: e.tensor_scalar(
                        out=acc0, in0=mc[:, 3:1027], scalar1=cwt[:, m * 4 + 3:m * 4 + 4],
                        scalar2=None, op0=ALU.mult))
                    O("vector", lambda e, m=m: e.scalar_tensor_tensor(
                        out=acc1, in0=mc[:, 2:1026], scalar=cwt[:, m * 4 + 2:m * 4 + 3],
                        in1=acc0, op0=ALU.mult, op1=ALU.add))
                    O("vector", lambda e, m=m: e.scalar_tensor_tensor(
                        out=acc0, in0=mc[:, 1:1025], scalar=cwt[:, m * 4 + 1:m * 4 + 2],
                        in1=acc1, op0=ALU.mult, op1=ALU.add))
                    O("vector", lambda e, m=m: e.scalar_tensor_tensor(
                        out=acc1, in0=mc[:, 0:1024], scalar=cwt[:, m * 4 + 0:m * 4 + 1],
                        in1=acc0, op0=ALU.mult, op1=ALU.add))
                    O("scalar", lambda e: e.activation(out=mtmp, in_=acc1, func=AF.Silu))
                    if m < 2:
                        O("sync", lambda e, m=m, tsl=tsl: e.dma_start(
                            out=qb_hbm[m * 128:(m + 1) * 128, tsl], in_=mtmp), "dma")
                    elif m < 4:
                        O("sync", lambda e, m=m, tsl=tsl: e.dma_start(
                            out=kb_hbm[(m - 2) * 128:(m - 1) * 128, tsl], in_=mtmp), "dma")
                    else:
                        O("sync", lambda e, m=m, tsl=tsl: e.dma_start(
                            out=vb_hbm[(m - 4) * 128:(m - 3) * 128, tsl], in_=mtmp), "dma")
                    if m < 4:
                        O("scalar", lambda e: e.activation(out=sqt, in_=mtmp, func=AF.Square))
                        O("tensor", lambda e: e.matmul(
                            psS1[0:1, 0:512], lhsT=ONEB, rhs=sqt[:, 0:512], start=True, stop=True))
                        O("tensor", lambda e: e.matmul(
                            psA[0:1, 0:512], lhsT=ONEB, rhs=sqt[:, 512:1024], start=True, stop=True))
                        O("scalar", lambda e: e.copy(
                            out=lrrow[:, 0:512], in_=psS1[0:1, 0:512]))
                        O("scalar", lambda e: e.copy(
                            out=lrrow[:, 512:1024], in_=psA[0:1, 0:512]))
                        O("sync", lambda e, m=m, tsl=tsl: e.dma_start(
                            out=lr_hbm[m:m + 1, tsl], in_=lrrow), "dma")
                        lay = lrqlay if m < 2 else lrklay
                        hh = m % 2
                        bb, c0 = tc // 4, (tc % 4) * 8
                        for r in range(2):
                            base_ch = c0 * 8 + (2 * hh + r) * 2 + bb
                            O("sync", lambda e, lay=lay, base_ch=base_ch: e.dma_start(
                                out=lay[base_ch:base_ch + 57:8, :],
                                in_=lrrow.rearrange("o (c p) -> o c p", p=128)), "dma")
                    if dbg >= 1:
                        O("sync", lambda e, m=m, tsl=tsl: e.dma_start(
                            out=dbg_aps["d_mix"][m * 128:(m + 1) * 128, tsl], in_=mtmp), "dma")
                elif m < 12:
                    O("scalar", lambda e, pm=pm: e.activation(out=mtmp, in_=pm, func=AF.Silu))
                    O("sync", lambda e, m=m, tsl=tsl: e.dma_start(
                        out=zb_hbm[(m - 8) * 128:(m - 7) * 128, tsl], in_=mtmp), "dma")
                    if dbg >= 1:
                        O("sync", lambda e, m=m, tsl=tsl: e.dma_start(
                            out=dbg_aps["d_z"][(m - 8) * 128:(m - 7) * 128, tsl], in_=mtmp), "dma")
                else:
                    O("scalar", lambda e, pm=pm, tc=tc: e.copy(
                        out=batile[:, tc * 1024:(tc + 1) * 1024], in_=pm))

        pool.reset(m1b)
        spA = pool.alloc(8, BT, F32)
        spB = pool.alloc(8, BT, F32)
        lngt = pool.alloc(8, BT, F32)
        O("vector", lambda e: e.tensor_scalar(
            out=spA, in0=batile, scalar1=SGN8, scalar2=DTA8, op0=ALU.mult, op1=ALU.add))
        O("scalar", lambda e: e.activation(out=spB, in_=spA, func=AF.Abs))
        O("scalar", lambda e: e.activation(out=spB, in_=spB, func=AF.Exp, scale=-1.0))
        O("scalar", lambda e: e.activation(out=spB, in_=spB, func=AF.Ln, bias=BONE[0:8, :]))
        O("vector", lambda e: e.tensor_scalar(
            out=spA, in0=spA, scalar1=0.0, scalar2=None, op0=ALU.max))
        O("vector", lambda e: e.tensor_tensor(out=spA, in0=spA, in1=spB, op=ALU.add))
        O("vector", lambda e: e.tensor_scalar(
            out=lngt, in0=spA, scalar1=ncoef8, scalar2=None, op0=ALU.mult))
        O("sync", lambda e: e.dma_start(out=lng_hbm[:], in_=lngt), "dma")
        O("sync", lambda e: e.dma_start(
            out=glay.rearrange("(c h b) p -> h b c p", h=4, b=2),
            in_=lngt[4:8, :].rearrange("h (b c p) -> h b c p", b=2, c=32)), "dma")
        O("sync", lambda e: e.dma_start(
            out=blay.rearrange("(c h b) p -> h b c p", h=4, b=2),
            in_=lngt[0:4, :].rearrange("h (b c p) -> h b c p", b=2, c=32)), "dma")
        if dbg >= 1:
            O("sync", lambda e: e.dma_start(out=dbg_aps["d_lng"][:], in_=lngt), "dma")
            O("sync", lambda e: e.dma_start(out=dbg_aps["d_lr"][:], in_=lr_hbm[:]), "dma")

        cgt = [pool.alloc(128, 128, F32) for _ in range(2)]
        cgt2 = [pool.alloc(128, 128, F32) for _ in range(2)]
        lnbt = [pool.alloc(128, 128, F32) for _ in range(2)]
        lrkt = [pool.alloc(128, 128, F32) for _ in range(2)]
        lrqt = [pool.alloc(128, 128, F32) for _ in range(2)]
        for t2 in range(2):
            cs = slice(t2 * 128, (t2 + 1) * 128)
            with ch.group():
                O("sync", lambda e, t2=t2, cs=cs: e.dma_start(out=cgt[t2][:, :], in_=glay[cs]), "dma")
                O("sync", lambda e, t2=t2, cs=cs: e.dma_start(out=lnbt[t2][:, :], in_=blay[cs]), "dma")
                O("sync", lambda e, t2=t2, cs=cs: e.dma_start(out=lrkt[t2][:, :], in_=lrklay[cs]), "dma")
                O("sync", lambda e, t2=t2, cs=cs: e.dma_start(out=lrqt[t2][:, :], in_=lrqlay[cs]), "dma")
        for t2 in range(2):
            O("scalar", lambda e, t2=t2: e.activation(
                out=lrkt[t2], in_=lrkt[t2], func=AF.Ln, bias=BEP))
            O("scalar", lambda e, t2=t2: e.activation(
                out=lrqt[t2], in_=lrqt[t2], func=AF.Ln, scale=128.0, bias=BQ128))
        for t2 in range(2):
            src, dst = cgt[t2], cgt2[t2]
            s = 1
            while s < 128:
                O("vector", lambda e, src=src, dst=dst, s=s: e.tensor_tensor(
                    out=dst[:, s:128], in0=src[:, s:128], in1=src[:, 0:128 - s], op=ALU.add))
                O("vector", lambda e, src=src, dst=dst, s=s: e.tensor_copy(
                    out=dst[:, 0:s], in_=src[:, 0:s]))
                src, dst = dst, src
                s *= 2
            if src is not cgt[t2]:
                O("vector", lambda e, src=src, t2=t2: e.tensor_copy(out=cgt[t2], in_=src))

        av = [pool.alloc(128, 128, F32) for _ in range(2)]
        bv = [pool.alloc(128, 128, F32) for _ in range(2)]
        qv = [pool.alloc(128, 128, F32) for _ in range(2)]
        ncg = pool.alloc(128, 128, F32)
        sct = pool.alloc(128, 128, F32)
        scb = pool.alloc(128, 128, BF16)
        for t2 in range(2):
            O("vector", lambda e, t2=t2: e.scalar_tensor_tensor(
                out=av[t2], in0=lrkt[t2], scalar=-0.5, in1=cgt[t2], op0=ALU.mult, op1=ALU.add))
            O("vector", lambda e, t2=t2: e.tensor_tensor(
                out=av[t2], in0=av[t2], in1=lnbt[t2], op=ALU.add))
            O("vector", lambda e, t2=t2: e.tensor_scalar(
                out=ncg, in0=cgt[t2], scalar1=-1.0, scalar2=None, op0=ALU.mult))
            O("vector", lambda e, t2=t2: e.scalar_tensor_tensor(
                out=bv[t2], in0=lrkt[t2], scalar=-0.5, in1=ncg, op0=ALU.mult, op1=ALU.add))
            O("vector", lambda e, t2=t2: e.scalar_tensor_tensor(
                out=qv[t2], in0=lrqt[t2], scalar=-0.5, in1=cgt[t2], op0=ALU.mult, op1=ALU.add))
            O("scalar", lambda e, t2=t2: e.activation(out=sct, in_=av[t2], func=AF.Exp))
            O("vector", lambda e: e.tensor_scalar(
                out=scb, in0=sct, scalar1=-1.0, scalar2=None, op0=ALU.mult))
            O("sync", lambda e, t2=t2: e.dma_start(
                out=scl_hbm[0, t2 * 128:(t2 + 1) * 128, :], in_=scb), "dma")
            O("scalar", lambda e, t2=t2: e.activation(
                out=scb, in_=qv[t2], func=AF.Exp))
            O("sync", lambda e, t2=t2: e.dma_start(
                out=scl_hbm[1, t2 * 128:(t2 + 1) * 128, :], in_=scb), "dma")
            O("vector", lambda e, t2=t2: e.tensor_scalar(
                out=sct, in0=bv[t2], scalar1=cgt[t2][:, 127:128], scalar2=None, op0=ALU.add))
            O("scalar", lambda e: e.activation(out=scb, in_=sct, func=AF.Exp))
            O("sync", lambda e, t2=t2: e.dma_start(
                out=scl_hbm[2, t2 * 128:(t2 + 1) * 128, :], in_=scb), "dma")
            O("scalar", lambda e, t2=t2: e.activation(out=scb, in_=lnbt[t2], func=AF.Exp))
            O("sync", lambda e, t2=t2: e.dma_start(
                out=scl_hbm[3, t2 * 128:(t2 + 1) * 128, :], in_=scb), "dma")
            O("sync", lambda e, t2=t2: e.dma_start(
                out=evec_hbm[0, t2 * 128:(t2 + 1) * 128, :], in_=av[t2]), "dma")
            O("sync", lambda e, t2=t2: e.dma_start(
                out=evec_hbm[1, t2 * 128:(t2 + 1) * 128, :], in_=bv[t2]), "dma")
            O("sync", lambda e, t2=t2: e.dma_start(
                out=evec_hbm[2, t2 * 128:(t2 + 1) * 128, :], in_=qv[t2]), "dma")
            O("sync", lambda e, t2=t2: e.dma_start(
                out=gcx_hbm[t2 * 128:(t2 + 1) * 128, :], in_=cgt[t2][:, 127:128]), "dma")
        for i in range(4):
            O("sync", lambda e, i=i: e.dma_start(
                out=sclT[:, i * 256:(i + 1) * 256], in_=scl_hbm[i], transpose=True), "dma")
            O("scalar", lambda e, i=i: e.copy(
                out=sclF[:, i * 256:(i + 1) * 256], in_=sclT[:, i * 256:(i + 1) * 256]))
        gcrow = pool.alloc(1, 256, F32)
        O("sync", lambda e: e.dma_start(
            out=gcrow, in_=gcx_hbm.rearrange("p o -> o p")), "dma")
        O("scalar", lambda e: e.activation(out=gcrow, in_=gcrow, func=AF.Exp))
        O("tensor", lambda e: e.matmul(psS1[:, 0:256], lhsT=ONESF, rhs=gcrow, start=True, stop=True))
        O("scalar", lambda e: e.copy(out=gcb[:], in_=psS1[:, 0:256]))
        pool.reset(0)

        for p in range(8):
            tk = O("vector", lambda e, p=p: e.memset(St[p][:], 0.0))
            tk2 = O("vector", lambda e, p=p: e.memset(Sbt[p][:], 0.0))

        m2 = pool.mark()
        rows = pool.alloc(2, 8 * 1024, F32)
        O("vector", lambda e: e.memset(rows[:], 1.0))
        KFM = [pool.alloc(128, 512, BF16) for _ in range(2)]
        QFM = [pool.alloc(128, 512, BF16) for _ in range(2)]
        KTM = [pool.alloc(128, 512, BF16) for _ in range(2)]
        VTM = [pool.alloc(128, 1024, BF16) for _ in range(2)]
        ZTM = [pool.alloc(128, 1024, BF16) for _ in range(2)]
        Zpg = [[pool.alloc(128, 1024, BF16) for _ in range(7)] for _ in range(2)]
        Ppg = [[pool.alloc(128, 1024, BF16) for _ in range(6)] for _ in range(2)]
        ELg = pool.alloc(128, 1024, BF16)
        EQg = pool.alloc(128, 1024, BF16)
        nKKg = pool.alloc(128, 1024, BF16)
        Ptlg = [pool.alloc(128, 1024, BF16) for _ in range(2)]
        Xg = [pool.alloc(128, 1024, BF16) for _ in range(2)]
        bVtT = pool.alloc(128, 1024, BF16)
        qsbg = pool.alloc(128, 1024, BF16)
        Dbarg = pool.alloc(128, 1024, BF16)
        ObufT = pool.alloc(128, 1024, BF16)
        ggtT = pool.alloc(128, 1024, BF16)
        sqgT = pool.alloc(128, 1024, BF16)
        gstT = pool.alloc(128, 24, F32)

        bA = [psU[:, 0:512], psU[:, 512:1024]]
        bEL = psU[:, 1024:1536]
        bEQ = psU[:, 1536:2048]
        bZ = psV[:, 0:512]
        bP = psV[:, 512:1024]
        bX = [psV[:, 1024:1536], psV[:, 1536:2048]]

        class Res:
            __slots__ = ("w", "readers")

            def __init__(s):
                s.w = []
                s.readers = []

            def wdeps(s):
                return list(s.readers) + list(s.w)

            def wrote(s, *toks):
                s.w = list(toks)
                s.readers = []

            def read(s, tok):
                s.readers.append(tok)

            def rdeps(s):
                return list(s.w)

        RS = {}

        def res(name):
            if name not in RS:
                RS[name] = Res()
            return RS[name]

        ph2_entry = ch.all_tokens()
        for p in range(8):
            res(f"Sbt{p}").wrote(*ph2_entry)
            res(f"St0_{p}").wrote(*ph2_entry)

        def xop(eng, fn, deps=()):
            return ch.op(eng, fn, deps, nosame=True)

        def xdma(q, fn, deps=()):
            return ch.dma(q, fn, deps, nosame=True)

        def dep0(c):
            return ph2_entry if c <= 1 else ()

        def issue_loads(c):
            par = c % 2
            r = res(f"rows{par}")
            d = list(r.wdeps()) + list(dep0(c))
            toks = []
            for kind, src in ((0, 0), (1, 1), (2, 1), (3, 2)):
                row = 1 if kind < 2 else 0
                toks.append(xdma("gpsimd", lambda e, kind=kind, src=src, row=row, par=par, c=c: e.dma_start(
                    out=rows[row:row + 1, (par * 4 + kind) * 1024:(par * 4 + kind + 1) * 1024],
                    in_=evec_hbm[src, c * 8:(c + 1) * 8, :].rearrange("c p -> () (c p)")),
                    deps=d))
                d = []
            r.wrote(*toks)
            for name, tile in (("KFM", KFM), ("QFM", QFM), ("KTM", KTM), ("VTM", VTM), ("ZTM", ZTM)):
                rr = res(f"{name}{par}")
                d = list(rr.wdeps()) + list(dep0(c))
                toks = []
                for b2 in range(2):
                    off = b2 * T + c * 128
                    if name == "KTM":
                        toks.append(xdma("sync", lambda e, b2=b2, off=off, tile=tile, par=par: e.dma_start(
                            out=tile[par][:, b2 * 256:(b2 + 1) * 256],
                            in_=kb_hbm[:, off:off + 128], transpose=True), deps=d))
                    elif name == "VTM":
                        toks.append(xdma("sync", lambda e, b2=b2, off=off, tile=tile, par=par: e.dma_start(
                            out=tile[par][:, b2 * 512:(b2 + 1) * 512],
                            in_=vb_hbm[:, off:off + 128], transpose=True), deps=d))
                    elif name == "ZTM":
                        toks.append(xdma("sync", lambda e, b2=b2, off=off, tile=tile, par=par: e.dma_start(
                            out=tile[par][:, b2 * 512:(b2 + 1) * 512],
                            in_=zb_hbm[:, off:off + 128], transpose=True), deps=d))
                    else:
                        hb = kb_hbm if name == "KFM" else qb_hbm
                        for hh in range(2):
                            toks.append(xdma("sync", lambda e, b2=b2, hh=hh, off=off, tile=tile, par=par, hb=hb: e.dma_start(
                                out=tile[par][:, (b2 * 2 + hh) * 128:(b2 * 2 + hh + 1) * 128],
                                in_=hb[hh * 128:(hh + 1) * 128, off:off + 128]), deps=d))
                    d = []
                rr.wrote(*toks)

        def pidx(p):
            b2, vh = p // 4, p % 4
            return b2, vh, vh // 2, vh * 2 + b2

        def emit_A1(c):
            par = c % 2
            rowres = res(f"rows{par}")
            rowtok = rowres.rdeps()
            for bank, msk, rkL, rkR, outg, bname, gname in (
                    (bEL, mskL4[:], 0, 2, ELg, "bEL", "ELg"),
                    (bEQ, mskQ4[:], 1, 3, EQg, "bEQ", "EQg")):
                B = res(bname)
                for hgi in range(2):
                    d = B.wdeps() + rowtok + list(dep0(c))
                    t = xop("tensor", lambda e, bank=bank, msk=msk: e.matmul(
                        bank, lhsT=IDB, rhs=msk, start=True, stop=True), deps=d)
                    for vh in range(4):
                        p = hgi * 4 + vh
                        b2, _, _, pp = pidx(p)
                        lsl = (par * 4 + rkL) * 1024 + pp * 128
                        rsl = (par * 4 + rkR) * 1024 + pp * 128
                        t = xop("tensor", lambda e, bank=bank, vh=vh, lsl=lsl, rsl=rsl: e.matmul(
                            bank[:, vh * 128:(vh + 1) * 128],
                            lhsT=rows[0:2, lsl:lsl + 128], rhs=rows[0:2, rsl:rsl + 128],
                            start=False, stop=True))
                        rowres.read(t)
                    B.wrote(t)
                    rg = res(f"{gname}_h{hgi}")
                    ev = xop("scalar", lambda e, bank=bank, outg=outg, hgi=hgi: e.activation(
                        out=outg[:, hgi * 512:(hgi + 1) * 512], in_=bank, func=AF.Exp),
                        deps=B.rdeps() + rg.wdeps())
                    B.read(ev)
                    rg.wrote(ev)
            nktoks = []
            ptltoks = []
            for u in range(4):
                B = res(f"bA{u % 2}")
                bank = bA[u % 2]
                pA = 2 * u
                hgi = u // 2
                d = B.wdeps() + list(dep0(c))
                t = None
                for i2 in range(2):
                    p = pA + i2
                    b2, vh, kh, pp = pidx(p)
                    ks = (b2 * 2 + kh) * 128
                    t = xop("tensor", lambda e, bank=bank, i2=i2, ks=ks, par=par: e.matmul(
                        bank[:, i2 * 128:(i2 + 1) * 128],
                        lhsT=KFM[par][:, ks:ks + 128], rhs=KFM[par][:, ks:ks + 128],
                        start=True, stop=True), deps=d)
                    d = []
                    res(f"KFM{par}").read(t)
                    t = xop("tensor", lambda e, bank=bank, i2=i2, ks=ks, par=par: e.matmul(
                        bank[:, 256 + i2 * 128:256 + (i2 + 1) * 128],
                        lhsT=KFM[par][:, ks:ks + 128], rhs=QFM[par][:, ks:ks + 128],
                        start=True, stop=True))
                    res(f"KFM{par}").read(t)
                    res(f"QFM{par}").read(t)
                B.wrote(t)
                ev1 = xop("vector", lambda e, bank=bank, pA=pA: e.tensor_scalar(
                    out=nKKg[:, pA * 128:(pA + 2) * 128], in0=bank[:, 0:256],
                    scalar1=-1.0, scalar2=None, op0=ALU.mult),
                    deps=B.rdeps() + res("nKK").wdeps())
                ev2 = xop("vector", lambda e, bank=bank, pA=pA, par=par: e.tensor_tensor(
                    out=Ptlg[par][:, pA * 128:(pA + 2) * 128], in0=bank[:, 256:512],
                    in1=EQg[:, pA * 128:(pA + 2) * 128], op=ALU.mult),
                    deps=B.rdeps() + res(f"EQg_h{hgi}").rdeps() + res(f"Ptl{par}").wdeps())
                res(f"EQg_h{hgi}").read(ev2)
                B.read(ev1)
                B.read(ev2)
                nktoks.append(ev1)
                ptltoks.append(ev2)
            res("nKK").wrote(*nktoks)
            res(f"Ptl{par}").wrote(*ptltoks)
            for hgi in range(2):
                sl = slice(hgi * 512, (hgi + 1) * 512)
                t = xop("vector", lambda e, sl=sl, par=par: e.tensor_tensor(
                    out=Ppg[par][0][:, sl], in0=nKKg[:, sl], in1=ELg[:, sl], op=ALU.mult),
                    deps=res("nKK").rdeps() + res(f"ELg_h{hgi}").rdeps() + res(f"P{par}0h{hgi}").wdeps())
                res(f"ELg_h{hgi}").read(t)
                res("nKK").read(t)
                res(f"P{par}0h{hgi}").wrote(t)
            B = res("bEL")
            for hgi in range(2):
                d = B.wdeps() + res(f"P{par}0h{hgi}").rdeps()
                for vh in range(4):
                    p = hgi * 4 + vh
                    t = xop("tensor", lambda e, p=p, vh=vh, par=par: e.matmul(
                        bEL[:, vh * 128:(vh + 1) * 128],
                        lhsT=Ppg[par][0][:, p * 128:(p + 1) * 128], rhs=IDB,
                        start=True, stop=True), deps=d)
                    d = []
                    res(f"P{par}0h{hgi}").read(t)
                B.wrote(t)
                ev = xop("vector", lambda e, hgi=hgi, par=par: e.tensor_copy(
                    out=Zpg[par][0][:, hgi * 512:(hgi + 1) * 512], in_=bEL),
                    deps=B.rdeps() + res(f"Z{par}0h{hgi}").wdeps())
                B.read(ev)
                res(f"Z{par}0h{hgi}").wrote(ev)

        def emit_B_level(c, j):
            par = c % 2
            for hgi in range(2):
                dz = res("bZ").wdeps() + res(f"Z{par}{j - 1}h{hgi}").rdeps() + res(f"P{par}{j - 1}h{hgi}").rdeps()
                t = None
                for vh in range(4):
                    p = hgi * 4 + vh
                    sl = slice(p * 128, (p + 1) * 128)
                    t = xop("tensor", lambda e, vh=vh, sl=sl, j=j, par=par: e.matmul(
                        bZ[:, vh * 128:(vh + 1) * 128],
                        lhsT=Ppg[par][j - 1][:, sl], rhs=Zpg[par][j - 1][:, sl],
                        start=True, stop=True), deps=dz)
                    dz = ()
                    res(f"Z{par}{j - 1}h{hgi}").read(t)
                    res(f"P{par}{j - 1}h{hgi}").read(t)
                res("bZ").wrote(t)
                ev = xop("scalar", lambda e, hgi=hgi, j=j, par=par: e.copy(
                    out=Zpg[par][j][:, hgi * 512:(hgi + 1) * 512], in_=bZ),
                    deps=res("bZ").rdeps() + res(f"Z{par}{j}h{hgi}").wdeps())
                res("bZ").read(ev)
                res(f"Z{par}{j}h{hgi}").wrote(ev)
                if j < 6:
                    dp = res("bP").wdeps()
                    t = None
                    for vh in range(4):
                        p = hgi * 4 + vh
                        sl = slice(p * 128, (p + 1) * 128)
                        t = xop("tensor", lambda e, vh=vh, sl=sl, j=j, par=par: e.matmul(
                            bP[:, vh * 128:(vh + 1) * 128],
                            lhsT=Zpg[par][j - 1][:, sl], rhs=Ppg[par][j - 1][:, sl],
                            start=True, stop=True), deps=dp)
                        dp = ()
                        res(f"Z{par}{j - 1}h{hgi}").read(t)
                        res(f"P{par}{j - 1}h{hgi}").read(t)
                    res("bP").wrote(t)
                    ev = xop("vector", lambda e, hgi=hgi, j=j, par=par: e.tensor_copy(
                        out=Ppg[par][j][:, hgi * 512:(hgi + 1) * 512], in_=bP),
                        deps=res("bP").rdeps() + res(f"P{par}{j}h{hgi}").wdeps())
                    res("bP").read(ev)
                    res(f"P{par}{j}h{hgi}").wrote(ev)

        def emit_A2(c):
            par = c % 2
            rb = res("bVt")
            dW = rb.wdeps()
            bvt_toks = []
            for p in range(8):
                b2, vh, kh, pp = pidx(p)
                cc = c * 8 + pp
                t = xop("gpsimd", lambda e, p=p, cc=cc, par=par: e.tensor_scalar(
                    out=bVtT[:, p * 128:(p + 1) * 128], in0=VTM[par][:, p * 128:(p + 1) * 128],
                    scalar1=bexT[:, cc:cc + 1], scalar2=None, op0=ALU.mult),
                    deps=res(f"VTM{par}").rdeps() + dW)
                dW = []
                res(f"VTM{par}").read(t)
                bvt_toks.append(t)
            rb.wrote(*bvt_toks)
            x0toks = [[], []]
            qstoks = []
            for u in range(4):
                B = res(f"bA{u % 2}")
                bank = bA[u % 2]
                pA = 2 * u
                d = B.wdeps() + list(dep0(c))
                t = None
                for i2 in range(2):
                    p = pA + i2
                    b2, vh, kh, pp = pidx(p)
                    ks = (b2 * 2 + kh) * 128
                    t = xop("tensor", lambda e, bank=bank, i2=i2, ks=ks, p=p, par=par: e.matmul(
                        bank[:, i2 * 128:(i2 + 1) * 128],
                        lhsT=KFM[par][:, ks:ks + 128], rhs=Sbt[p][:],
                        start=True, stop=True), deps=d + res(f"Sbt{p}").rdeps())
                    d = []
                    res(f"KFM{par}").read(t)
                    res(f"Sbt{p}").read(t)
                    t = xop("tensor", lambda e, bank=bank, i2=i2, ks=ks, p=p, par=par: e.matmul(
                        bank[:, 256 + i2 * 128:256 + (i2 + 1) * 128],
                        lhsT=QFM[par][:, ks:ks + 128], rhs=Sbt[p][:],
                        start=True, stop=True))
                    res(f"QFM{par}").read(t)
                    res(f"Sbt{p}").read(t)
                B.wrote(t)
                for i2 in range(2):
                    p = pA + i2
                    b2, vh, kh, pp = pidx(p)
                    cc = c * 8 + pp
                    hgi = p // 4
                    ev = xop("vector", lambda e, bank=bank, i2=i2, p=p, cc=cc: e.scalar_tensor_tensor(
                        out=Xg[0][:, p * 128:(p + 1) * 128], in0=bank[:, i2 * 128:(i2 + 1) * 128],
                        scalar=nrscT[:, cc:cc + 1], in1=bVtT[:, p * 128:(p + 1) * 128],
                        op0=ALU.mult, op1=ALU.add),
                        deps=B.rdeps() + [bvt_toks[p]] + res(f"Xg0h{hgi}").wdeps())
                    rb.read(ev)
                    B.read(ev)
                    x0toks[hgi].append(ev)
                    ev2 = xop("vector", lambda e, bank=bank, i2=i2, p=p, cc=cc: e.tensor_scalar(
                        out=qsbg[:, p * 128:(p + 1) * 128], in0=bank[:, 256 + i2 * 128:256 + (i2 + 1) * 128],
                        scalar1=oscT[:, cc:cc + 1], scalar2=None, op0=ALU.mult),
                        deps=B.rdeps() + res("qsb").wdeps())
                    B.read(ev2)
                    qstoks.append(ev2)
            res("Xg0h0").wrote(*x0toks[0])
            res("Xg0h1").wrote(*x0toks[1])
            res("qsb").wrote(*qstoks)

        def emit_C_level(c, j):
            par = c % 2
            xin_h = [res(f"Xg{j % 2}h0"), res(f"Xg{j % 2}h1")]
            for hgi in range(2):
                B = res(f"bX{hgi}")
                bank = bX[hgi]
                d = B.wdeps() + res(f"Z{par}{j}h{hgi}").rdeps() + xin_h[hgi].rdeps()
                t = None
                for vh in range(4):
                    p = hgi * 4 + vh
                    sl = slice(p * 128, (p + 1) * 128)
                    t = xop("tensor", lambda e, vh=vh, sl=sl, j=j, par=par, bank=bank: e.matmul(
                        bank[:, vh * 128:(vh + 1) * 128],
                        lhsT=Zpg[par][j][:, sl], rhs=Xg[j % 2][:, sl],
                        start=True, stop=False), deps=d)
                    d = []
                    t = xop("tensor", lambda e, vh=vh, sl=sl, j=j, bank=bank: e.matmul(
                        bank[:, vh * 128:(vh + 1) * 128],
                        lhsT=IDB, rhs=Xg[j % 2][:, sl],
                        start=False, stop=True))
                    res(f"Z{par}{j}h{hgi}").read(t)
                    xin_h[hgi].read(t)
                B.wrote(t)
                ev = xop("scalar", lambda e, hgi=hgi, j=j, bank=bank: e.copy(
                    out=Xg[(j + 1) % 2][:, hgi * 512:(hgi + 1) * 512], in_=bank),
                    deps=B.rdeps() + res(f"Xg{(j + 1) % 2}h{hgi}").wdeps())
                B.read(ev)
                res(f"Xg{(j + 1) % 2}h{hgi}").wrote(ev)

        def emit_D(c, out_toks):
            par = c % 2
            nxt = (c + 1) % 2
            cur = c % 2
            for p in range(8):
                b2, vh, kh, pp = pidx(p)
                cc = c * 8 + pp
                hgi = p // 4
                t = xop("gpsimd", lambda e, p=p, cc=cc: e.tensor_scalar(
                    out=Dbarg[:, p * 128:(p + 1) * 128], in0=Xg[1][:, p * 128:(p + 1) * 128],
                    scalar1=dscT[:, cc:cc + 1], scalar2=None, op0=ALU.mult),
                    deps=res(f"Xg1h{hgi}").rdeps() + res(f"Db{p}").wdeps())
                res(f"Xg1h{hgi}").read(t)
                res(f"Db{p}").wrote(t)
            for hgi in range(2):
                B = res(f"bX{hgi}")
                bank = bX[hgi]
                d = B.wdeps() + res(f"Ptl{par}").rdeps() + res(f"Xg1h{hgi}").rdeps()
                t = None
                for vh in range(4):
                    p = hgi * 4 + vh
                    sl = slice(p * 128, (p + 1) * 128)
                    t = xop("tensor", lambda e, vh=vh, sl=sl, par=par, bank=bank: e.matmul(
                        bank[:, vh * 128:(vh + 1) * 128],
                        lhsT=Ptlg[par][:, sl], rhs=Xg[1][:, sl],
                        start=True, stop=True), deps=d)
                    d = []
                    res(f"Ptl{par}").read(t)
                    res(f"Xg1h{hgi}").read(t)
                B.wrote(t)
                ev = xop("vector", lambda e, hgi=hgi, bank=bank: e.tensor_tensor(
                    out=ObufT[:, hgi * 512:(hgi + 1) * 512], in0=bank,
                    in1=qsbg[:, hgi * 512:(hgi + 1) * 512], op=ALU.add),
                    deps=B.rdeps() + res("qsb").rdeps() + res(f"Obuf{hgi}").wdeps())
                B.read(ev)
                res("qsb").read(ev)
                res(f"Obuf{hgi}").wrote(ev)
                d = B.wdeps() + res(f"KTM{par}").rdeps()
                stoks = []
                for vh in range(4):
                    p = hgi * 4 + vh
                    b2, _, kh, pp = pidx(p)
                    cc = c * 8 + pp
                    ks = (b2 * 2 + kh) * 128
                    t = xop("tensor", lambda e, vh=vh, ks=ks, p=p, par=par, bank=bank: e.matmul(
                        bank[:, vh * 128:(vh + 1) * 128],
                        lhsT=KTM[par][:, ks:ks + 128], rhs=Dbarg[:, p * 128:(p + 1) * 128],
                        start=True, stop=True), deps=d + res(f"Db{p}").rdeps())
                    d = []
                    res(f"KTM{par}").read(t)
                    res(f"Db{p}").read(t)
                    stoks.append(t)
                B.wrote(stoks[-1])
                for vh in range(4):
                    p = hgi * 4 + vh
                    b2, _, _, pp = pidx(p)
                    cc = c * 8 + pp
                    ev = xop("vector", lambda e, vh=vh, p=p, cc=cc, nxt=nxt, cur=cur, bank=bank: e.scalar_tensor_tensor(
                        out=(St2[p] if nxt else St[p])[:], in0=(St2[p] if cur else St[p])[:],
                        scalar=gcb[:, cc:cc + 1], in1=bank[:, vh * 128:(vh + 1) * 128],
                        op0=ALU.mult, op1=ALU.add),
                        deps=[stoks[vh]] + res(f"St{nxt}_{p}").wdeps() + res(f"St{cur}_{p}").rdeps())
                    B.read(ev)
                    res(f"St{nxt}_{p}").wrote(ev)
                    res(f"St{cur}_{p}").read(ev)
                    cp = xop("gpsimd", lambda e, p=p, nxt=nxt: e.tensor_copy(
                        out=Sbt[p][:], in_=(St2[p] if nxt else St[p])[:]),
                        deps=[ev] + res(f"Sbt{p}").wdeps())
                    res(f"St{nxt}_{p}").read(cp)
                    res(f"Sbt{p}").wrote(cp)
                sq = xop("scalar", lambda e, hgi=hgi: e.activation(
                    out=sqgT[:, hgi * 512:(hgi + 1) * 512],
                    in_=ObufT[:, hgi * 512:(hgi + 1) * 512], func=AF.Square),
                    deps=res(f"Obuf{hgi}").rdeps() + res(f"sqg{hgi}").wdeps())
                res(f"Obuf{hgi}").read(sq)
                res(f"sqg{hgi}").wrote(sq)
                g0 = hgi * 12
                rd = xop("vector", lambda e, hgi=hgi, g0=g0: e.tensor_reduce(
                    out=gstT[:, g0:g0 + 4].rearrange("p (h o) -> p h o", o=1),
                    in_=sqgT[:, hgi * 512:(hgi + 1) * 512].rearrange("p (h d) -> p h d", h=4),
                    axis=AX.X, op=ALU.add),
                    deps=[sq] + res(f"gst{hgi}").wdeps())
                res(f"sqg{hgi}").read(rd)
                ln = xop("scalar", lambda e, g0=g0: e.activation(
                    out=gstT[:, g0 + 4:g0 + 8], in_=gstT[:, g0:g0 + 4], func=AF.Ln,
                    scale=1.0 / 128, bias=BEP), deps=[rd])
                ex = xop("scalar", lambda e, g0=g0: e.activation(
                    out=gstT[:, g0 + 8:g0 + 12], in_=gstT[:, g0 + 4:g0 + 8], func=AF.Exp,
                    scale=-0.5), deps=[ln])
                mtoks = []
                for vh in range(4):
                    p = hgi * 4 + vh
                    mt = xop("gpsimd", lambda e, p=p, vh=vh, hgi=hgi, g0=g0: e.tensor_scalar(
                        out=ggtT[:, p * 128:(p + 1) * 128], in0=ObufT[:, p * 128:(p + 1) * 128],
                        scalar1=gstT[:, g0 + 8 + vh:g0 + 9 + vh], scalar2=None, op0=ALU.mult),
                        deps=[ex] + res(f"Obuf{hgi}").rdeps() + res(f"ggt{hgi}").wdeps())
                    res(f"Obuf{hgi}").read(mt)
                    mtoks.append(mt)
                res(f"gst{hgi}").wrote(ex)
                res(f"gst{hgi}").read(mtoks[-1])
                gg = xop("gpsimd", lambda e, hgi=hgi, par=par: e.tensor_tensor(
                    out=ggtT[:, hgi * 512:(hgi + 1) * 512], in0=ggtT[:, hgi * 512:(hgi + 1) * 512],
                    in1=ZTM[par][:, hgi * 512:(hgi + 1) * 512], op=ALU.mult),
                    deps=mtoks + res(f"ZTM{par}").rdeps())
                res(f"ZTM{par}").read(gg)
                res(f"ggt{hgi}").wrote(gg)
                off = hgi * T + c * 128
                dt = xdma("sync", lambda e, off=off, hgi=hgi: e.dma_start(
                    out=gated_tm[off:off + 128, :], in_=ggtT[:, hgi * 512:(hgi + 1) * 512]),
                    deps=[gg])
                res(f"ggt{hgi}").read(dt)
                out_toks.append(dt)

        gated_toks = []
        issue_loads(0)
        for k in range(NCH + 1):
            if k < NCH:
                emit_A1(k)
            for j in range(7):
                if k > 0 and j == 0:
                    emit_A2(k - 1)
                if j >= 1 and k < NCH:
                    emit_B_level(k, j)
                if k > 0:
                    emit_C_level(k - 1, j)
            if k > 0:
                emit_D(k - 1, gated_toks)
            if k + 1 < NCH:
                issue_loads(k + 1)
        ch.serial_last = ch.all_tokens()
        pool.reset(m2)

        if dbg >= 2:
            O("sync", lambda e: e.dma_start(out=dbg_aps["d_gat"][:], in_=gated_tm[:]), "dma")

        m3 = pool.mark()
        gfm = [pool.alloc(128, BT, BF16) for _ in range(4)]
        wobq = pool.alloc(128, 4 * 512, BF16)
        ptile = pool.alloc(128, 512, BF16)
        oab = pool.alloc(128, 8 * 2048, BF16)
        for vt in range(4):
            O("sync", lambda e, vt=vt: e.dma_start(
                out=gfm[vt], in_=gated_tm[:, vt * 128:(vt + 1) * 128], transpose=True), "dma")
        for dq in range(4):
            with ch.group():
                for vt in range(4):
                    O("sync", lambda e, dq=dq, vt=vt: e.dma_start(
                        out=wobq[:, vt * 512:(vt + 1) * 512],
                        in_=wob_sl[vt * 128:(vt + 1) * 128, dq * 512:(dq + 1) * 512]), "dma")
            for tt in range(64):
                for vt in range(4):
                    O("tensor", lambda e, tt=tt, vt=vt: e.matmul(
                        psS1[:, :], lhsT=gfm[vt][:, tt * 128:(tt + 1) * 128],
                        rhs=wobq[:, vt * 512:(vt + 1) * 512], start=(vt == 0), stop=(vt == 3)))
                O("scalar", lambda e: e.copy(out=ptile, in_=psS1[:, :]))
                O("sync", lambda e, dq=dq, tt=tt: e.dma_start(
                    out=part_hbm[tt * 128:(tt + 1) * 128, dq * 512:(dq + 1) * 512],
                    in_=ptile), "dma")
        O("gpsimd", lambda e: e.collective_compute(
            "ReduceScatter", ALU.add, replica_groups=RG, ins=[part_hbm[:]], outs=[rs_out[:]]), "cc")
        pool.reset(m3)
        gfm_unused = None
        oab = pool.alloc(128, 8 * 2048, BF16)
        with ch.group():
            for tt in range(8):
                O("sync", lambda e, tt=tt: e.dma_start(
                    out=oab[:, tt * 2048:(tt + 1) * 2048],
                    in_=rs_out[tt * 128:(tt + 1) * 128, :]), "dma")
        oaf = pool.alloc(128, D, F32)
        xt3 = pool.alloc(128, D, F32)
        x2t = pool.alloc(128, D, F32)
        scr3 = pool.alloc(128, D, F32)
        st3 = pool.alloc(128, 8, F32)
        x2lb = pool.alloc(128, D, BF16)
        for tt in range(8):
            O("scalar", lambda e, tt=tt: e.copy(out=oaf, in_=oab[:, tt * 2048:(tt + 1) * 2048]))
            O("sync", lambda e, tt=tt: e.dma_start(out=xt3, in_=x_sh[tt * 128:(tt + 1) * 128, :]), "dma")
            ln_tile(oaf, None, scr3, st3, out_f32=x2t, res=xt3)
            O("sync", lambda e, tt=tt: e.dma_start(out=x2_hbm[tt * 128:(tt + 1) * 128, :], in_=x2t), "dma")
            ln_tile(x2t, x2lb, scr3, st3)
            O("sync", lambda e, tt=tt: e.dma_start(out=x2ln_hbm[tt * 128:(tt + 1) * 128, :], in_=x2lb), "dma")
            if dbg >= 3:
                O("sync", lambda e, tt=tt: e.dma_start(
                    out=dbg_aps["d_x2"][tt * 128:(tt + 1) * 128, :], in_=x2t), "dma")
        pool.reset(m3)

        wtt = pool.alloc(128, 8192, BF16)
        for k in range(16):
            O("sync", lambda e, k=k: e.dma_start(
                out=wtt, in_=ag_w1[:, k * 128:(k + 1) * 128], transpose=True), "dma")
            O("sync", lambda e, k=k: e.dma_start(out=w1t_hbm[k * 128:(k + 1) * 128, :], in_=wtt), "dma")
            O("sync", lambda e, k=k: e.dma_start(
                out=wtt, in_=ag_w3[:, k * 128:(k + 1) * 128], transpose=True), "dma")
            O("sync", lambda e, k=k: e.dma_start(out=w3t_hbm[k * 128:(k + 1) * 128, :], in_=wtt), "dma")
        pool.reset(m3)

        x2lT = pool.alloc(128, 16 * 1024, BF16)
        with ch.group():
            for k in range(16):
                O("sync", lambda e, k=k: e.dma_start(
                    out=x2lT[:, k * 1024:(k + 1) * 1024],
                    in_=x2ln_hbm[:, k * 128:(k + 1) * 128], transpose=True), "dma")
        m4pre = pool.mark()
        blk1 = pool.alloc(128, 4096, BF16)
        blk3 = pool.alloc(128, 4096, BF16)
        sg4 = pool.alloc(128, 1024, BF16)
        hid4 = pool.alloc(128, 1024, BF16)
        yt4 = pool.alloc(128, 1024, BF16)
        w1v = w1t_hbm.rearrange("(k p) m -> p k m", p=128)
        w3v = w3t_hbm.rearrange("(k p) m -> p k m", p=128)
        for mp in range(32):
            msl = slice(mp * 256, (mp + 1) * 256)
            with ch.group():
                O("sync", lambda e, msl=msl: e.dma_start(
                    out=blk1.rearrange("p (k m) -> p k m", k=16), in_=w1v[:, :, msl]), "dma")
                O("sync", lambda e, msl=msl: e.dma_start(
                    out=blk3.rearrange("p (k m) -> p k m", k=16), in_=w3v[:, :, msl]), "dma")
            for mh in range(2):
                m = 2 * mp + mh
                moff = mh * 128
                for k in range(16):
                    for h in range(2):
                        O("tensor", lambda e, k=k, h=h, moff=moff: e.matmul(
                            psBig[:, h * 512:(h + 1) * 512],
                            lhsT=blk1[:, k * 256 + moff: k * 256 + moff + 128],
                            rhs=x2lT[:, k * 1024 + h * 512: k * 1024 + (h + 1) * 512],
                            start=(k == 0), stop=(k == 15)))
                for k in range(16):
                    for h in range(2):
                        O("tensor", lambda e, k=k, h=h, moff=moff: e.matmul(
                            psBig2[:, h * 512:(h + 1) * 512],
                            lhsT=blk3[:, k * 256 + moff: k * 256 + moff + 128],
                            rhs=x2lT[:, k * 1024 + h * 512: k * 1024 + (h + 1) * 512],
                            start=(k == 0), stop=(k == 15)))
                O("scalar", lambda e: e.activation(out=sg4, in_=psBig[:, :], func=AF.Silu))
                O("vector", lambda e: e.tensor_tensor(out=hid4, in0=sg4, in1=psBig2[:, :], op=ALU.mult))
                for h in range(2):
                    O("tensor", lambda e, h=h: e.matmul(
                        psBig[:, h * 512:(h + 1) * 512], lhsT=h128t[:],
                        rhs=hid4[:, h * 512:(h + 1) * 512], start=True, stop=True))
                O("scalar", lambda e: e.activation(out=yt4, in_=psBig[:, :], func=AF.Copy, scale=1.0 / 128))
                O("sync", lambda e, m=m: e.dma_start(out=ybuf_hbm[m], in_=yt4), "dma")
        pool.reset(m3)

        yr = pool.alloc(64, 1024, BF16)
        h2t = pool.alloc(64, 1024, BF16)
        for j in range(128):
            O("sync", lambda e, j=j: e.dma_start(out=yr, in_=ybuf_hbm[:, j, :]), "dma")
            for h in range(2):
                O("tensor", lambda e, h=h: e.matmul(
                    psBig[0:64, h * 512:(h + 1) * 512], lhsT=h64t[:],
                    rhs=yr[:, h * 512:(h + 1) * 512], start=True, stop=True))
            O("scalar", lambda e: e.activation(
                out=h2t, in_=psBig[0:64, 0:1024], func=AF.Copy, scale=float(np.sqrt(2.0))))
            O("sync", lambda e, j=j: e.dma_start(
                out=h2_hbm[:, j % 2, :, j // 2, :].rearrange("tt a tp -> a tt tp"),
                in_=h2t.rearrange("a (tt tp) -> a tt tp", tt=8)), "dma")
        pool.reset(m3)

        w2q = pool.alloc(128, 64 * 512, BF16)
        h2blk = pool.alloc(128, 8192, BF16)
        ffob = pool.alloc(128, 8 * 2048, BF16)
        w2v = ag_w2t.rearrange("(a b) d -> b a d", a=64)
        for dq in range(4):
            with ch.group():
                for k in range(64):
                    O("sync", lambda e, dq=dq, k=k: e.dma_start(
                        out=w2q[:, k * 512:(k + 1) * 512],
                        in_=w2v[2 * k:2 * k + 2, :, dq * 512:(dq + 1) * 512]), "dma")
            for tt in range(8):
                O("sync", lambda e, tt=tt: e.dma_start(
                    out=h2blk, in_=h2_hbm[tt].rearrange("h a m p -> (h a) (m p)")), "dma")
                for k in range(64):
                    O("tensor", lambda e, k=k: e.matmul(
                        psS1[:, :], lhsT=h2blk[:, k * 128:(k + 1) * 128],
                        rhs=w2q[:, k * 512:(k + 1) * 512], start=(k == 0), stop=(k == 63)))
                O("scalar", lambda e, dq=dq, tt=tt: e.copy(
                    out=ffob[:, tt * 2048 + dq * 512: tt * 2048 + (dq + 1) * 512], in_=psS1[:, :]))
        ff_f = pool.alloc(128, D, F32)
        x2t4 = pool.alloc(128, D, F32)
        outt = pool.alloc(128, D, F32)
        scr4 = pool.alloc(128, D, F32)
        st4 = pool.alloc(128, 8, F32)
        for tt in range(8):
            O("scalar", lambda e, tt=tt: e.copy(out=ff_f, in_=ffob[:, tt * 2048:(tt + 1) * 2048]))
            O("sync", lambda e, tt=tt: e.dma_start(out=x2t4, in_=x2_hbm[tt * 128:(tt + 1) * 128, :]), "dma")
            ln_tile(ff_f, None, scr4, st4, out_f32=outt, res=x2t4)
            O("sync", lambda e, tt=tt: e.dma_start(out=out_sh[tt * 128:(tt + 1) * 128, :], in_=outt), "dma")

        ch.emit(block, drain=ch.all_tokens())

    return nc, dbg_aps


def make_inputs(core, inp):
    c = core
    qc = slice(256 * c, 256 * (c + 1))
    kc = slice(2048 + 256 * c, 2048 + 256 * (c + 1))
    vc = slice(4096 + 512 * c, 4096 + 512 * (c + 1))
    wqkv = np.asarray(inp["L0_attn_w_qkv"], dtype=np.float32)
    conv = np.asarray(inp["L0_attn_conv_w"], dtype=np.float32)
    wq = np.concatenate([wqkv[:, qc], wqkv[:, kc], wqkv[:, vc]], axis=1)
    cw = np.concatenate([conv[:, qc], conv[:, kc], conv[:, vc]], axis=1)
    x = np.asarray(inp["L0_attn_x"], dtype=np.float32).reshape(BT, D)
    return {
        "x_sh": np.ascontiguousarray(x[c * TSH:(c + 1) * TSH]),
        "wq_sl": np.ascontiguousarray(wq),
        "wz_sl": np.ascontiguousarray(np.asarray(inp["L0_attn_w_z"], dtype=np.float32)[:, 512 * c:512 * (c + 1)]),
        "wba_sl": np.ascontiguousarray(np.concatenate(
            [np.asarray(inp["L0_attn_w_b"], dtype=np.float32)[:, 4 * c:4 * (c + 1)],
             np.asarray(inp["L0_attn_w_a"], dtype=np.float32)[:, 4 * c:4 * (c + 1)]], axis=1)),
        "cwT_sl": np.ascontiguousarray(cw.T),
        "dta8": np.ascontiguousarray(np.concatenate(
            [np.zeros(4, np.float32), np.asarray(inp["L0_attn_dt_bias"], dtype=np.float32)[4 * c:4 * (c + 1)]])[:, None]),
        "alog8": np.ascontiguousarray(np.concatenate(
            [np.zeros(4, np.float32), np.asarray(inp["L0_attn_A_log"], dtype=np.float32)[4 * c:4 * (c + 1)]])[:, None]),
        "normw": np.ascontiguousarray(np.asarray(inp["L0_attn_norm_w"], dtype=np.float32)[:, None]),
        "wo_sl": np.ascontiguousarray(np.asarray(inp["L0_attn_w_out"], dtype=np.float32)[512 * c:512 * (c + 1), :]),
        "w1_sl": np.ascontiguousarray(np.asarray(inp["L0_ffn_W1"], dtype=np.float32)[1024 * c:1024 * (c + 1), :]),
        "w3_sl": np.ascontiguousarray(np.asarray(inp["L0_ffn_W3"], dtype=np.float32)[1024 * c:1024 * (c + 1), :]),
        "w2_sl": np.ascontiguousarray(np.asarray(inp["L0_ffn_W2"], dtype=np.float32)[:, 1024 * c:1024 * (c + 1)]),
    }


def kernel(**inp):
    nc, _ = build(dbg=0)
    in_maps = [make_inputs(c, inp) for c in range(8)]
    res = run_bass_kernel_spmd(nc, in_maps, core_ids=list(range(8)))
    out = np.concatenate([res.results[c]["out_sh"] for c in range(8)], axis=0)
    return out.reshape(B, T, D).astype(np.float32)


# version 10
# speedup vs baseline: 2.1031x; 1.5264x over previous
import numpy as np
import ml_dtypes
import concourse.bass as bass
import concourse.mybir as mybir
from concourse.bass_utils import run_bass_kernel_spmd

F32 = mybir.dt.float32
BF16 = mybir.dt.bfloat16
AF = mybir.ActivationFunctionType
ALU = mybir.AluOpType
AX = mybir.AxisListType

B, T, D = 2, 4096, 2048
BT = B * T
CCH = 128
NCH = T // CCH
TSH = BT // 8
EPS = 1e-6
LNEPS = 1e-5
NEGINF = -30000.0
BF = ml_dtypes.bfloat16
LN128 = float(np.log(128.0))


def _hadamard(n):
    h = np.array([[1.0]])
    while h.shape[0] < n:
        h = np.block([[h, h], [h, -h]])
    return h


ENGS = ("sync", "scalar", "vector", "tensor", "gpsimd")


class Sched:

    NSLOT = 8

    def __init__(self, nc):
        self.nc = nc
        self.ops = {e: [] for e in ENGS}
        self.cnt = {e: 0 for e in ENGS}
        self.csem = {e: nc.alloc_semaphore(name=f"cs_{e}") for e in ENGS}
        self.waited = {e: {} for e in ENGS}
        self.dsem = {}
        self.dcnt = {}
        self.drr = {}
        for q in ("sync", "scalar", "vector", "gpsimd"):
            self.drr[q] = 0
            for i in range(self.NSLOT):
                self.dsem[(q, i)] = nc.alloc_semaphore(name=f"d_{q}{i}")
                self.dcnt[(q, i)] = 0
        self.n_cc = 0
        self.serial_last = ()
        self._grp = None

    SKIP_SAME = False
    PRUNE = True

    def _waits(self, eng, deps, nosame=False):
        need = {}
        own = self.csem[eng]
        for t in deps:
            if t is None:
                continue
            sem, val = t
            if (nosame or self.SKIP_SAME) and sem is own:
                continue
            k = sem.num
            if k not in need or need[k][1] < val:
                need[k] = (sem, val)
        out = []
        w = self.waited[eng]
        for k, (sem, val) in need.items():
            if self.PRUNE and w.get(k, -1) >= val:
                continue
            w[k] = val
            out.append((sem, val))
        return out

    def op(self, eng, fn, deps=(), nosame=False):
        waits = self._waits(eng, deps, nosame)
        self.cnt[eng] += 1
        tok = (self.csem[eng], self.cnt[eng])
        self.ops[eng].append((waits, fn, tok[0], 1))
        return tok

    def dma(self, q, fn, deps=(), nosame=False):
        slot = self.drr[q] % self.NSLOT
        self.drr[q] += 1
        sem = self.dsem[(q, slot)]
        prev = self.dcnt[(q, slot)]
        deps = list(deps)
        if prev:
            deps.append((sem, prev))
        waits = self._waits(q, deps, nosame)
        self.dcnt[(q, slot)] = prev + 16
        self.ops[q].append((waits, fn, sem, 16))
        return (sem, prev + 16)

    def cc(self, fn, deps=()):
        sem = self.nc.alloc_semaphore(name=f"ccs{self.n_cc}")
        self.n_cc += 1
        waits = self._waits("gpsimd", deps)
        self.ops["gpsimd"].append((waits, fn, sem, 1))
        return (sem, 1)

    def C(self, eng, fn, kind="comp"):
        deps = self.serial_last
        if kind == "dma":
            tok = self.dma(eng, fn, deps)
        elif kind == "cc":
            tok = self.cc(fn, deps)
        else:
            tok = self.op(eng, fn, deps)
        if self._grp is None:
            self.serial_last = (tok,)
        else:
            self._grp.append(tok)
        return tok

    def group(self):
        sched = self

        class G:
            def __enter__(self):
                sched._grp = []

            def __exit__(self, *a):
                if sched._grp:
                    sched.serial_last = tuple(sched._grp)
                sched._grp = None
                return False

        return G()

    def all_tokens(self):
        toks = [(self.csem[e], self.cnt[e]) for e in ENGS if self.cnt[e] > 0]
        for key, c in self.dcnt.items():
            if c > 0:
                toks.append((self.dsem[key], c))
        return tuple(toks)

    def emit(self, block, drain=()):
        fin = self._waits("sync", drain)
        for e in ENGS:
            ops = self.ops[e]
            tail = fin if e == "sync" else []

            def body(eng, ops=ops, tail=tail):
                for waits, fn, sem, inc in ops:
                    for s, v in waits:
                        eng.wait_ge(s, v)
                    inst = fn(eng)
                    if inst is not None and inc:
                        inst.then_inc(sem, inc)
                for s, v in tail:
                    eng.wait_ge(s, v)

            getattr(block, e)(body)


class Pool:

    def __init__(self, nc, name, nbytes):
        self.t = nc.alloc_sbuf_tensor(name, [128, nbytes], mybir.dt.uint8)
        self.off = 0
        self.peak = 0
        self.nbytes = nbytes

    def alloc(self, p, n, dtype):
        esz = mybir.dt.size(dtype)
        start = (self.off + 31) // 32 * 32
        self.off = start + n * esz
        self.peak = max(self.peak, self.off)
        assert self.off <= self.nbytes, f"pool overflow {self.off}"
        return self.t[0:p, start:start + n * esz].bitcast(dtype)

    def mark(self):
        return self.off

    def reset(self, m):
        self.off = m


def build(dbg=0):
    nc = bass.Bass("TRN2", target_bir_lowering=False)
    RG = [list(range(8))]

    x_sh = nc.dram_tensor("x_sh", [TSH, D], F32, kind="ExternalInput").ap()
    wq_sl = nc.dram_tensor("wq_sl", [D, 1024], F32, kind="ExternalInput").ap()
    wz_sl = nc.dram_tensor("wz_sl", [D, 512], F32, kind="ExternalInput").ap()
    wba_sl = nc.dram_tensor("wba_sl", [D, 8], F32, kind="ExternalInput").ap()
    cwT_sl = nc.dram_tensor("cwT_sl", [1024, 4], F32, kind="ExternalInput").ap()
    dta8 = nc.dram_tensor("dta8", [8, 1], F32, kind="ExternalInput").ap()
    alog8 = nc.dram_tensor("alog8", [8, 1], F32, kind="ExternalInput").ap()
    normw = nc.dram_tensor("normw", [128, 1], F32, kind="ExternalInput").ap()
    wo_sl = nc.dram_tensor("wo_sl", [512, D], F32, kind="ExternalInput").ap()
    w1_sl = nc.dram_tensor("w1_sl", [1024, D], F32, kind="ExternalInput").ap()
    w3_sl = nc.dram_tensor("w3_sl", [1024, D], F32, kind="ExternalInput").ap()
    w2_sl = nc.dram_tensor("w2_sl", [D, 1024], F32, kind="ExternalInput").ap()
    out_sh = nc.dram_tensor("out_sh", [TSH, D], F32, kind="ExternalOutput").ap()

    dram = nc.dram_tensor
    xln_sl = dram("xln_sl", [TSH, D], BF16).ap()
    ag_xln = dram("ag_xln", [BT, D], BF16).ap()
    w1b_sl = dram("w1b_sl", [1024, D], BF16).ap()
    w3b_sl = dram("w3b_sl", [1024, D], BF16).ap()
    w2b_sl = dram("w2b_sl", [D, 1024], BF16).ap()
    w2t_sl = dram("w2t_sl", [1024, D], BF16).ap()
    w1t_sl = dram("w1t_sl", [D, 1024], BF16).ap()
    w3t_sl = dram("w3t_sl", [D, 1024], BF16).ap()
    wob_sl = dram("wob_sl", [512, D], BF16).ap()
    ag_w1t = dram("ag_w1t", [8 * D, 1024], BF16).ap()
    ag_w3t = dram("ag_w3t", [8 * D, 1024], BF16).ap()
    ag_w2t = dram("ag_w2t", [8192, D], BF16).ap()
    part_q = [dram(f"part_q{i}", [BT, 512], BF16).ap() for i in range(4)]
    rs_q = [dram(f"rs_q{i}", [TSH, 512], BF16).ap() for i in range(4)]

    qb_hbm = dram("qb_hbm", [256, BT], BF16).ap()
    kb_hbm = dram("kb_hbm", [256, BT], BF16).ap()
    vb_hbm = dram("vb_hbm", [512, BT], BF16).ap()
    zb_hbm = dram("zb_hbm", [512, BT], BF16).ap()
    lng_hbm = dram("lng_hbm", [8, BT], F32).ap()
    lr_hbm = dram("lr_hbm", [4, BT], F32).ap()
    glay = dram("glay", [256, 128], F32).ap()
    blay = dram("blay", [256, 128], F32).ap()
    lrklay = dram("lrklay", [256, 128], F32).ap()
    lrqlay = dram("lrqlay", [256, 128], F32).ap()
    evec_hbm = dram("evec_hbm", [3, 256, CCH], F32).ap()
    scl_hbm = dram("scl_hbm", [4, 256, CCH], BF16).ap()
    gcx_hbm = dram("gcx_hbm", [256, 1], F32).ap()
    gated_tm = dram("gated_tm", [BT, 512], BF16).ap()
    part_hbm = dram("part_hbm", [BT, D], BF16).ap()
    rs_out = dram("rs_out", [TSH, D], BF16).ap()
    x2_hbm = dram("x2_hbm", [TSH, D], F32).ap()
    x2ln_hbm = dram("x2ln_hbm", [TSH, D], BF16).ap()
    w1t_hbm = dram("w1t_hbm", [D, 8192], BF16).ap()
    w3t_hbm = dram("w3t_hbm", [D, 8192], BF16).ap()
    ybuf_hbm = dram("ybuf_hbm", [64, 128, TSH], BF16).ap()
    h2_hbm = dram("h2_hbm", [8, 2, 64, 64, 128], BF16).ap()

    dbg_aps = {}
    if dbg >= 1:
        dbg_aps["d_mix"] = dram("d_mix", [1024, BT], F32, kind="ExternalOutput").ap()
        dbg_aps["d_lng"] = dram("d_lng", [8, BT], F32, kind="ExternalOutput").ap()
        dbg_aps["d_lr"] = dram("d_lr", [4, BT], F32, kind="ExternalOutput").ap()
        dbg_aps["d_xln"] = dram("d_xln", [BT, D], F32, kind="ExternalOutput").ap()
        dbg_aps["d_z"] = dram("d_z", [512, BT], F32, kind="ExternalOutput").ap()
    if dbg >= 2:
        dbg_aps["d_gat"] = dram("d_gat", [BT, 512], F32, kind="ExternalOutput").ap()
    if dbg >= 3:
        dbg_aps["d_x2"] = dram("d_x2", [TSH, D], F32, kind="ExternalOutput").ap()
        dbg_aps["d_rs"] = dram("d_rs", [TSH, D], F32, kind="ExternalOutput").ap()
        dbg_aps["d_y1"] = dram("d_y1", [1024, TSH], F32, kind="ExternalOutput").ap()
        dbg_aps["d_h2"] = dram("d_h2", [128, 8192], F32, kind="ExternalOutput").ap()
        dbg_aps["d_ffo"] = dram("d_ffo", [128, 16384], F32, kind="ExternalOutput").ap()
        dbg_aps["d_pp"] = dram("d_pp", [BT, 512], F32, kind="ExternalOutput").ap()

    ti = np.arange(CCH)
    maskL = np.where(ti[:, None] > ti[None, :], 0.0, NEGINF).astype(np.float32)
    maskU = maskL.T.copy()
    maskQ = np.where(ti[:, None] <= ti[None, :], -0.5 * LN128, NEGINF).astype(np.float32)
    c_maskL = nc.inline_tensor(maskL, name="c_maskL").ap()
    c_maskU = nc.inline_tensor(maskU, name="c_maskU").ap()
    c_maskQ = nc.inline_tensor(maskQ, name="c_maskQ").ap()
    c_idf = nc.inline_tensor(np.eye(128, dtype=np.float32), name="c_idf").ap()
    c_idb = nc.inline_tensor(np.eye(128, dtype=BF), name="c_idb").ap()
    c_onesb = nc.inline_tensor(np.ones((128, 1), dtype=BF), name="c_onesb").ap()
    c_onesf = nc.inline_tensor(np.ones((1, 128), dtype=np.float32), name="c_onesf").ap()
    c_sign8 = nc.inline_tensor(
        np.array([-1, -1, -1, -1, 1, 1, 1, 1], dtype=np.float32)[:, None], name="c_sign8").ap()
    c_h128 = nc.inline_tensor(_hadamard(128).astype(BF), name="c_h128").ap()
    c_h64 = nc.inline_tensor(_hadamard(64).astype(BF), name="c_h64").ap()
    mL4 = np.tile(np.where(ti[:, None] > ti[None, :], 0.0, NEGINF), (1, 4)).astype(BF)
    mQ4 = np.tile(np.where(ti[:, None] <= ti[None, :], 0.0, NEGINF), (1, 4)).astype(BF)
    c_mskL4 = nc.inline_tensor(mL4, name="c_mskL4").ap()
    c_mskQ4 = nc.inline_tensor(mQ4, name="c_mskQ4").ap()
    c_bias = nc.inline_tensor(np.tile(np.array(
        [LNEPS, EPS, 1.0, 128.0 * EPS], dtype=np.float32), (128, 1)), name="c_bias").ap()

    sb = nc.alloc_sbuf_tensor
    cst = sb("cst", [128, 4 * 128 + 16 + 128 + 4], F32)
    idbt = sb("idbt", [128, 129], BF16)
    h128t = sb("h128t", [128, 128], BF16)
    h64t = sb("h64t", [64, 64], BF16)
    St = [sb(f"St{p}", [128, 128], F32) for p in range(8)]
    St2 = [sb(f"St2_{p}", [128, 128], F32) for p in range(8)]
    Sbt = [sb(f"Sbt{p}", [128, 128], BF16) for p in range(8)]
    sclT = sb("sclT", [128, 4 * 256], BF16)
    sclF = sb("sclF", [128, 4 * 256], F32)
    gcb = sb("gcb", [128, 256], F32)
    mskL4 = sb("mskL4", [128, 512], BF16)
    mskQ4 = sb("mskQ4", [128, 512], BF16)

    MKL, MKU, MKQ = cst[:, 0:128], cst[:, 128:256], cst[:, 256:384]
    IDF = cst[:, 384:512]
    SGN8, DTA8, ALOG8 = cst[0:8, 512:513], cst[0:8, 513:514], cst[0:8, 514:515]
    NRMW = cst[0:128, 515:516]
    ONESF = cst[0:1, 528:656]
    BLN = cst[:, 656:657]
    BEP = cst[:, 657:658]
    BONE = cst[:, 658:659]
    BQ128 = cst[:, 659:660]
    IDB, ONEB = idbt[:, 0:128], idbt[:, 128:129]
    nrscT = sclF[:, 0:256]
    oscT = sclF[:, 256:512]
    dscT = sclF[:, 512:768]
    bexT = sclF[:, 768:1024]

    pool = Pool(nc, "pool", 167 * 1024)

    ps = nc.alloc_psum_tensor
    psU = ps("psU", [128, 2048], F32)
    psV = ps("psV", [128, 2048], F32)
    psBig = psU[:, 0:1024]
    psBig2 = psU[:, 1024:2048]
    psS1 = psV[:, 0:512]
    psA = psV[:, 512:1024]
    psB = psV[:, 1024:1536]

    with nc.Block() as block:
        ch = Sched(nc)
        O = ch.C

        def ln_tile(xt, outb, scr, st, eps=None, out_f32=None, res=None):
            O("vector", lambda e: e.reduce_sum(out=st[:, 0:1], in_=xt, axis=AX.X))
            O("vector", lambda e: e.scalar_tensor_tensor(
                out=scr, in0=xt, scalar=1.0, in1=xt, op0=ALU.mult, op1=ALU.mult,
                accum_out=st[:, 1:2]))
            O("vector", lambda e: e.tensor_scalar(
                out=st[:, 2:3], in0=st[:, 0:1], scalar1=-1.0 / D, scalar2=None, op0=ALU.mult))
            O("vector", lambda e: e.tensor_tensor(
                out=st[:, 3:4], in0=st[:, 2:3], in1=st[:, 2:3], op=ALU.mult))
            O("vector", lambda e: e.scalar_tensor_tensor(
                out=st[:, 4:5], in0=st[:, 1:2], scalar=1.0 / D, in1=st[:, 3:4],
                op0=ALU.mult, op1=ALU.subtract))
            ba = eps if eps is not None else BLN
            O("scalar", lambda e, ba=ba: e.activation(out=st[:, 5:6], in_=st[:, 4:5], func=AF.Ln, bias=ba))
            O("scalar", lambda e: e.activation(out=st[:, 6:7], in_=st[:, 5:6], func=AF.Exp, scale=-0.5))
            if outb is not None:
                O("vector", lambda e: e.tensor_scalar(
                    out=outb, in0=xt, scalar1=st[:, 2:3], scalar2=st[:, 6:7],
                    op0=ALU.add, op1=ALU.mult))
            if res is not None:
                O("vector", lambda e: e.tensor_scalar(
                    out=scr, in0=xt, scalar1=st[:, 2:3], scalar2=st[:, 6:7],
                    op0=ALU.add, op1=ALU.mult))
                O("vector", lambda e: e.tensor_tensor(out=out_f32, in0=scr, in1=res, op=ALU.add))

        with ch.group():
            O("gpsimd", lambda e: e.dma_start(out=w1b_sl[:], in_=w1_sl[:]), "dma")
            O("gpsimd", lambda e: e.dma_start(out=w3b_sl[:], in_=w3_sl[:]), "dma")
            O("gpsimd", lambda e: e.dma_start(out=w2b_sl[:], in_=w2_sl[:]), "dma")
            O("sync", lambda e: e.dma_start(out=MKL, in_=c_maskL[:]), "dma")
            O("sync", lambda e: e.dma_start(out=MKU, in_=c_maskU[:]), "dma")
            O("sync", lambda e: e.dma_start(out=MKQ, in_=c_maskQ[:]), "dma")
            O("sync", lambda e: e.dma_start(out=IDF, in_=c_idf[:]), "dma")
            O("sync", lambda e: e.dma_start(out=SGN8, in_=c_sign8[:]), "dma")
            O("sync", lambda e: e.dma_start(out=DTA8, in_=dta8[:]), "dma")
            O("sync", lambda e: e.dma_start(out=ALOG8, in_=alog8[:]), "dma")
            O("sync", lambda e: e.dma_start(out=NRMW, in_=normw[:]), "dma")
            O("sync", lambda e: e.dma_start(out=ONESF, in_=c_onesf[:]), "dma")
            O("sync", lambda e: e.dma_start(out=idbt[:, 0:128], in_=c_idb[:]), "dma")
            O("sync", lambda e: e.dma_start(out=idbt[:, 128:129], in_=c_onesb[:]), "dma")
            O("sync", lambda e: e.dma_start(out=h128t[:], in_=c_h128[:]), "dma")
            O("sync", lambda e: e.dma_start(out=h64t[:], in_=c_h64[:]), "dma")
            O("sync", lambda e: e.dma_start(out=cst[:, 656:660], in_=c_bias[:]), "dma")
            O("sync", lambda e: e.dma_start(out=mskL4[:], in_=c_mskL4[:]), "dma")
            O("sync", lambda e: e.dma_start(out=mskQ4[:], in_=c_mskQ4[:]), "dma")

        m0 = pool.mark()
        coef8 = cst[0:8, 516:517]
        ncoef8 = cst[0:8, 517:518]
        O("scalar", lambda e: e.activation(out=coef8, in_=ALOG8, func=AF.Exp))
        O("vector", lambda e: e.tensor_scalar(
            out=ncoef8, in0=coef8, scalar1=-1.0, scalar2=None, op0=ALU.mult))

        xt0 = pool.alloc(128, D, F32)
        scr0 = pool.alloc(128, D, F32)
        st0 = pool.alloc(128, 8, F32)
        xlnb0 = pool.alloc(128, D, BF16)
        for i in range(8):
            O("sync", lambda e, i=i: e.dma_start(out=xt0, in_=x_sh[i * 128:(i + 1) * 128, :]), "dma")
            ln_tile(xt0, xlnb0, scr0, st0)
            O("sync", lambda e, i=i: e.dma_start(out=xln_sl[i * 128:(i + 1) * 128, :], in_=xlnb0), "dma")
        pre_ag = ch.serial_last
        O("gpsimd", lambda e: e.collective_compute(
            "AllGather", ALU.bypass, replica_groups=RG, ins=[xln_sl[:]], outs=[ag_xln[:]]), "cc")
        ch.serial_last = pre_ag

        w2tt = pool.alloc(128, D, BF16)
        for i in range(8):
            O("sync", lambda e, i=i: e.dma_start(
                out=w2tt, in_=w2b_sl[:, i * 128:(i + 1) * 128], transpose=True), "dma")
            O("sync", lambda e, i=i: e.dma_start(
                out=w2t_sl[i * 128:(i + 1) * 128, :], in_=w2tt), "dma")
        for k in range(16):
            O("sync", lambda e, k=k: e.dma_start(
                out=w2tt[:, 0:1024], in_=w1b_sl[:, k * 128:(k + 1) * 128], transpose=True), "dma")
            O("sync", lambda e, k=k: e.dma_start(
                out=w1t_sl[k * 128:(k + 1) * 128, :], in_=w2tt[:, 0:1024]), "dma")
            O("sync", lambda e, k=k: e.dma_start(
                out=w2tt[:, 1024:2048], in_=w3b_sl[:, k * 128:(k + 1) * 128], transpose=True), "dma")
            O("sync", lambda e, k=k: e.dma_start(
                out=w3t_sl[k * 128:(k + 1) * 128, :], in_=w2tt[:, 1024:2048]), "dma")

        wot_f = pool.alloc(128, D, F32)
        wot_b = pool.alloc(128, D, BF16)
        for i in range(4):
            O("sync", lambda e, i=i: e.dma_start(out=wot_f, in_=wo_sl[i * 128:(i + 1) * 128, :]), "dma")
            O("vector", lambda e: e.tensor_scalar(
                out=wot_b, in0=wot_f, scalar1=NRMW, scalar2=None, op0=ALU.mult))
            O("sync", lambda e, i=i: e.dma_start(out=wob_sl[i * 128:(i + 1) * 128, :], in_=wot_b), "dma")

        pre_ag = ch.serial_last
        O("gpsimd", lambda e: e.collective_compute(
            "AllGather", ALU.bypass, replica_groups=RG, ins=[w1t_sl[:]], outs=[ag_w1t[:]]), "cc")
        O("gpsimd", lambda e: e.collective_compute(
            "AllGather", ALU.bypass, replica_groups=RG, ins=[w3t_sl[:]], outs=[ag_w3t[:]]), "cc")
        O("gpsimd", lambda e: e.collective_compute(
            "AllGather", ALU.bypass, replica_groups=RG, ins=[w2t_sl[:]], outs=[ag_w2t[:]]), "cc")
        ch.serial_last = ch.all_tokens()
        pool.reset(m0)

        batile = pool.alloc(8, BT, F32)
        m1b = pool.mark()
        wqb = pool.alloc(128, 16 * 1024, BF16)
        wzb = pool.alloc(128, 16 * 512, BF16)
        wbab = pool.alloc(128, 16 * 8, BF16)
        wst = pool.alloc(128, 1024, F32)
        cwt = pool.alloc(128, 32, F32)
        for k in range(16):
            O("sync", lambda e, k=k: e.dma_start(out=wst, in_=wq_sl[k * 128:(k + 1) * 128, :]), "dma")
            O("scalar", lambda e, k=k: e.copy(out=wqb[:, k * 1024:(k + 1) * 1024], in_=wst))
            O("sync", lambda e, k=k: e.dma_start(out=wst[:, 0:512], in_=wz_sl[k * 128:(k + 1) * 128, :]), "dma")
            O("scalar", lambda e, k=k: e.copy(out=wzb[:, k * 512:(k + 1) * 512], in_=wst[:, 0:512]))
            O("sync", lambda e, k=k: e.dma_start(out=wst[:, 0:8], in_=wba_sl[k * 128:(k + 1) * 128, :]), "dma")
            O("scalar", lambda e, k=k: e.copy(out=wbab[:, k * 8:(k + 1) * 8], in_=wst[:, 0:8]))
        for m in range(8):
            O("sync", lambda e, m=m: e.dma_start(
                out=cwt[:, m * 4:(m + 1) * 4], in_=cwT_sl[m * 128:(m + 1) * 128, :]), "dma")

        xlt = pool.alloc(128, 16 * 1024, BF16)
        mc = pool.alloc(128, 1027, F32)
        acc0 = pool.alloc(128, 1024, F32)
        acc1 = pool.alloc(128, 1024, F32)
        tails = pool.alloc(128, 24, F32)
        sqt = pool.alloc(128, 1024, BF16)
        mtmp = pool.alloc(128, 1024, BF16)
        lrrow = pool.alloc(1, 1024, F32)

        def lhsT_ph1(m, k):
            if m < 8:
                return wqb[:, k * 1024 + m * 128: k * 1024 + (m + 1) * 128]
            if m < 12:
                return wzb[:, k * 512 + (m - 8) * 128: k * 512 + (m - 7) * 128]
            return wbab[:, k * 8:(k + 1) * 8]

        for tc in range(8):
            tsl = slice(tc * 1024, (tc + 1) * 1024)
            with ch.group():
                for k in range(16):
                    O("sync", lambda e, k=k, tc=tc: e.dma_start(
                        out=xlt[:, k * 1024:(k + 1) * 1024],
                        in_=ag_xln[tc * 1024:(tc + 1) * 1024, k * 128:(k + 1) * 128],
                        transpose=True), "dma")
            for m in range(13):
                pst = psBig if (m % 2 == 0) else psBig2
                pm = pst[0:128, :] if m < 12 else pst[0:8, :]
                for k in range(16):
                    for h in range(2):
                        O("tensor", lambda e, m=m, k=k, h=h, pm=pm: e.matmul(
                            pm[:, h * 512:(h + 1) * 512], lhsT=lhsT_ph1(m, k),
                            rhs=xlt[:, k * 1024 + h * 512: k * 1024 + (h + 1) * 512],
                            start=(k == 0), stop=(k == 15)))
                if m < 8:
                    O("scalar", lambda e, pm=pm: e.copy(out=mc[:, 3:1027], in_=pm))
                    if tc in (0, 4):
                        O("vector", lambda e: e.memset(mc[:, 0:3], 0.0))
                    else:
                        O("scalar", lambda e, m=m: e.copy(out=mc[:, 0:3], in_=tails[:, m * 3:(m + 1) * 3]))
                    O("scalar", lambda e, m=m: e.copy(out=tails[:, m * 3:(m + 1) * 3], in_=mc[:, 1024:1027]))
                    O("vector", lambda e, m=m: e.tensor_scalar(
                        out=acc0, in0=mc[:, 3:1027], scalar1=cwt[:, m * 4 + 3:m * 4 + 4],
                        scalar2=None, op0=ALU.mult))
                    O("vector", lambda e, m=m: e.scalar_tensor_tensor(
                        out=acc1, in0=mc[:, 2:1026], scalar=cwt[:, m * 4 + 2:m * 4 + 3],
                        in1=acc0, op0=ALU.mult, op1=ALU.add))
                    O("vector", lambda e, m=m: e.scalar_tensor_tensor(
                        out=acc0, in0=mc[:, 1:1025], scalar=cwt[:, m * 4 + 1:m * 4 + 2],
                        in1=acc1, op0=ALU.mult, op1=ALU.add))
                    O("vector", lambda e, m=m: e.scalar_tensor_tensor(
                        out=acc1, in0=mc[:, 0:1024], scalar=cwt[:, m * 4 + 0:m * 4 + 1],
                        in1=acc0, op0=ALU.mult, op1=ALU.add))
                    O("scalar", lambda e: e.activation(out=mtmp, in_=acc1, func=AF.Silu))
                    if m < 2:
                        O("sync", lambda e, m=m, tsl=tsl: e.dma_start(
                            out=qb_hbm[m * 128:(m + 1) * 128, tsl], in_=mtmp), "dma")
                    elif m < 4:
                        O("sync", lambda e, m=m, tsl=tsl: e.dma_start(
                            out=kb_hbm[(m - 2) * 128:(m - 1) * 128, tsl], in_=mtmp), "dma")
                    else:
                        O("sync", lambda e, m=m, tsl=tsl: e.dma_start(
                            out=vb_hbm[(m - 4) * 128:(m - 3) * 128, tsl], in_=mtmp), "dma")
                    if m < 4:
                        O("scalar", lambda e: e.activation(out=sqt, in_=mtmp, func=AF.Square))
                        O("tensor", lambda e: e.matmul(
                            psS1[0:1, 0:512], lhsT=ONEB, rhs=sqt[:, 0:512], start=True, stop=True))
                        O("tensor", lambda e: e.matmul(
                            psA[0:1, 0:512], lhsT=ONEB, rhs=sqt[:, 512:1024], start=True, stop=True))
                        O("scalar", lambda e: e.copy(
                            out=lrrow[:, 0:512], in_=psS1[0:1, 0:512]))
                        O("scalar", lambda e: e.copy(
                            out=lrrow[:, 512:1024], in_=psA[0:1, 0:512]))
                        O("sync", lambda e, m=m, tsl=tsl: e.dma_start(
                            out=lr_hbm[m:m + 1, tsl], in_=lrrow), "dma")
                        lay = lrqlay if m < 2 else lrklay
                        hh = m % 2
                        bb, c0 = tc // 4, (tc % 4) * 8
                        for r in range(2):
                            base_ch = c0 * 8 + (2 * hh + r) * 2 + bb
                            O("sync", lambda e, lay=lay, base_ch=base_ch: e.dma_start(
                                out=lay[base_ch:base_ch + 57:8, :],
                                in_=lrrow.rearrange("o (c p) -> o c p", p=128)), "dma")
                    if dbg >= 1:
                        O("gpsimd", lambda e, m=m, tsl=tsl: e.dma_start(
                            out=dbg_aps["d_mix"][m * 128:(m + 1) * 128, tsl], in_=mtmp), "dma")
                elif m < 12:
                    O("scalar", lambda e, pm=pm: e.activation(out=mtmp, in_=pm, func=AF.Silu))
                    O("sync", lambda e, m=m, tsl=tsl: e.dma_start(
                        out=zb_hbm[(m - 8) * 128:(m - 7) * 128, tsl], in_=mtmp), "dma")
                    if dbg >= 1:
                        O("gpsimd", lambda e, m=m, tsl=tsl: e.dma_start(
                            out=dbg_aps["d_z"][(m - 8) * 128:(m - 7) * 128, tsl], in_=mtmp), "dma")
                else:
                    O("scalar", lambda e, pm=pm, tc=tc: e.copy(
                        out=batile[:, tc * 1024:(tc + 1) * 1024], in_=pm))

        pool.reset(m1b)
        spA = pool.alloc(8, BT, F32)
        spB = pool.alloc(8, BT, F32)
        lngt = pool.alloc(8, BT, F32)
        O("vector", lambda e: e.tensor_scalar(
            out=spA, in0=batile, scalar1=SGN8, scalar2=DTA8, op0=ALU.mult, op1=ALU.add))
        O("scalar", lambda e: e.activation(out=spB, in_=spA, func=AF.Abs))
        O("scalar", lambda e: e.activation(out=spB, in_=spB, func=AF.Exp, scale=-1.0))
        O("scalar", lambda e: e.activation(out=spB, in_=spB, func=AF.Ln, bias=BONE[0:8, :]))
        O("vector", lambda e: e.tensor_scalar(
            out=spA, in0=spA, scalar1=0.0, scalar2=None, op0=ALU.max))
        O("vector", lambda e: e.tensor_tensor(out=spA, in0=spA, in1=spB, op=ALU.add))
        O("vector", lambda e: e.tensor_scalar(
            out=lngt, in0=spA, scalar1=ncoef8, scalar2=None, op0=ALU.mult))
        O("sync", lambda e: e.dma_start(out=lng_hbm[:], in_=lngt), "dma")
        O("sync", lambda e: e.dma_start(
            out=glay.rearrange("(c h b) p -> h b c p", h=4, b=2),
            in_=lngt[4:8, :].rearrange("h (b c p) -> h b c p", b=2, c=32)), "dma")
        O("sync", lambda e: e.dma_start(
            out=blay.rearrange("(c h b) p -> h b c p", h=4, b=2),
            in_=lngt[0:4, :].rearrange("h (b c p) -> h b c p", b=2, c=32)), "dma")
        if dbg >= 1:
            O("sync", lambda e: e.dma_start(out=dbg_aps["d_lng"][:], in_=lngt), "dma")
            O("sync", lambda e: e.dma_start(out=dbg_aps["d_lr"][:], in_=lr_hbm[:]), "dma")

        cgt = [pool.alloc(128, 128, F32) for _ in range(2)]
        cgt2 = [pool.alloc(128, 128, F32) for _ in range(2)]
        lnbt = [pool.alloc(128, 128, F32) for _ in range(2)]
        lrkt = [pool.alloc(128, 128, F32) for _ in range(2)]
        lrqt = [pool.alloc(128, 128, F32) for _ in range(2)]
        for t2 in range(2):
            cs = slice(t2 * 128, (t2 + 1) * 128)
            with ch.group():
                O("sync", lambda e, t2=t2, cs=cs: e.dma_start(out=cgt[t2][:, :], in_=glay[cs]), "dma")
                O("sync", lambda e, t2=t2, cs=cs: e.dma_start(out=lnbt[t2][:, :], in_=blay[cs]), "dma")
                O("sync", lambda e, t2=t2, cs=cs: e.dma_start(out=lrkt[t2][:, :], in_=lrklay[cs]), "dma")
                O("sync", lambda e, t2=t2, cs=cs: e.dma_start(out=lrqt[t2][:, :], in_=lrqlay[cs]), "dma")
        for t2 in range(2):
            O("scalar", lambda e, t2=t2: e.activation(
                out=lrkt[t2], in_=lrkt[t2], func=AF.Ln, bias=BEP))
            O("scalar", lambda e, t2=t2: e.activation(
                out=lrqt[t2], in_=lrqt[t2], func=AF.Ln, scale=128.0, bias=BQ128))
        for t2 in range(2):
            src, dst = cgt[t2], cgt2[t2]
            s = 1
            while s < 128:
                O("vector", lambda e, src=src, dst=dst, s=s: e.tensor_tensor(
                    out=dst[:, s:128], in0=src[:, s:128], in1=src[:, 0:128 - s], op=ALU.add))
                O("vector", lambda e, src=src, dst=dst, s=s: e.tensor_copy(
                    out=dst[:, 0:s], in_=src[:, 0:s]))
                src, dst = dst, src
                s *= 2
            if src is not cgt[t2]:
                O("vector", lambda e, src=src, t2=t2: e.tensor_copy(out=cgt[t2], in_=src))

        av = [pool.alloc(128, 128, F32) for _ in range(2)]
        bv = [pool.alloc(128, 128, F32) for _ in range(2)]
        qv = [pool.alloc(128, 128, F32) for _ in range(2)]
        ncg = pool.alloc(128, 128, F32)
        sct = pool.alloc(128, 128, F32)
        scb = pool.alloc(128, 128, BF16)
        for t2 in range(2):
            O("vector", lambda e, t2=t2: e.scalar_tensor_tensor(
                out=av[t2], in0=lrkt[t2], scalar=-0.5, in1=cgt[t2], op0=ALU.mult, op1=ALU.add))
            O("vector", lambda e, t2=t2: e.tensor_tensor(
                out=av[t2], in0=av[t2], in1=lnbt[t2], op=ALU.add))
            O("vector", lambda e, t2=t2: e.tensor_scalar(
                out=ncg, in0=cgt[t2], scalar1=-1.0, scalar2=None, op0=ALU.mult))
            O("vector", lambda e, t2=t2: e.scalar_tensor_tensor(
                out=bv[t2], in0=lrkt[t2], scalar=-0.5, in1=ncg, op0=ALU.mult, op1=ALU.add))
            O("vector", lambda e, t2=t2: e.scalar_tensor_tensor(
                out=qv[t2], in0=lrqt[t2], scalar=-0.5, in1=cgt[t2], op0=ALU.mult, op1=ALU.add))
            O("scalar", lambda e, t2=t2: e.activation(out=sct, in_=av[t2], func=AF.Exp))
            O("vector", lambda e: e.tensor_scalar(
                out=scb, in0=sct, scalar1=-1.0, scalar2=None, op0=ALU.mult))
            O("sync", lambda e, t2=t2: e.dma_start(
                out=scl_hbm[0, t2 * 128:(t2 + 1) * 128, :], in_=scb), "dma")
            O("scalar", lambda e, t2=t2: e.activation(
                out=scb, in_=qv[t2], func=AF.Exp))
            O("sync", lambda e, t2=t2: e.dma_start(
                out=scl_hbm[1, t2 * 128:(t2 + 1) * 128, :], in_=scb), "dma")
            O("vector", lambda e, t2=t2: e.tensor_scalar(
                out=sct, in0=bv[t2], scalar1=cgt[t2][:, 127:128], scalar2=None, op0=ALU.add))
            O("scalar", lambda e: e.activation(out=scb, in_=sct, func=AF.Exp))
            O("sync", lambda e, t2=t2: e.dma_start(
                out=scl_hbm[2, t2 * 128:(t2 + 1) * 128, :], in_=scb), "dma")
            O("scalar", lambda e, t2=t2: e.activation(out=scb, in_=lnbt[t2], func=AF.Exp))
            O("sync", lambda e, t2=t2: e.dma_start(
                out=scl_hbm[3, t2 * 128:(t2 + 1) * 128, :], in_=scb), "dma")
            O("sync", lambda e, t2=t2: e.dma_start(
                out=evec_hbm[0, t2 * 128:(t2 + 1) * 128, :], in_=av[t2]), "dma")
            O("sync", lambda e, t2=t2: e.dma_start(
                out=evec_hbm[1, t2 * 128:(t2 + 1) * 128, :], in_=bv[t2]), "dma")
            O("sync", lambda e, t2=t2: e.dma_start(
                out=evec_hbm[2, t2 * 128:(t2 + 1) * 128, :], in_=qv[t2]), "dma")
            O("sync", lambda e, t2=t2: e.dma_start(
                out=gcx_hbm[t2 * 128:(t2 + 1) * 128, :], in_=cgt[t2][:, 127:128]), "dma")
        for i in range(4):
            O("sync", lambda e, i=i: e.dma_start(
                out=sclT[:, i * 256:(i + 1) * 256], in_=scl_hbm[i], transpose=True), "dma")
            O("scalar", lambda e, i=i: e.copy(
                out=sclF[:, i * 256:(i + 1) * 256], in_=sclT[:, i * 256:(i + 1) * 256]))
        gcrow = pool.alloc(1, 256, F32)
        O("sync", lambda e: e.dma_start(
            out=gcrow, in_=gcx_hbm.rearrange("p o -> o p")), "dma")
        O("scalar", lambda e: e.activation(out=gcrow, in_=gcrow, func=AF.Exp))
        O("tensor", lambda e: e.matmul(psS1[:, 0:256], lhsT=ONESF, rhs=gcrow, start=True, stop=True))
        O("scalar", lambda e: e.copy(out=gcb[:], in_=psS1[:, 0:256]))
        pool.reset(0)

        for p in range(8):
            tk = O("vector", lambda e, p=p: e.memset(St[p][:], 0.0))
            tk2 = O("vector", lambda e, p=p: e.memset(Sbt[p][:], 0.0))

        m2 = pool.mark()
        rows = pool.alloc(2, 8 * 1024, F32)
        O("vector", lambda e: e.memset(rows[:], 1.0))
        KFM = [pool.alloc(128, 512, BF16) for _ in range(2)]
        QFM = [pool.alloc(128, 512, BF16) for _ in range(2)]
        KTM = [pool.alloc(128, 512, BF16) for _ in range(2)]
        VTM = [pool.alloc(128, 1024, BF16) for _ in range(2)]
        ZTM = [pool.alloc(128, 1024, BF16) for _ in range(2)]
        Zpg = [[pool.alloc(128, 1024, BF16) for _ in range(7)] for _ in range(2)]
        Ppg = [[pool.alloc(128, 1024, BF16) for _ in range(6)] for _ in range(2)]
        ELg = pool.alloc(128, 1024, BF16)
        EQg = pool.alloc(128, 1024, BF16)
        nKKg = pool.alloc(128, 1024, BF16)
        Ptlg = [pool.alloc(128, 1024, BF16) for _ in range(2)]
        Xg = [pool.alloc(128, 1024, BF16) for _ in range(2)]
        bVtT = pool.alloc(128, 1024, BF16)
        qsbg = pool.alloc(128, 1024, BF16)
        Dbarg = pool.alloc(128, 1024, BF16)
        ObufT = pool.alloc(128, 1024, BF16)
        ggtT = pool.alloc(128, 1024, BF16)
        sqgT = pool.alloc(128, 1024, BF16)
        gstT = pool.alloc(128, 24, F32)

        bA = [psU[:, 0:512], psU[:, 512:1024]]
        bEL = psU[:, 1024:1536]
        bEQ = psU[:, 1536:2048]
        bZ = psV[:, 0:512]
        bP = psV[:, 512:1024]
        bX = [psV[:, 1024:1536], psV[:, 1536:2048]]

        class Res:
            __slots__ = ("w", "readers")

            def __init__(s):
                s.w = []
                s.readers = []

            def wdeps(s):
                return list(s.readers) + list(s.w)

            def wrote(s, *toks):
                s.w = list(toks)
                s.readers = []

            def read(s, tok):
                s.readers.append(tok)

            def rdeps(s):
                return list(s.w)

        RS = {}

        def res(name):
            if name not in RS:
                RS[name] = Res()
            return RS[name]

        ph2_entry = ch.all_tokens()
        for p in range(8):
            res(f"Sbt{p}").wrote(*ph2_entry)
            res(f"St0_{p}").wrote(*ph2_entry)

        def xop(eng, fn, deps=()):
            return ch.op(eng, fn, deps, nosame=True)

        def xdma(q, fn, deps=()):
            return ch.dma(q, fn, deps, nosame=True)

        def dep0(c):
            return ph2_entry if c <= 1 else ()

        def issue_loads(c):
            par = c % 2
            r = res(f"rows{par}")
            d = list(r.wdeps()) + list(dep0(c))
            toks = []
            for kind, src in ((0, 0), (1, 1), (2, 1), (3, 2)):
                row = 1 if kind < 2 else 0
                toks.append(xdma("gpsimd", lambda e, kind=kind, src=src, row=row, par=par, c=c: e.dma_start(
                    out=rows[row:row + 1, (par * 4 + kind) * 1024:(par * 4 + kind + 1) * 1024],
                    in_=evec_hbm[src, c * 8:(c + 1) * 8, :].rearrange("c p -> () (c p)")),
                    deps=d))
                d = []
            r.wrote(*toks)
            for name, tile in (("KFM", KFM), ("QFM", QFM), ("KTM", KTM), ("VTM", VTM), ("ZTM", ZTM)):
                rr = res(f"{name}{par}")
                d = list(rr.wdeps()) + list(dep0(c))
                toks = []
                for b2 in range(2):
                    off = b2 * T + c * 128
                    if name == "KTM":
                        toks.append(xdma("sync", lambda e, b2=b2, off=off, tile=tile, par=par: e.dma_start(
                            out=tile[par][:, b2 * 256:(b2 + 1) * 256],
                            in_=kb_hbm[:, off:off + 128], transpose=True), deps=d))
                    elif name == "VTM":
                        toks.append(xdma("sync", lambda e, b2=b2, off=off, tile=tile, par=par: e.dma_start(
                            out=tile[par][:, b2 * 512:(b2 + 1) * 512],
                            in_=vb_hbm[:, off:off + 128], transpose=True), deps=d))
                    elif name == "ZTM":
                        toks.append(xdma("sync", lambda e, b2=b2, off=off, tile=tile, par=par: e.dma_start(
                            out=tile[par][:, b2 * 512:(b2 + 1) * 512],
                            in_=zb_hbm[:, off:off + 128], transpose=True), deps=d))
                    else:
                        hb = kb_hbm if name == "KFM" else qb_hbm
                        for hh in range(2):
                            toks.append(xdma("sync", lambda e, b2=b2, hh=hh, off=off, tile=tile, par=par, hb=hb: e.dma_start(
                                out=tile[par][:, (b2 * 2 + hh) * 128:(b2 * 2 + hh + 1) * 128],
                                in_=hb[hh * 128:(hh + 1) * 128, off:off + 128]), deps=d))
                    d = []
                rr.wrote(*toks)

        def pidx(p):
            b2, vh = p // 4, p % 4
            return b2, vh, vh // 2, vh * 2 + b2

        def emit_A1(c):
            par = c % 2
            rowres = res(f"rows{par}")
            rowtok = rowres.rdeps()
            for bank, msk, rkL, rkR, outg, bname, gname in (
                    (bEL, mskL4[:], 0, 2, ELg, "bEL", "ELg"),
                    (bEQ, mskQ4[:], 1, 3, EQg, "bEQ", "EQg")):
                B = res(bname)
                for hgi in range(2):
                    d = B.wdeps() + rowtok + list(dep0(c))
                    t = xop("tensor", lambda e, bank=bank, msk=msk: e.matmul(
                        bank, lhsT=IDB, rhs=msk, start=True, stop=True), deps=d)
                    for vh in range(4):
                        p = hgi * 4 + vh
                        b2, _, _, pp = pidx(p)
                        lsl = (par * 4 + rkL) * 1024 + pp * 128
                        rsl = (par * 4 + rkR) * 1024 + pp * 128
                        t = xop("tensor", lambda e, bank=bank, vh=vh, lsl=lsl, rsl=rsl: e.matmul(
                            bank[:, vh * 128:(vh + 1) * 128],
                            lhsT=rows[0:2, lsl:lsl + 128], rhs=rows[0:2, rsl:rsl + 128],
                            start=False, stop=True))
                        rowres.read(t)
                    B.wrote(t)
                    rg = res(f"{gname}_h{hgi}")
                    ev = xop("scalar", lambda e, bank=bank, outg=outg, hgi=hgi: e.activation(
                        out=outg[:, hgi * 512:(hgi + 1) * 512], in_=bank, func=AF.Exp),
                        deps=B.rdeps() + rg.wdeps())
                    B.read(ev)
                    rg.wrote(ev)
            nktoks = []
            ptltoks = []
            for u in range(4):
                B = res(f"bA{u % 2}")
                bank = bA[u % 2]
                pA = 2 * u
                hgi = u // 2
                d = B.wdeps() + list(dep0(c))
                t = None
                for i2 in range(2):
                    p = pA + i2
                    b2, vh, kh, pp = pidx(p)
                    ks = (b2 * 2 + kh) * 128
                    t = xop("tensor", lambda e, bank=bank, i2=i2, ks=ks, par=par: e.matmul(
                        bank[:, i2 * 128:(i2 + 1) * 128],
                        lhsT=KFM[par][:, ks:ks + 128], rhs=KFM[par][:, ks:ks + 128],
                        start=True, stop=True), deps=d)
                    d = []
                    res(f"KFM{par}").read(t)
                    t = xop("tensor", lambda e, bank=bank, i2=i2, ks=ks, par=par: e.matmul(
                        bank[:, 256 + i2 * 128:256 + (i2 + 1) * 128],
                        lhsT=KFM[par][:, ks:ks + 128], rhs=QFM[par][:, ks:ks + 128],
                        start=True, stop=True))
                    res(f"KFM{par}").read(t)
                    res(f"QFM{par}").read(t)
                B.wrote(t)
                ev1 = xop("vector", lambda e, bank=bank, pA=pA: e.tensor_scalar(
                    out=nKKg[:, pA * 128:(pA + 2) * 128], in0=bank[:, 0:256],
                    scalar1=-1.0, scalar2=None, op0=ALU.mult),
                    deps=B.rdeps() + res("nKK").wdeps())
                ev2 = xop("vector", lambda e, bank=bank, pA=pA, par=par: e.tensor_tensor(
                    out=Ptlg[par][:, pA * 128:(pA + 2) * 128], in0=bank[:, 256:512],
                    in1=EQg[:, pA * 128:(pA + 2) * 128], op=ALU.mult),
                    deps=B.rdeps() + res(f"EQg_h{hgi}").rdeps() + res(f"Ptl{par}").wdeps())
                res(f"EQg_h{hgi}").read(ev2)
                B.read(ev1)
                B.read(ev2)
                nktoks.append(ev1)
                ptltoks.append(ev2)
            res("nKK").wrote(*nktoks)
            res(f"Ptl{par}").wrote(*ptltoks)
            for hgi in range(2):
                sl = slice(hgi * 512, (hgi + 1) * 512)
                t = xop("vector", lambda e, sl=sl, par=par: e.tensor_tensor(
                    out=Ppg[par][0][:, sl], in0=nKKg[:, sl], in1=ELg[:, sl], op=ALU.mult),
                    deps=res("nKK").rdeps() + res(f"ELg_h{hgi}").rdeps() + res(f"P{par}0h{hgi}").wdeps())
                res(f"ELg_h{hgi}").read(t)
                res("nKK").read(t)
                res(f"P{par}0h{hgi}").wrote(t)
            B = res("bEL")
            for hgi in range(2):
                d = B.wdeps() + res(f"P{par}0h{hgi}").rdeps()
                for vh in range(4):
                    p = hgi * 4 + vh
                    t = xop("tensor", lambda e, p=p, vh=vh, par=par: e.matmul(
                        bEL[:, vh * 128:(vh + 1) * 128],
                        lhsT=Ppg[par][0][:, p * 128:(p + 1) * 128], rhs=IDB,
                        start=True, stop=True), deps=d)
                    d = []
                    res(f"P{par}0h{hgi}").read(t)
                B.wrote(t)
                ev = xop("vector", lambda e, hgi=hgi, par=par: e.tensor_copy(
                    out=Zpg[par][0][:, hgi * 512:(hgi + 1) * 512], in_=bEL),
                    deps=B.rdeps() + res(f"Z{par}0h{hgi}").wdeps())
                B.read(ev)
                res(f"Z{par}0h{hgi}").wrote(ev)

        def emit_B_level(c, j):
            par = c % 2
            for hgi in range(2):
                dz = res("bZ").wdeps() + res(f"Z{par}{j - 1}h{hgi}").rdeps() + res(f"P{par}{j - 1}h{hgi}").rdeps()
                t = None
                for vh in range(4):
                    p = hgi * 4 + vh
                    sl = slice(p * 128, (p + 1) * 128)
                    t = xop("tensor", lambda e, vh=vh, sl=sl, j=j, par=par: e.matmul(
                        bZ[:, vh * 128:(vh + 1) * 128],
                        lhsT=Ppg[par][j - 1][:, sl], rhs=Zpg[par][j - 1][:, sl],
                        start=True, stop=True), deps=dz)
                    dz = ()
                    res(f"Z{par}{j - 1}h{hgi}").read(t)
                    res(f"P{par}{j - 1}h{hgi}").read(t)
                res("bZ").wrote(t)
                ev = xop("scalar", lambda e, hgi=hgi, j=j, par=par: e.copy(
                    out=Zpg[par][j][:, hgi * 512:(hgi + 1) * 512], in_=bZ),
                    deps=res("bZ").rdeps() + res(f"Z{par}{j}h{hgi}").wdeps())
                res("bZ").read(ev)
                res(f"Z{par}{j}h{hgi}").wrote(ev)
                if j < 6:
                    dp = res("bP").wdeps()
                    t = None
                    for vh in range(4):
                        p = hgi * 4 + vh
                        sl = slice(p * 128, (p + 1) * 128)
                        t = xop("tensor", lambda e, vh=vh, sl=sl, j=j, par=par: e.matmul(
                            bP[:, vh * 128:(vh + 1) * 128],
                            lhsT=Zpg[par][j - 1][:, sl], rhs=Ppg[par][j - 1][:, sl],
                            start=True, stop=True), deps=dp)
                        dp = ()
                        res(f"Z{par}{j - 1}h{hgi}").read(t)
                        res(f"P{par}{j - 1}h{hgi}").read(t)
                    res("bP").wrote(t)
                    ev = xop("vector", lambda e, hgi=hgi, j=j, par=par: e.tensor_copy(
                        out=Ppg[par][j][:, hgi * 512:(hgi + 1) * 512], in_=bP),
                        deps=res("bP").rdeps() + res(f"P{par}{j}h{hgi}").wdeps())
                    res("bP").read(ev)
                    res(f"P{par}{j}h{hgi}").wrote(ev)

        def emit_A2(c):
            par = c % 2
            rb = res("bVt")
            dW = rb.wdeps()
            bvt_toks = []
            for p in range(8):
                b2, vh, kh, pp = pidx(p)
                cc = c * 8 + pp
                t = xop("gpsimd", lambda e, p=p, cc=cc, par=par: e.tensor_scalar(
                    out=bVtT[:, p * 128:(p + 1) * 128], in0=VTM[par][:, p * 128:(p + 1) * 128],
                    scalar1=bexT[:, cc:cc + 1], scalar2=None, op0=ALU.mult),
                    deps=res(f"VTM{par}").rdeps() + dW)
                dW = []
                res(f"VTM{par}").read(t)
                bvt_toks.append(t)
            rb.wrote(*bvt_toks)
            x0toks = [[], []]
            qstoks = []
            for u in range(4):
                B = res(f"bA{u % 2}")
                bank = bA[u % 2]
                pA = 2 * u
                d = B.wdeps() + list(dep0(c))
                t = None
                for i2 in range(2):
                    p = pA + i2
                    b2, vh, kh, pp = pidx(p)
                    ks = (b2 * 2 + kh) * 128
                    t = xop("tensor", lambda e, bank=bank, i2=i2, ks=ks, p=p, par=par: e.matmul(
                        bank[:, i2 * 128:(i2 + 1) * 128],
                        lhsT=KFM[par][:, ks:ks + 128], rhs=Sbt[p][:],
                        start=True, stop=True), deps=d + res(f"Sbt{p}").rdeps())
                    d = []
                    res(f"KFM{par}").read(t)
                    res(f"Sbt{p}").read(t)
                    t = xop("tensor", lambda e, bank=bank, i2=i2, ks=ks, p=p, par=par: e.matmul(
                        bank[:, 256 + i2 * 128:256 + (i2 + 1) * 128],
                        lhsT=QFM[par][:, ks:ks + 128], rhs=Sbt[p][:],
                        start=True, stop=True))
                    res(f"QFM{par}").read(t)
                    res(f"Sbt{p}").read(t)
                B.wrote(t)
                for i2 in range(2):
                    p = pA + i2
                    b2, vh, kh, pp = pidx(p)
                    cc = c * 8 + pp
                    hgi = p // 4
                    ev = xop("vector", lambda e, bank=bank, i2=i2, p=p, cc=cc: e.scalar_tensor_tensor(
                        out=Xg[0][:, p * 128:(p + 1) * 128], in0=bank[:, i2 * 128:(i2 + 1) * 128],
                        scalar=nrscT[:, cc:cc + 1], in1=bVtT[:, p * 128:(p + 1) * 128],
                        op0=ALU.mult, op1=ALU.add),
                        deps=B.rdeps() + [bvt_toks[p]] + res(f"Xg0h{hgi}").wdeps())
                    rb.read(ev)
                    B.read(ev)
                    x0toks[hgi].append(ev)
                    ev2 = xop("vector", lambda e, bank=bank, i2=i2, p=p, cc=cc: e.tensor_scalar(
                        out=qsbg[:, p * 128:(p + 1) * 128], in0=bank[:, 256 + i2 * 128:256 + (i2 + 1) * 128],
                        scalar1=oscT[:, cc:cc + 1], scalar2=None, op0=ALU.mult),
                        deps=B.rdeps() + res("qsb").wdeps())
                    B.read(ev2)
                    qstoks.append(ev2)
            res("Xg0h0").wrote(*x0toks[0])
            res("Xg0h1").wrote(*x0toks[1])
            res("qsb").wrote(*qstoks)

        def emit_C_level(c, j):
            par = c % 2
            xin_h = [res(f"Xg{j % 2}h0"), res(f"Xg{j % 2}h1")]
            for hgi in range(2):
                B = res(f"bX{hgi}")
                bank = bX[hgi]
                d = B.wdeps() + res(f"Z{par}{j}h{hgi}").rdeps() + xin_h[hgi].rdeps()
                t = None
                for vh in range(4):
                    p = hgi * 4 + vh
                    sl = slice(p * 128, (p + 1) * 128)
                    t = xop("tensor", lambda e, vh=vh, sl=sl, j=j, par=par, bank=bank: e.matmul(
                        bank[:, vh * 128:(vh + 1) * 128],
                        lhsT=Zpg[par][j][:, sl], rhs=Xg[j % 2][:, sl],
                        start=True, stop=False), deps=d)
                    d = []
                    t = xop("tensor", lambda e, vh=vh, sl=sl, j=j, bank=bank: e.matmul(
                        bank[:, vh * 128:(vh + 1) * 128],
                        lhsT=IDB, rhs=Xg[j % 2][:, sl],
                        start=False, stop=True))
                    res(f"Z{par}{j}h{hgi}").read(t)
                    xin_h[hgi].read(t)
                B.wrote(t)
                ev = xop("scalar", lambda e, hgi=hgi, j=j, bank=bank: e.copy(
                    out=Xg[(j + 1) % 2][:, hgi * 512:(hgi + 1) * 512], in_=bank),
                    deps=B.rdeps() + res(f"Xg{(j + 1) % 2}h{hgi}").wdeps())
                B.read(ev)
                res(f"Xg{(j + 1) % 2}h{hgi}").wrote(ev)

        def emit_D(c, out_toks):
            par = c % 2
            nxt = (c + 1) % 2
            cur = c % 2
            for p in range(8):
                b2, vh, kh, pp = pidx(p)
                cc = c * 8 + pp
                hgi = p // 4
                t = xop("gpsimd", lambda e, p=p, cc=cc: e.tensor_scalar(
                    out=Dbarg[:, p * 128:(p + 1) * 128], in0=Xg[1][:, p * 128:(p + 1) * 128],
                    scalar1=dscT[:, cc:cc + 1], scalar2=None, op0=ALU.mult),
                    deps=res(f"Xg1h{hgi}").rdeps() + res(f"Db{p}").wdeps())
                res(f"Xg1h{hgi}").read(t)
                res(f"Db{p}").wrote(t)
            for hgi in range(2):
                B = res(f"bX{hgi}")
                bank = bX[hgi]
                d = B.wdeps() + res(f"Ptl{par}").rdeps() + res(f"Xg1h{hgi}").rdeps()
                t = None
                for vh in range(4):
                    p = hgi * 4 + vh
                    sl = slice(p * 128, (p + 1) * 128)
                    t = xop("tensor", lambda e, vh=vh, sl=sl, par=par, bank=bank: e.matmul(
                        bank[:, vh * 128:(vh + 1) * 128],
                        lhsT=Ptlg[par][:, sl], rhs=Xg[1][:, sl],
                        start=True, stop=True), deps=d)
                    d = []
                    res(f"Ptl{par}").read(t)
                    res(f"Xg1h{hgi}").read(t)
                B.wrote(t)
                ev = xop("vector", lambda e, hgi=hgi, bank=bank: e.tensor_tensor(
                    out=ObufT[:, hgi * 512:(hgi + 1) * 512], in0=bank,
                    in1=qsbg[:, hgi * 512:(hgi + 1) * 512], op=ALU.add),
                    deps=B.rdeps() + res("qsb").rdeps() + res(f"Obuf{hgi}").wdeps())
                B.read(ev)
                res("qsb").read(ev)
                res(f"Obuf{hgi}").wrote(ev)
                d = B.wdeps() + res(f"KTM{par}").rdeps()
                stoks = []
                for vh in range(4):
                    p = hgi * 4 + vh
                    b2, _, kh, pp = pidx(p)
                    cc = c * 8 + pp
                    ks = (b2 * 2 + kh) * 128
                    t = xop("tensor", lambda e, vh=vh, ks=ks, p=p, par=par, bank=bank: e.matmul(
                        bank[:, vh * 128:(vh + 1) * 128],
                        lhsT=KTM[par][:, ks:ks + 128], rhs=Dbarg[:, p * 128:(p + 1) * 128],
                        start=True, stop=True), deps=d + res(f"Db{p}").rdeps())
                    d = []
                    res(f"KTM{par}").read(t)
                    res(f"Db{p}").read(t)
                    stoks.append(t)
                B.wrote(stoks[-1])
                for vh in range(4):
                    p = hgi * 4 + vh
                    b2, _, _, pp = pidx(p)
                    cc = c * 8 + pp
                    ev = xop("vector", lambda e, vh=vh, p=p, cc=cc, nxt=nxt, cur=cur, bank=bank: e.scalar_tensor_tensor(
                        out=(St2[p] if nxt else St[p])[:], in0=(St2[p] if cur else St[p])[:],
                        scalar=gcb[:, cc:cc + 1], in1=bank[:, vh * 128:(vh + 1) * 128],
                        op0=ALU.mult, op1=ALU.add),
                        deps=[stoks[vh]] + res(f"St{nxt}_{p}").wdeps() + res(f"St{cur}_{p}").rdeps())
                    B.read(ev)
                    res(f"St{nxt}_{p}").wrote(ev)
                    res(f"St{cur}_{p}").read(ev)
                    cp = xop("gpsimd", lambda e, p=p, nxt=nxt: e.tensor_copy(
                        out=Sbt[p][:], in_=(St2[p] if nxt else St[p])[:]),
                        deps=[ev] + res(f"Sbt{p}").wdeps())
                    res(f"St{nxt}_{p}").read(cp)
                    res(f"Sbt{p}").wrote(cp)
                sq = xop("scalar", lambda e, hgi=hgi: e.activation(
                    out=sqgT[:, hgi * 512:(hgi + 1) * 512],
                    in_=ObufT[:, hgi * 512:(hgi + 1) * 512], func=AF.Square),
                    deps=res(f"Obuf{hgi}").rdeps() + res(f"sqg{hgi}").wdeps())
                res(f"Obuf{hgi}").read(sq)
                res(f"sqg{hgi}").wrote(sq)
                g0 = hgi * 12
                rd = xop("vector", lambda e, hgi=hgi, g0=g0: e.tensor_reduce(
                    out=gstT[:, g0:g0 + 4].rearrange("p (h o) -> p h o", o=1),
                    in_=sqgT[:, hgi * 512:(hgi + 1) * 512].rearrange("p (h d) -> p h d", h=4),
                    axis=AX.X, op=ALU.add),
                    deps=[sq] + res(f"gst{hgi}").wdeps())
                res(f"sqg{hgi}").read(rd)
                ln = xop("scalar", lambda e, g0=g0: e.activation(
                    out=gstT[:, g0 + 4:g0 + 8], in_=gstT[:, g0:g0 + 4], func=AF.Ln,
                    scale=1.0 / 128, bias=BEP), deps=[rd])
                ex = xop("scalar", lambda e, g0=g0: e.activation(
                    out=gstT[:, g0 + 8:g0 + 12], in_=gstT[:, g0 + 4:g0 + 8], func=AF.Exp,
                    scale=-0.5), deps=[ln])
                mtoks = []
                for vh in range(4):
                    p = hgi * 4 + vh
                    mt = xop("gpsimd", lambda e, p=p, vh=vh, hgi=hgi, g0=g0: e.tensor_scalar(
                        out=ggtT[:, p * 128:(p + 1) * 128], in0=ObufT[:, p * 128:(p + 1) * 128],
                        scalar1=gstT[:, g0 + 8 + vh:g0 + 9 + vh], scalar2=None, op0=ALU.mult),
                        deps=[ex] + res(f"Obuf{hgi}").rdeps() + res(f"ggt{hgi}").wdeps())
                    res(f"Obuf{hgi}").read(mt)
                    mtoks.append(mt)
                res(f"gst{hgi}").wrote(ex)
                res(f"gst{hgi}").read(mtoks[-1])
                gg = xop("gpsimd", lambda e, hgi=hgi, par=par: e.tensor_tensor(
                    out=ggtT[:, hgi * 512:(hgi + 1) * 512], in0=ggtT[:, hgi * 512:(hgi + 1) * 512],
                    in1=ZTM[par][:, hgi * 512:(hgi + 1) * 512], op=ALU.mult),
                    deps=mtoks + res(f"ZTM{par}").rdeps())
                res(f"ZTM{par}").read(gg)
                res(f"ggt{hgi}").wrote(gg)
                off = hgi * T + c * 128
                dt = xdma("sync", lambda e, off=off, hgi=hgi: e.dma_start(
                    out=gated_tm[off:off + 128, :], in_=ggtT[:, hgi * 512:(hgi + 1) * 512]),
                    deps=[gg])
                res(f"ggt{hgi}").read(dt)
                out_toks.append(dt)

        gated_toks = []
        issue_loads(0)
        for k in range(NCH + 1):
            if k < NCH:
                emit_A1(k)
            for j in range(7):
                if k > 0 and j == 0:
                    emit_A2(k - 1)
                if j >= 1 and k < NCH:
                    emit_B_level(k, j)
                if k > 0:
                    emit_C_level(k - 1, j)
            if k > 0:
                emit_D(k - 1, gated_toks)
            if k + 1 < NCH:
                issue_loads(k + 1)
        ch.serial_last = ch.all_tokens()
        pool.reset(m2)

        if dbg >= 2:
            O("gpsimd", lambda e: e.dma_start(out=dbg_aps["d_gat"][:], in_=gated_tm[:]), "dma")

        e3 = ch.serial_last
        m3 = pool.mark()
        gfm = [pool.alloc(128, BT, BF16) for _ in range(4)]
        wobq = [pool.alloc(128, 4 * 512, BF16) for _ in range(2)]
        ptile = [pool.alloc(128, 512, BF16) for _ in range(2)]
        bW3 = [psV[:, 0:512], psV[:, 512:1024]]
        gtoks = []
        for vt in range(4):
            gtoks.append(xdma("sync", lambda e, vt=vt: e.dma_start(
                out=gfm[vt], in_=gated_tm[:, vt * 128:(vt + 1) * 128], transpose=True),
                deps=e3))
        wqtoks = [[], []]
        for dqp in range(2):
            wqtoks[dqp % 2] = [xdma("gpsimd", lambda e, dq=dqp, vt=vt: e.dma_start(
                out=wobq[dq % 2][:, vt * 512:(vt + 1) * 512],
                in_=wob_sl[vt * 128:(vt + 1) * 128, dq * 512:(dq + 1) * 512]),
                deps=list(e3)) for vt in range(4)]
        rs_toks = []
        for dq in range(4):
            store_toks = []
            for tt in range(64):
                bk = res(f"pW3{tt % 2}")
                bank = bW3[tt % 2]
                d = bk.wdeps() + wqtoks[dq % 2] + (gtoks if tt == 0 else [])
                t = None
                for vt in range(4):
                    t = xop("tensor", lambda e, tt=tt, vt=vt, dq=dq, bank=bank: e.matmul(
                        bank, lhsT=gfm[vt][:, tt * 128:(tt + 1) * 128],
                        rhs=wobq[dq % 2][:, vt * 512:(vt + 1) * 512],
                        start=(vt == 0), stop=(vt == 3)), deps=d)
                    d = []
                res(f"wq{dq % 2}").read(t)
                bk.wrote(t)
                pt = res(f"pt{tt % 2}")
                ev = xop("scalar", lambda e, tt=tt, bank=bank: e.copy(
                    out=ptile[tt % 2], in_=bank),
                    deps=bk.rdeps() + pt.wdeps())
                bk.read(ev)
                st = xdma(("sync", "scalar")[tt % 2], lambda e, dq=dq, tt=tt: e.dma_start(
                    out=part_q[dq][tt * 128:(tt + 1) * 128, :], in_=ptile[tt % 2]),
                    deps=[ev])
                pt.wrote(st)
                store_toks.append(st)
            if dq + 1 < 4:
                wqtoks[(dq + 1) % 2] = [xdma("gpsimd", lambda e, dq=dq + 1, vt=vt: e.dma_start(
                    out=wobq[dq % 2][:, vt * 512:(vt + 1) * 512],
                    in_=wob_sl[vt * 128:(vt + 1) * 128, dq * 512:(dq + 1) * 512]),
                    deps=res(f"wq{(dq + 1) % 2}").wdeps()) for vt in range(4)]
            rs_toks.append(ch.cc(lambda e, dq=dq: e.collective_compute(
                "ReduceScatter", ALU.add, replica_groups=RG,
                ins=[part_q[dq][:]], outs=[rs_q[dq][:]]), deps=store_toks))
        ch.serial_last = tuple(rs_toks) + ch.all_tokens()
        if dbg >= 3:
            for dq in range(4):
                O("gpsimd", lambda e, dq=dq: e.dma_start(
                    out=dbg_aps["d_rs"][:, dq * 512:(dq + 1) * 512], in_=rs_q[dq][:]), "dma")
            O("gpsimd", lambda e: e.dma_start(out=dbg_aps["d_pp"][:], in_=part_q[0][:]), "dma")
        pool.reset(m3)
        oab = pool.alloc(128, 8 * 2048, BF16)
        with ch.group():
            for tt in range(8):
                for dq in range(4):
                    O("sync", lambda e, tt=tt, dq=dq: e.dma_start(
                        out=oab[:, tt * 2048 + dq * 512:tt * 2048 + (dq + 1) * 512],
                        in_=rs_q[dq][tt * 128:(tt + 1) * 128, :]), "dma")
        oaf = pool.alloc(128, D, F32)
        xt3 = pool.alloc(128, D, F32)
        x2t = pool.alloc(128, D, F32)
        scr3 = pool.alloc(128, D, F32)
        st3 = pool.alloc(128, 8, F32)
        x2lb = pool.alloc(128, D, BF16)
        for tt in range(8):
            O("scalar", lambda e, tt=tt: e.copy(out=oaf, in_=oab[:, tt * 2048:(tt + 1) * 2048]))
            O("sync", lambda e, tt=tt: e.dma_start(out=xt3, in_=x_sh[tt * 128:(tt + 1) * 128, :]), "dma")
            ln_tile(oaf, None, scr3, st3, out_f32=x2t, res=xt3)
            O("sync", lambda e, tt=tt: e.dma_start(out=x2_hbm[tt * 128:(tt + 1) * 128, :], in_=x2t), "dma")
            ln_tile(x2t, x2lb, scr3, st3)
            O("sync", lambda e, tt=tt: e.dma_start(out=x2ln_hbm[tt * 128:(tt + 1) * 128, :], in_=x2lb), "dma")
        pool.reset(m3)

        x2lT = pool.alloc(128, 16 * 1024, BF16)
        with ch.group():
            for k in range(16):
                O("sync", lambda e, k=k: e.dma_start(
                    out=x2lT[:, k * 1024:(k + 1) * 1024],
                    in_=x2ln_hbm[:, k * 128:(k + 1) * 128], transpose=True), "dma")
        e4 = ch.all_tokens()
        ch.serial_last = e4
        m4pre = pool.mark()
        blk1 = [pool.alloc(128, 4096, BF16) for _ in range(2)]
        blk3 = [pool.alloc(128, 4096, BF16) for _ in range(2)]
        sg4 = [pool.alloc(128, 1024, BF16) for _ in range(2)]
        hid4 = [pool.alloc(128, 1024, BF16) for _ in range(2)]
        yt4 = [pool.alloc(128, 1024, BF16) for _ in range(2)]
        w1v = ag_w1t.rearrange("(c k p) m -> c p k m", p=128, k=16)
        w3v = ag_w3t.rearrange("(c k p) m -> c p k m", p=128, k=16)
        pWu = [psU[:, 0:1024], psU[:, 1024:2048]]
        pWv = [psV[:, 0:1024], psV[:, 1024:2048]]

        def load_blk(mp):
            pq = mp % 2
            c2 = mp // 4
            lsl = slice((mp % 4) * 256, (mp % 4 + 1) * 256)
            r1, r3 = res(f"blk1_{pq}"), res(f"blk3_{pq}")
            t1 = xdma("sync", lambda e, lsl=lsl, pq=pq, c2=c2: e.dma_start(
                out=blk1[pq].rearrange("p (k m) -> p k m", k=16), in_=w1v[c2][:, :, lsl]),
                deps=r1.wdeps() + (list(e4) if mp < 2 else []))
            t3 = xdma("scalar", lambda e, lsl=lsl, pq=pq, c2=c2: e.dma_start(
                out=blk3[pq].rearrange("p (k m) -> p k m", k=16), in_=w3v[c2][:, :, lsl]),
                deps=r3.wdeps() + (list(e4) if mp < 2 else []))
            r1.wrote(t1)
            r3.wrote(t3)

        load_blk(0)
        load_blk(1)
        ytoks = []
        for mp in range(32):
            pq = mp % 2
            for mh in range(2):
                m = 2 * mp + mh
                q = m % 2
                moff = mh * 128
                rU, rV = res(f"pWu{q}"), res(f"pWv{q}")
                d = rU.wdeps() + res(f"blk1_{pq}").rdeps() + (list(e4) if m < 2 else [])
                t = None
                for k in range(16):
                    for h in range(2):
                        t = xop("tensor", lambda e, k=k, h=h, moff=moff, pq=pq, q=q: e.matmul(
                            pWu[q][:, h * 512:(h + 1) * 512],
                            lhsT=blk1[pq][:, k * 256 + moff: k * 256 + moff + 128],
                            rhs=x2lT[:, k * 1024 + h * 512: k * 1024 + (h + 1) * 512],
                            start=(k == 0), stop=(k == 15)), deps=d)
                        d = []
                res(f"blk1_{pq}").read(t)
                rU.wrote(t)
                d = rV.wdeps() + res(f"blk3_{pq}").rdeps()
                for k in range(16):
                    for h in range(2):
                        t = xop("tensor", lambda e, k=k, h=h, moff=moff, pq=pq, q=q: e.matmul(
                            pWv[q][:, h * 512:(h + 1) * 512],
                            lhsT=blk3[pq][:, k * 256 + moff: k * 256 + moff + 128],
                            rhs=x2lT[:, k * 1024 + h * 512: k * 1024 + (h + 1) * 512],
                            start=(k == 0), stop=(k == 15)), deps=d)
                        d = []
                res(f"blk3_{pq}").read(t)
                rV.wrote(t)
                sl = xop("scalar", lambda e, q=q: e.activation(
                    out=sg4[q], in_=pWu[q], func=AF.Silu),
                    deps=rU.rdeps() + res(f"sg{q}").wdeps())
                rU.read(sl)
                res(f"sg{q}").wrote(sl)
                ml = xop("vector", lambda e, q=q: e.tensor_tensor(
                    out=hid4[q], in0=sg4[q], in1=pWv[q], op=ALU.mult),
                    deps=[sl] + rV.rdeps() + res(f"hid{q}").wdeps())
                rV.read(ml)
                res(f"sg{q}").read(ml)
                res(f"hid{q}").wrote(ml)
                d = rU.wdeps()
                for h in range(2):
                    t = xop("tensor", lambda e, h=h, q=q: e.matmul(
                        pWu[q][:, h * 512:(h + 1) * 512], lhsT=h128t[:],
                        rhs=hid4[q][:, h * 512:(h + 1) * 512], start=True, stop=True),
                        deps=d + [ml])
                    d = []
                res(f"hid{q}").read(t)
                rU.wrote(t)
                ev = xop("scalar", lambda e, q=q: e.activation(
                    out=yt4[q], in_=pWu[q], func=AF.Copy, scale=1.0 / 128),
                    deps=rU.rdeps() + res(f"yt{q}").wdeps())
                rU.read(ev)
                st = xdma("sync", lambda e, m=m, q=q: e.dma_start(out=ybuf_hbm[m], in_=yt4[q]),
                          deps=[ev])
                res(f"yt{q}").wrote(st)
                ytoks.append(st)
            if mp + 2 < 32:
                load_blk(mp + 2)
        if dbg >= 3:
            ch.serial_last = ch.all_tokens()
            for mi in range(8):
                O("gpsimd", lambda e, mi=mi: e.dma_start(
                    out=dbg_aps["d_y1"][mi * 128:(mi + 1) * 128, :], in_=ybuf_hbm[mi]), "dma")
        pool.reset(m3)

        yr2 = [pool.alloc(64, 2048, BF16) for _ in range(2)]
        h2t2 = [pool.alloc(64, 2048, BF16) for _ in range(2)]
        h2_all = []
        for jp in range(64):
            q = jp % 2
            rl = res(f"yr{q}")
            tl = xdma("sync", lambda e, jp=jp, q=q: e.dma_start(
                out=yr2[q].rearrange("a (j p) -> a j p", j=2),
                in_=ybuf_hbm[:, 2 * jp:2 * jp + 2, :]),
                deps=rl.wdeps() + (ytoks if jp < 2 else []))
            rl.wrote(tl)
            bk = res(f"pH64{q}")
            pH = psU if q == 0 else psV
            d = bk.wdeps() + [tl]
            t = None
            for h in range(4):
                t = xop("tensor", lambda e, h=h, q=q, pH=pH: e.matmul(
                    pH[0:64, h * 512:(h + 1) * 512], lhsT=h64t[:],
                    rhs=yr2[q][:, h * 512:(h + 1) * 512], start=True, stop=True), deps=d)
                d = []
            rl.read(t)
            bk.wrote(t)
            ev = xop("scalar", lambda e, q=q, pH=pH: e.activation(
                out=h2t2[q], in_=pH[0:64, :], func=AF.Copy, scale=float(np.sqrt(2.0))),
                deps=bk.rdeps() + res(f"h2t{q}").wdeps())
            bk.read(ev)
            sts = []
            for jj in range(2):
                j = 2 * jp + jj
                sts.append(xdma("sync", lambda e, j=j, jj=jj, q=q: e.dma_start(
                    out=h2_hbm[:, j % 2, :, j // 2, :].rearrange("tt a tp -> a tt tp"),
                    in_=h2t2[q][:, jj * 1024:(jj + 1) * 1024].rearrange("a (tt tp) -> a tt tp", tt=8)),
                    deps=[ev]))
            res(f"h2t{q}").wrote(*sts)
            h2_all.extend(sts)
        h2toks = h2_all
        if dbg >= 3:
            ch.serial_last = ch.all_tokens()
            O("gpsimd", lambda e: e.dma_start(
                out=dbg_aps["d_h2"][:], in_=h2_hbm[0].rearrange("h a m p -> (h a) (m p)")), "dma")
        pool.reset(m3)

        ffob = pool.alloc(128, 8 * 2048, BF16)
        m5 = pool.mark()
        w2q = [pool.alloc(128, 32 * 512, BF16) for _ in range(3)]
        h2blk = [pool.alloc(128, 8192, BF16) for _ in range(2)]
        w2v = ag_w2t.rearrange("(a b) d -> b a d", a=64)
        bW2 = [psV[:, 0:512], psV[:, 512:1024], psV[:, 1024:1536], psV[:, 1536:2048]]

        def load_w2h(dq, kh):
            qq = (dq * 2 + kh) % 3
            r = res(f"w2q{qq}")
            toks = [xdma("gpsimd", lambda e, dq=dq, k=k, kh=kh, qq=qq: e.dma_start(
                out=w2q[qq][:, (k - kh * 32) * 512:(k - kh * 32 + 1) * 512],
                in_=w2v[2 * k:2 * k + 2, :, dq * 512:(dq + 1) * 512]),
                deps=(r.wdeps() + list(h2toks)) if k == kh * 32 else [])
                for k in range(kh * 32, kh * 32 + 32)]
            r.wrote(*toks)

        load_w2h(0, 0)
        load_w2h(0, 1)
        w2ev = []
        for dq in range(4):
            if dq + 1 < 4:
                load_w2h(dq + 1, 0)
            q0 = (dq * 2) % 3
            q1 = (dq * 2 + 1) % 3
            for tt in range(8):
                qb = res(f"h2b{tt % 2}")
                tl = xdma("sync", lambda e, tt=tt: e.dma_start(
                    out=h2blk[tt % 2], in_=h2_hbm[tt].rearrange("h a m p -> (h a) (m p)")),
                    deps=qb.wdeps() + (h2toks if dq == 0 and tt < 2 else []))
                qb.wrote(tl)
                bk = res(f"pW2{tt % 4}")
                bank = bW2[tt % 4]
                d = bk.wdeps() + [tl] + res(f"w2q{q0}").rdeps() + res(f"w2q{q1}").rdeps()
                t = None
                for k in range(64):
                    qq = q0 if k < 32 else q1
                    t = xop("tensor", lambda e, k=k, qq=qq, tt=tt, bank=bank: e.matmul(
                        bank, lhsT=h2blk[tt % 2][:, k * 128:(k + 1) * 128],
                        rhs=w2q[qq][:, (k % 32) * 512:(k % 32 + 1) * 512],
                        start=(k == 0), stop=(k == 63)), deps=d)
                    d = []
                qb.read(t)
                res(f"w2q{q0}").read(t)
                res(f"w2q{q1}").read(t)
                bk.wrote(t)
                ev = xop("scalar", lambda e, dq=dq, tt=tt, bank=bank: e.copy(
                    out=ffob[:, tt * 2048 + dq * 512: tt * 2048 + (dq + 1) * 512], in_=bank),
                    deps=bk.rdeps())
                bk.read(ev)
                w2ev.append(ev)
            if dq + 1 < 4:
                load_w2h(dq + 1, 1)
        ch.serial_last = tuple(w2ev) + ch.all_tokens()
        if dbg >= 3:
            O("gpsimd", lambda e: e.dma_start(out=dbg_aps["d_ffo"][:], in_=ffob), "dma")
        pool.reset(m5)
        ff_f = pool.alloc(128, D, F32)
        x2t4 = pool.alloc(128, D, F32)
        outt = pool.alloc(128, D, F32)
        scr4 = pool.alloc(128, D, F32)
        st4 = pool.alloc(128, 8, F32)
        for tt in range(8):
            O("scalar", lambda e, tt=tt: e.copy(out=ff_f, in_=ffob[:, tt * 2048:(tt + 1) * 2048]))
            O("sync", lambda e, tt=tt: e.dma_start(out=x2t4, in_=x2_hbm[tt * 128:(tt + 1) * 128, :]), "dma")
            ln_tile(ff_f, None, scr4, st4, out_f32=outt, res=x2t4)
            O("sync", lambda e, tt=tt: e.dma_start(out=out_sh[tt * 128:(tt + 1) * 128, :], in_=outt), "dma")

        ch.emit(block, drain=ch.all_tokens())

    return nc, dbg_aps


def make_inputs(core, inp):
    c = core
    qc = slice(256 * c, 256 * (c + 1))
    kc = slice(2048 + 256 * c, 2048 + 256 * (c + 1))
    vc = slice(4096 + 512 * c, 4096 + 512 * (c + 1))
    wqkv = np.asarray(inp["L0_attn_w_qkv"], dtype=np.float32)
    conv = np.asarray(inp["L0_attn_conv_w"], dtype=np.float32)
    wq = np.concatenate([wqkv[:, qc], wqkv[:, kc], wqkv[:, vc]], axis=1)
    cw = np.concatenate([conv[:, qc], conv[:, kc], conv[:, vc]], axis=1)
    x = np.asarray(inp["L0_attn_x"], dtype=np.float32).reshape(BT, D)
    return {
        "x_sh": np.ascontiguousarray(x[c * TSH:(c + 1) * TSH]),
        "wq_sl": np.ascontiguousarray(wq),
        "wz_sl": np.ascontiguousarray(np.asarray(inp["L0_attn_w_z"], dtype=np.float32)[:, 512 * c:512 * (c + 1)]),
        "wba_sl": np.ascontiguousarray(np.concatenate(
            [np.asarray(inp["L0_attn_w_b"], dtype=np.float32)[:, 4 * c:4 * (c + 1)],
             np.asarray(inp["L0_attn_w_a"], dtype=np.float32)[:, 4 * c:4 * (c + 1)]], axis=1)),
        "cwT_sl": np.ascontiguousarray(cw.T),
        "dta8": np.ascontiguousarray(np.concatenate(
            [np.zeros(4, np.float32), np.asarray(inp["L0_attn_dt_bias"], dtype=np.float32)[4 * c:4 * (c + 1)]])[:, None]),
        "alog8": np.ascontiguousarray(np.concatenate(
            [np.zeros(4, np.float32), np.asarray(inp["L0_attn_A_log"], dtype=np.float32)[4 * c:4 * (c + 1)]])[:, None]),
        "normw": np.ascontiguousarray(np.asarray(inp["L0_attn_norm_w"], dtype=np.float32)[:, None]),
        "wo_sl": np.ascontiguousarray(np.asarray(inp["L0_attn_w_out"], dtype=np.float32)[512 * c:512 * (c + 1), :]),
        "w1_sl": np.ascontiguousarray(np.asarray(inp["L0_ffn_W1"], dtype=np.float32)[1024 * c:1024 * (c + 1), :]),
        "w3_sl": np.ascontiguousarray(np.asarray(inp["L0_ffn_W3"], dtype=np.float32)[1024 * c:1024 * (c + 1), :]),
        "w2_sl": np.ascontiguousarray(np.asarray(inp["L0_ffn_W2"], dtype=np.float32)[:, 1024 * c:1024 * (c + 1)]),
    }


def kernel(**inp):
    nc, _ = build(dbg=0)
    in_maps = [make_inputs(c, inp) for c in range(8)]
    res = run_bass_kernel_spmd(nc, in_maps, core_ids=list(range(8)))
    out = np.concatenate([res.results[c]["out_sh"] for c in range(8)], axis=0)
    return out.reshape(B, T, D).astype(np.float32)


# version 14 (final)
# speedup vs baseline: 2.2952x; 1.0913x over previous
import numpy as np
import ml_dtypes
import concourse.bass as bass
import concourse.mybir as mybir
from concourse.bass_utils import run_bass_kernel_spmd

F32 = mybir.dt.float32
BF16 = mybir.dt.bfloat16
AF = mybir.ActivationFunctionType
ALU = mybir.AluOpType
AX = mybir.AxisListType

B, T, D = 2, 4096, 2048
BT = B * T
CCH = 128
NCH = T // CCH
TSH = BT // 8
EPS = 1e-6
LNEPS = 1e-5
NEGINF = -30000.0
BF = ml_dtypes.bfloat16
LN128 = float(np.log(128.0))


def _hadamard(n):
    h = np.array([[1.0]])
    while h.shape[0] < n:
        h = np.block([[h, h], [h, -h]])
    return h


ENGS = ("sync", "scalar", "vector", "tensor", "gpsimd")


class Sched:

    NSLOT = 8

    def __init__(self, nc):
        self.nc = nc
        self.ops = {e: [] for e in ENGS}
        self.cnt = {e: 0 for e in ENGS}
        self.csem = {e: nc.alloc_semaphore(name=f"cs_{e}") for e in ENGS}
        self.waited = {e: {} for e in ENGS}
        self.dsem = {}
        self.dcnt = {}
        self.drr = {}
        for q in ("sync", "scalar", "vector", "gpsimd"):
            self.drr[q] = 0
            for i in range(self.NSLOT):
                self.dsem[(q, i)] = nc.alloc_semaphore(name=f"d_{q}{i}")
                self.dcnt[(q, i)] = 0
        self.n_cc = 0
        self.serial_last = ()
        self._grp = None

    SKIP_SAME = False
    PRUNE = True

    def _waits(self, eng, deps, nosame=False):
        need = {}
        own = self.csem[eng]
        for t in deps:
            if t is None:
                continue
            sem, val = t
            if (nosame or self.SKIP_SAME) and sem is own:
                continue
            k = sem.num
            if k not in need or need[k][1] < val:
                need[k] = (sem, val)
        out = []
        w = self.waited[eng]
        for k, (sem, val) in need.items():
            if self.PRUNE and w.get(k, -1) >= val:
                continue
            w[k] = val
            out.append((sem, val))
        return out

    def op(self, eng, fn, deps=(), nosame=False):
        waits = self._waits(eng, deps, nosame)
        self.cnt[eng] += 1
        tok = (self.csem[eng], self.cnt[eng])
        self.ops[eng].append((waits, fn, tok[0], 1))
        return tok

    def dma(self, q, fn, deps=(), nosame=False):
        slot = self.drr[q] % self.NSLOT
        self.drr[q] += 1
        sem = self.dsem[(q, slot)]
        prev = self.dcnt[(q, slot)]
        deps = list(deps)
        if prev:
            deps.append((sem, prev))
        waits = self._waits(q, deps, nosame)
        self.dcnt[(q, slot)] = prev + 16
        self.ops[q].append((waits, fn, sem, 16))
        return (sem, prev + 16)

    def cc(self, fn, deps=()):
        sem = self.nc.alloc_semaphore(name=f"ccs{self.n_cc}")
        self.n_cc += 1
        waits = self._waits("gpsimd", deps)
        self.ops["gpsimd"].append((waits, fn, sem, 1))
        return (sem, 1)

    def C(self, eng, fn, kind="comp"):
        deps = self.serial_last
        if kind == "dma":
            tok = self.dma(eng, fn, deps)
        elif kind == "cc":
            tok = self.cc(fn, deps)
        else:
            tok = self.op(eng, fn, deps)
        if self._grp is None:
            self.serial_last = (tok,)
        else:
            self._grp.append(tok)
        return tok

    def group(self):
        sched = self

        class G:
            def __enter__(self):
                sched._grp = []

            def __exit__(self, *a):
                if sched._grp:
                    sched.serial_last = tuple(sched._grp)
                sched._grp = None
                return False

        return G()

    def all_tokens(self):
        toks = [(self.csem[e], self.cnt[e]) for e in ENGS if self.cnt[e] > 0]
        for key, c in self.dcnt.items():
            if c > 0:
                toks.append((self.dsem[key], c))
        return tuple(toks)

    def emit(self, block, drain=()):
        fin = self._waits("sync", drain)
        for e in ENGS:
            ops = self.ops[e]
            tail = fin if e == "sync" else []

            def body(eng, ops=ops, tail=tail):
                for waits, fn, sem, inc in ops:
                    for s, v in waits:
                        eng.wait_ge(s, v)
                    inst = fn(eng)
                    if inst is not None and inc:
                        inst.then_inc(sem, inc)
                for s, v in tail:
                    eng.wait_ge(s, v)

            getattr(block, e)(body)


class Pool:

    def __init__(self, nc, name, nbytes):
        self.t = nc.alloc_sbuf_tensor(name, [128, nbytes], mybir.dt.uint8)
        self.off = 0
        self.peak = 0
        self.nbytes = nbytes

    def alloc(self, p, n, dtype):
        esz = mybir.dt.size(dtype)
        start = (self.off + 31) // 32 * 32
        self.off = start + n * esz
        self.peak = max(self.peak, self.off)
        assert self.off <= self.nbytes, f"pool overflow {self.off}"
        return self.t[0:p, start:start + n * esz].bitcast(dtype)

    def mark(self):
        return self.off

    def reset(self, m):
        self.off = m


def build(dbg=0):
    nc = bass.Bass("TRN2", target_bir_lowering=False)
    RG = [list(range(8))]

    x_sh = nc.dram_tensor("x_sh", [TSH, D], F32, kind="ExternalInput").ap()
    wq_sl = nc.dram_tensor("wq_sl", [D, 1024], F32, kind="ExternalInput").ap()
    wz_sl = nc.dram_tensor("wz_sl", [D, 512], F32, kind="ExternalInput").ap()
    wba_sl = nc.dram_tensor("wba_sl", [D, 8], F32, kind="ExternalInput").ap()
    cwT_sl = nc.dram_tensor("cwT_sl", [1024, 4], F32, kind="ExternalInput").ap()
    dta8 = nc.dram_tensor("dta8", [8, 1], F32, kind="ExternalInput").ap()
    alog8 = nc.dram_tensor("alog8", [8, 1], F32, kind="ExternalInput").ap()
    normw = nc.dram_tensor("normw", [128, 1], F32, kind="ExternalInput").ap()
    wo_sl = nc.dram_tensor("wo_sl", [512, D], F32, kind="ExternalInput").ap()
    w1_sl = nc.dram_tensor("w1_sl", [1024, D], F32, kind="ExternalInput").ap()
    w3_sl = nc.dram_tensor("w3_sl", [1024, D], F32, kind="ExternalInput").ap()
    w2_sl = nc.dram_tensor("w2_sl", [D, 1024], F32, kind="ExternalInput").ap()
    out_sh = nc.dram_tensor("out_sh", [TSH, D], F32, kind="ExternalOutput").ap()

    dram = nc.dram_tensor
    xln_sl = dram("xln_sl", [TSH, D], BF16).ap()
    ag_xln = dram("ag_xln", [BT, D], BF16).ap()
    w1b_sl = dram("w1b_sl", [1024, D], BF16).ap()
    w3b_sl = dram("w3b_sl", [1024, D], BF16).ap()
    w2b_sl = dram("w2b_sl", [D, 1024], BF16).ap()
    w2t_sl = dram("w2t_sl", [1024, D], BF16).ap()
    w1t_sl = dram("w1t_sl", [D, 1024], BF16).ap()
    w3t_sl = dram("w3t_sl", [D, 1024], BF16).ap()
    wob_sl = dram("wob_sl", [512, D], BF16).ap()
    ag_w1t = dram("ag_w1t", [8 * D, 1024], BF16).ap()
    ag_w3t = dram("ag_w3t", [8 * D, 1024], BF16).ap()
    ag_w2t = dram("ag_w2t", [8192, D], BF16).ap()
    part_q = [dram(f"part_q{i}", [BT, 512], BF16).ap() for i in range(4)]
    rs_q = [dram(f"rs_q{i}", [TSH, 512], BF16).ap() for i in range(4)]

    qb_hbm = dram("qb_hbm", [256, BT], BF16).ap()
    kb_hbm = dram("kb_hbm", [256, BT], BF16).ap()
    vb_hbm = dram("vb_hbm", [512, BT], BF16).ap()
    zb_hbm = dram("zb_hbm", [512, BT], BF16).ap()
    lng_hbm = dram("lng_hbm", [8, BT], F32).ap()
    batile_hbm = dram("batile_hbm", [8, BT], F32).ap()
    lr_hbm = dram("lr_hbm", [4, BT], F32).ap()
    glay = dram("glay", [256, 128], F32).ap()
    blay = dram("blay", [256, 128], F32).ap()
    lrklay = dram("lrklay", [256, 128], F32).ap()
    lrqlay = dram("lrqlay", [256, 128], F32).ap()
    evec_hbm = dram("evec_hbm", [3, 256, CCH], F32).ap()
    scl_hbm = dram("scl_hbm", [4, 256, CCH], BF16).ap()
    gcx_hbm = dram("gcx_hbm", [256, 1], F32).ap()
    gated_tm = dram("gated_tm", [BT, 512], BF16).ap()
    part_hbm = dram("part_hbm", [BT, D], BF16).ap()
    rs_out = dram("rs_out", [TSH, D], BF16).ap()
    x2_hbm = dram("x2_hbm", [TSH, D], F32).ap()
    x2ln_hbm = dram("x2ln_hbm", [TSH, D], BF16).ap()
    w1t_hbm = dram("w1t_hbm", [D, 8192], BF16).ap()
    w3t_hbm = dram("w3t_hbm", [D, 8192], BF16).ap()
    ybuf_hbm = dram("ybuf_hbm", [64, 128, TSH], BF16).ap()
    h2_hbm = dram("h2_hbm", [8, 2, 64, 64, 128], BF16).ap()

    dbg_aps = {}
    if dbg >= 1:
        dbg_aps["d_mix"] = dram("d_mix", [1024, BT], F32, kind="ExternalOutput").ap()
        dbg_aps["d_lng"] = dram("d_lng", [8, BT], F32, kind="ExternalOutput").ap()
        dbg_aps["d_lr"] = dram("d_lr", [4, BT], F32, kind="ExternalOutput").ap()
        dbg_aps["d_xln"] = dram("d_xln", [BT, D], F32, kind="ExternalOutput").ap()
        dbg_aps["d_z"] = dram("d_z", [512, BT], F32, kind="ExternalOutput").ap()
    if dbg >= 2:
        dbg_aps["d_gat"] = dram("d_gat", [BT, 512], F32, kind="ExternalOutput").ap()
    if dbg >= 3:
        dbg_aps["d_x2"] = dram("d_x2", [TSH, D], F32, kind="ExternalOutput").ap()
        dbg_aps["d_rs"] = dram("d_rs", [TSH, D], F32, kind="ExternalOutput").ap()
        dbg_aps["d_y1"] = dram("d_y1", [1024, TSH], F32, kind="ExternalOutput").ap()
        dbg_aps["d_h2"] = dram("d_h2", [128, 8192], F32, kind="ExternalOutput").ap()
        dbg_aps["d_ffo"] = dram("d_ffo", [128, 16384], F32, kind="ExternalOutput").ap()
        dbg_aps["d_pp"] = dram("d_pp", [BT, 512], F32, kind="ExternalOutput").ap()

    ti = np.arange(CCH)
    maskL = np.where(ti[:, None] > ti[None, :], 0.0, NEGINF).astype(np.float32)
    maskU = maskL.T.copy()
    maskQ = np.where(ti[:, None] <= ti[None, :], -0.5 * LN128, NEGINF).astype(np.float32)
    c_maskL = nc.inline_tensor(maskL, name="c_maskL").ap()
    c_maskU = nc.inline_tensor(maskU, name="c_maskU").ap()
    c_maskQ = nc.inline_tensor(maskQ, name="c_maskQ").ap()
    c_idf = nc.inline_tensor(np.eye(128, dtype=np.float32), name="c_idf").ap()
    c_idb = nc.inline_tensor(np.eye(128, dtype=BF), name="c_idb").ap()
    c_onesb = nc.inline_tensor(np.ones((128, 1), dtype=BF), name="c_onesb").ap()
    c_onesf = nc.inline_tensor(np.ones((1, 128), dtype=np.float32), name="c_onesf").ap()
    c_sign8 = nc.inline_tensor(
        np.array([-1, -1, -1, -1, 1, 1, 1, 1], dtype=np.float32)[:, None], name="c_sign8").ap()
    c_h128 = nc.inline_tensor(_hadamard(128).astype(BF), name="c_h128").ap()
    c_h64 = nc.inline_tensor(_hadamard(64).astype(BF), name="c_h64").ap()
    mL4 = np.tile(np.where(ti[:, None] > ti[None, :], 0.0, NEGINF), (1, 4)).astype(BF)
    mQ4 = np.tile(np.where(ti[:, None] <= ti[None, :], 0.0, NEGINF), (1, 4)).astype(BF)
    c_mskL4 = nc.inline_tensor(mL4, name="c_mskL4").ap()
    c_mskQ4 = nc.inline_tensor(mQ4, name="c_mskQ4").ap()
    c_bias = nc.inline_tensor(np.tile(np.array(
        [LNEPS, EPS, 1.0, 128.0 * EPS], dtype=np.float32), (128, 1)), name="c_bias").ap()

    sb = nc.alloc_sbuf_tensor
    cst = sb("cst", [128, 4 * 128 + 16 + 128 + 4], F32)
    idbt = sb("idbt", [128, 129], BF16)
    h128t = sb("h128t", [128, 128], BF16)
    h64t = sb("h64t", [64, 64], BF16)
    St = [sb(f"St{p}", [128, 128], F32) for p in range(8)]
    St2 = [sb(f"St2_{p}", [128, 128], F32) for p in range(8)]
    Sbt = [sb(f"Sbt{p}", [128, 128], BF16) for p in range(8)]
    sclT = sb("sclT", [128, 4 * 256], BF16)
    sclF = sb("sclF", [128, 4 * 256], F32)
    gcb = sb("gcb", [128, 256], F32)
    mskL4 = sb("mskL4", [128, 512], BF16)
    mskQ4 = sb("mskQ4", [128, 512], BF16)

    MKL, MKU, MKQ = cst[:, 0:128], cst[:, 128:256], cst[:, 256:384]
    IDF = cst[:, 384:512]
    SGN8, DTA8, ALOG8 = cst[0:8, 512:513], cst[0:8, 513:514], cst[0:8, 514:515]
    NRMW = cst[0:128, 515:516]
    ONESF = cst[0:1, 528:656]
    BLN = cst[:, 656:657]
    BEP = cst[:, 657:658]
    BONE = cst[:, 658:659]
    BQ128 = cst[:, 659:660]
    IDB, ONEB = idbt[:, 0:128], idbt[:, 128:129]
    nrscT = sclF[:, 0:256]
    oscT = sclF[:, 256:512]
    dscT = sclF[:, 512:768]
    bexT = sclF[:, 768:1024]

    pool = Pool(nc, "pool", 167 * 1024)

    ps = nc.alloc_psum_tensor
    psU = ps("psU", [128, 2048], F32)
    psV = ps("psV", [128, 2048], F32)
    psBig = psU[:, 0:1024]
    psBig2 = psU[:, 1024:2048]
    psS1 = psV[:, 0:512]
    psA = psV[:, 512:1024]
    psB = psV[:, 1024:1536]

    with nc.Block() as block:
        ch = Sched(nc)
        O = ch.C

        def ln_tile(xt, outb, scr, st, eps=None, out_f32=None, res=None):
            O("vector", lambda e: e.reduce_sum(out=st[:, 0:1], in_=xt, axis=AX.X))
            O("vector", lambda e: e.scalar_tensor_tensor(
                out=scr, in0=xt, scalar=1.0, in1=xt, op0=ALU.mult, op1=ALU.mult,
                accum_out=st[:, 1:2]))
            O("vector", lambda e: e.tensor_scalar(
                out=st[:, 2:3], in0=st[:, 0:1], scalar1=-1.0 / D, scalar2=None, op0=ALU.mult))
            O("vector", lambda e: e.tensor_tensor(
                out=st[:, 3:4], in0=st[:, 2:3], in1=st[:, 2:3], op=ALU.mult))
            O("vector", lambda e: e.scalar_tensor_tensor(
                out=st[:, 4:5], in0=st[:, 1:2], scalar=1.0 / D, in1=st[:, 3:4],
                op0=ALU.mult, op1=ALU.subtract))
            ba = eps if eps is not None else BLN
            O("scalar", lambda e, ba=ba: e.activation(out=st[:, 5:6], in_=st[:, 4:5], func=AF.Ln, bias=ba))
            O("scalar", lambda e: e.activation(out=st[:, 6:7], in_=st[:, 5:6], func=AF.Exp, scale=-0.5))
            if outb is not None:
                O("vector", lambda e: e.tensor_scalar(
                    out=outb, in0=xt, scalar1=st[:, 2:3], scalar2=st[:, 6:7],
                    op0=ALU.add, op1=ALU.mult))
            if res is not None:
                O("vector", lambda e: e.tensor_scalar(
                    out=scr, in0=xt, scalar1=st[:, 2:3], scalar2=st[:, 6:7],
                    op0=ALU.add, op1=ALU.mult))
                O("vector", lambda e: e.tensor_tensor(out=out_f32, in0=scr, in1=res, op=ALU.add))

        class Res:
            __slots__ = ("w", "readers")

            def __init__(s):
                s.w = []
                s.readers = []

            def wdeps(s):
                return list(s.readers) + list(s.w)

            def wrote(s, *toks):
                s.w = list(toks)
                s.readers = []

            def read(s, tok):
                s.readers.append(tok)

            def rdeps(s):
                return list(s.w)

        RS = {}

        def res(name):
            if name not in RS:
                RS[name] = Res()
            return RS[name]

        def xop(eng, fn, deps=()):
            return ch.op(eng, fn, deps, nosame=True)

        def xdma(q, fn, deps=()):
            return ch.dma(q, fn, deps)

        with ch.group():
            O("gpsimd", lambda e: e.dma_start(out=w1b_sl[:], in_=w1_sl[:]), "dma")
            O("gpsimd", lambda e: e.dma_start(out=w3b_sl[:], in_=w3_sl[:]), "dma")
            O("gpsimd", lambda e: e.dma_start(out=w2b_sl[:], in_=w2_sl[:]), "dma")
            O("sync", lambda e: e.dma_start(out=MKL, in_=c_maskL[:]), "dma")
            O("sync", lambda e: e.dma_start(out=MKU, in_=c_maskU[:]), "dma")
            O("sync", lambda e: e.dma_start(out=MKQ, in_=c_maskQ[:]), "dma")
            O("sync", lambda e: e.dma_start(out=IDF, in_=c_idf[:]), "dma")
            O("sync", lambda e: e.dma_start(out=SGN8, in_=c_sign8[:]), "dma")
            O("sync", lambda e: e.dma_start(out=DTA8, in_=dta8[:]), "dma")
            O("sync", lambda e: e.dma_start(out=ALOG8, in_=alog8[:]), "dma")
            O("sync", lambda e: e.dma_start(out=NRMW, in_=normw[:]), "dma")
            O("sync", lambda e: e.dma_start(out=ONESF, in_=c_onesf[:]), "dma")
            O("sync", lambda e: e.dma_start(out=idbt[:, 0:128], in_=c_idb[:]), "dma")
            O("sync", lambda e: e.dma_start(out=idbt[:, 128:129], in_=c_onesb[:]), "dma")
            O("sync", lambda e: e.dma_start(out=h128t[:], in_=c_h128[:]), "dma")
            O("sync", lambda e: e.dma_start(out=h64t[:], in_=c_h64[:]), "dma")
            O("sync", lambda e: e.dma_start(out=cst[:, 656:660], in_=c_bias[:]), "dma")
            O("sync", lambda e: e.dma_start(out=mskL4[:], in_=c_mskL4[:]), "dma")
            O("sync", lambda e: e.dma_start(out=mskQ4[:], in_=c_mskQ4[:]), "dma")

        m0 = pool.mark()
        coef8 = cst[0:8, 516:517]
        ncoef8 = cst[0:8, 517:518]
        O("scalar", lambda e: e.activation(out=coef8, in_=ALOG8, func=AF.Exp))
        O("vector", lambda e: e.tensor_scalar(
            out=ncoef8, in0=coef8, scalar1=-1.0, scalar2=None, op0=ALU.mult))

        xt0 = pool.alloc(128, D, F32)
        scr0 = pool.alloc(128, D, F32)
        st0 = pool.alloc(128, 8, F32)
        xlnb0 = pool.alloc(128, D, BF16)
        for i in range(8):
            O("sync", lambda e, i=i: e.dma_start(out=xt0, in_=x_sh[i * 128:(i + 1) * 128, :]), "dma")
            ln_tile(xt0, xlnb0, scr0, st0)
            O("sync", lambda e, i=i: e.dma_start(out=xln_sl[i * 128:(i + 1) * 128, :], in_=xlnb0), "dma")
        pre_ag = ch.serial_last
        O("gpsimd", lambda e: e.collective_compute(
            "AllGather", ALU.bypass, replica_groups=RG, ins=[xln_sl[:]], outs=[ag_xln[:]]), "cc")
        ch.serial_last = pre_ag

        w2tt = pool.alloc(128, D, BF16)
        for i in range(8):
            O("sync", lambda e, i=i: e.dma_start(
                out=w2tt, in_=w2b_sl[:, i * 128:(i + 1) * 128], transpose=True), "dma")
            O("sync", lambda e, i=i: e.dma_start(
                out=w2t_sl[i * 128:(i + 1) * 128, :], in_=w2tt), "dma")
        for k in range(16):
            O("sync", lambda e, k=k: e.dma_start(
                out=w2tt[:, 0:1024], in_=w1b_sl[:, k * 128:(k + 1) * 128], transpose=True), "dma")
            O("sync", lambda e, k=k: e.dma_start(
                out=w1t_sl[k * 128:(k + 1) * 128, :], in_=w2tt[:, 0:1024]), "dma")
            O("sync", lambda e, k=k: e.dma_start(
                out=w2tt[:, 1024:2048], in_=w3b_sl[:, k * 128:(k + 1) * 128], transpose=True), "dma")
            O("sync", lambda e, k=k: e.dma_start(
                out=w3t_sl[k * 128:(k + 1) * 128, :], in_=w2tt[:, 1024:2048]), "dma")

        wot_f = pool.alloc(128, D, F32)
        wot_b = pool.alloc(128, D, BF16)
        for i in range(4):
            O("sync", lambda e, i=i: e.dma_start(out=wot_f, in_=wo_sl[i * 128:(i + 1) * 128, :]), "dma")
            O("vector", lambda e: e.tensor_scalar(
                out=wot_b, in0=wot_f, scalar1=NRMW, scalar2=None, op0=ALU.mult))
            O("sync", lambda e, i=i: e.dma_start(out=wob_sl[i * 128:(i + 1) * 128, :], in_=wot_b), "dma")

        pre_ag = ch.serial_last
        O("gpsimd", lambda e: e.collective_compute(
            "AllGather", ALU.bypass, replica_groups=RG, ins=[w1t_sl[:]], outs=[ag_w1t[:]]), "cc")
        O("gpsimd", lambda e: e.collective_compute(
            "AllGather", ALU.bypass, replica_groups=RG, ins=[w3t_sl[:]], outs=[ag_w3t[:]]), "cc")
        O("gpsimd", lambda e: e.collective_compute(
            "AllGather", ALU.bypass, replica_groups=RG, ins=[w2t_sl[:]], outs=[ag_w2t[:]]), "cc")
        ch.serial_last = ch.all_tokens()
        pool.reset(m0)

        batile = pool.alloc(8, BT, F32)
        m1b = pool.mark()
        wqb = pool.alloc(128, 16 * 1024, BF16)
        wzb = pool.alloc(128, 16 * 512, BF16)
        wbab = pool.alloc(128, 16 * 8, BF16)
        wst = pool.alloc(128, 1024, F32)
        cwt = pool.alloc(128, 32, F32)
        for k in range(16):
            O("sync", lambda e, k=k: e.dma_start(out=wst, in_=wq_sl[k * 128:(k + 1) * 128, :]), "dma")
            O("scalar", lambda e, k=k: e.copy(out=wqb[:, k * 1024:(k + 1) * 1024], in_=wst))
            O("sync", lambda e, k=k: e.dma_start(out=wst[:, 0:512], in_=wz_sl[k * 128:(k + 1) * 128, :]), "dma")
            O("scalar", lambda e, k=k: e.copy(out=wzb[:, k * 512:(k + 1) * 512], in_=wst[:, 0:512]))
            O("sync", lambda e, k=k: e.dma_start(out=wst[:, 0:8], in_=wba_sl[k * 128:(k + 1) * 128, :]), "dma")
            O("scalar", lambda e, k=k: e.copy(out=wbab[:, k * 8:(k + 1) * 8], in_=wst[:, 0:8]))
        for m in range(8):
            O("sync", lambda e, m=m: e.dma_start(
                out=cwt[:, m * 4:(m + 1) * 4], in_=cwT_sl[m * 128:(m + 1) * 128, :]), "dma")

        xlt = pool.alloc(128, 16 * 1024, BF16)
        mc = pool.alloc(128, 1027, F32)
        acc0 = pool.alloc(128, 1024, F32)
        acc1 = pool.alloc(128, 1024, F32)
        tails = pool.alloc(128, 24, F32)
        sqt = pool.alloc(128, 1024, BF16)
        mtmp = pool.alloc(128, 1024, BF16)
        lrrow = pool.alloc(1, 1024, F32)

        def lhsT_ph1(m, k):
            if m < 8:
                return wqb[:, k * 1024 + m * 128: k * 1024 + (m + 1) * 128]
            if m < 12:
                return wzb[:, k * 512 + (m - 8) * 128: k * 512 + (m - 7) * 128]
            return wbab[:, k * 8:(k + 1) * 8]

        for tc in range(8):
            tsl = slice(tc * 1024, (tc + 1) * 1024)
            with ch.group():
                for k in range(16):
                    O("sync", lambda e, k=k, tc=tc: e.dma_start(
                        out=xlt[:, k * 1024:(k + 1) * 1024],
                        in_=ag_xln[tc * 1024:(tc + 1) * 1024, k * 128:(k + 1) * 128],
                        transpose=True), "dma")
            for m in range(13):
                pst = psBig if (m % 2 == 0) else psBig2
                pm = pst[0:128, :] if m < 12 else pst[0:8, :]
                for k in range(16):
                    for h in range(2):
                        O("tensor", lambda e, m=m, k=k, h=h, pm=pm: e.matmul(
                            pm[:, h * 512:(h + 1) * 512], lhsT=lhsT_ph1(m, k),
                            rhs=xlt[:, k * 1024 + h * 512: k * 1024 + (h + 1) * 512],
                            start=(k == 0), stop=(k == 15)))
                if m < 8:
                    O("scalar", lambda e, pm=pm: e.copy(out=mc[:, 3:1027], in_=pm))
                    if tc in (0, 4):
                        O("vector", lambda e: e.memset(mc[:, 0:3], 0.0))
                    else:
                        O("scalar", lambda e, m=m: e.copy(out=mc[:, 0:3], in_=tails[:, m * 3:(m + 1) * 3]))
                    O("scalar", lambda e, m=m: e.copy(out=tails[:, m * 3:(m + 1) * 3], in_=mc[:, 1024:1027]))
                    O("vector", lambda e, m=m: e.tensor_scalar(
                        out=acc0, in0=mc[:, 3:1027], scalar1=cwt[:, m * 4 + 3:m * 4 + 4],
                        scalar2=None, op0=ALU.mult))
                    O("vector", lambda e, m=m: e.scalar_tensor_tensor(
                        out=acc1, in0=mc[:, 2:1026], scalar=cwt[:, m * 4 + 2:m * 4 + 3],
                        in1=acc0, op0=ALU.mult, op1=ALU.add))
                    O("vector", lambda e, m=m: e.scalar_tensor_tensor(
                        out=acc0, in0=mc[:, 1:1025], scalar=cwt[:, m * 4 + 1:m * 4 + 2],
                        in1=acc1, op0=ALU.mult, op1=ALU.add))
                    O("vector", lambda e, m=m: e.scalar_tensor_tensor(
                        out=acc1, in0=mc[:, 0:1024], scalar=cwt[:, m * 4 + 0:m * 4 + 1],
                        in1=acc0, op0=ALU.mult, op1=ALU.add))
                    O("scalar", lambda e: e.activation(out=mtmp, in_=acc1, func=AF.Silu))
                    if m < 2:
                        O("sync", lambda e, m=m, tsl=tsl: e.dma_start(
                            out=qb_hbm[m * 128:(m + 1) * 128, tsl], in_=mtmp), "dma")
                    elif m < 4:
                        O("sync", lambda e, m=m, tsl=tsl: e.dma_start(
                            out=kb_hbm[(m - 2) * 128:(m - 1) * 128, tsl], in_=mtmp), "dma")
                    else:
                        O("sync", lambda e, m=m, tsl=tsl: e.dma_start(
                            out=vb_hbm[(m - 4) * 128:(m - 3) * 128, tsl], in_=mtmp), "dma")
                    if m < 4:
                        O("scalar", lambda e: e.activation(out=sqt, in_=mtmp, func=AF.Square))
                        O("tensor", lambda e: e.matmul(
                            psS1[0:1, 0:512], lhsT=ONEB, rhs=sqt[:, 0:512], start=True, stop=True))
                        O("tensor", lambda e: e.matmul(
                            psA[0:1, 0:512], lhsT=ONEB, rhs=sqt[:, 512:1024], start=True, stop=True))
                        O("scalar", lambda e: e.copy(
                            out=lrrow[:, 0:512], in_=psS1[0:1, 0:512]))
                        O("scalar", lambda e: e.copy(
                            out=lrrow[:, 512:1024], in_=psA[0:1, 0:512]))
                        O("sync", lambda e, m=m, tsl=tsl: e.dma_start(
                            out=lr_hbm[m:m + 1, tsl], in_=lrrow), "dma")
                        lay = lrqlay if m < 2 else lrklay
                        hh = m % 2
                        bb, c0 = tc // 4, (tc % 4) * 8
                        for r in range(2):
                            base_ch = c0 * 8 + (2 * hh + r) * 2 + bb
                            O("sync", lambda e, lay=lay, base_ch=base_ch: e.dma_start(
                                out=lay[base_ch:base_ch + 57:8, :],
                                in_=lrrow.rearrange("o (c p) -> o c p", p=128)), "dma")
                    if dbg >= 1:
                        O("gpsimd", lambda e, m=m, tsl=tsl: e.dma_start(
                            out=dbg_aps["d_mix"][m * 128:(m + 1) * 128, tsl], in_=mtmp), "dma")
                elif m < 12:
                    O("scalar", lambda e, pm=pm: e.activation(out=mtmp, in_=pm, func=AF.Silu))
                    O("sync", lambda e, m=m, tsl=tsl: e.dma_start(
                        out=zb_hbm[(m - 8) * 128:(m - 7) * 128, tsl], in_=mtmp), "dma")
                    if dbg >= 1:
                        O("gpsimd", lambda e, m=m, tsl=tsl: e.dma_start(
                            out=dbg_aps["d_z"][(m - 8) * 128:(m - 7) * 128, tsl], in_=mtmp), "dma")
                else:
                    O("scalar", lambda e, pm=pm, tc=tc: e.copy(
                        out=batile[:, tc * 1024:(tc + 1) * 1024], in_=pm))

        pool.reset(m1b)
        spA = pool.alloc(8, BT, F32)
        spB = pool.alloc(8, BT, F32)
        lngt = pool.alloc(8, BT, F32)
        O("vector", lambda e: e.tensor_scalar(
            out=spA, in0=batile, scalar1=SGN8, scalar2=DTA8, op0=ALU.mult, op1=ALU.add))
        O("scalar", lambda e: e.activation(out=spB, in_=spA, func=AF.Abs))
        O("scalar", lambda e: e.activation(out=spB, in_=spB, func=AF.Exp, scale=-1.0))
        O("scalar", lambda e: e.activation(out=spB, in_=spB, func=AF.Ln, bias=BONE[0:8, :]))
        O("vector", lambda e: e.tensor_scalar(
            out=spA, in0=spA, scalar1=0.0, scalar2=None, op0=ALU.max))
        O("vector", lambda e: e.tensor_tensor(out=spA, in0=spA, in1=spB, op=ALU.add))
        O("vector", lambda e: e.tensor_scalar(
            out=lngt, in0=spA, scalar1=ncoef8, scalar2=None, op0=ALU.mult))
        O("sync", lambda e: e.dma_start(out=lng_hbm[:], in_=lngt), "dma")
        O("sync", lambda e: e.dma_start(
            out=glay.rearrange("(c h b) p -> h b c p", h=4, b=2),
            in_=lngt[4:8, :].rearrange("h (b c p) -> h b c p", b=2, c=32)), "dma")
        O("sync", lambda e: e.dma_start(
            out=blay.rearrange("(c h b) p -> h b c p", h=4, b=2),
            in_=lngt[0:4, :].rearrange("h (b c p) -> h b c p", b=2, c=32)), "dma")
        if dbg >= 1:
            O("sync", lambda e: e.dma_start(out=dbg_aps["d_lng"][:], in_=lngt), "dma")
            O("sync", lambda e: e.dma_start(out=dbg_aps["d_lr"][:], in_=lr_hbm[:]), "dma")

        cgt = [pool.alloc(128, 128, F32) for _ in range(2)]
        cgt2 = [pool.alloc(128, 128, F32) for _ in range(2)]
        lnbt = [pool.alloc(128, 128, F32) for _ in range(2)]
        lrkt = [pool.alloc(128, 128, F32) for _ in range(2)]
        lrqt = [pool.alloc(128, 128, F32) for _ in range(2)]
        for t2 in range(2):
            cs = slice(t2 * 128, (t2 + 1) * 128)
            with ch.group():
                O("sync", lambda e, t2=t2, cs=cs: e.dma_start(out=cgt[t2][:, :], in_=glay[cs]), "dma")
                O("sync", lambda e, t2=t2, cs=cs: e.dma_start(out=lnbt[t2][:, :], in_=blay[cs]), "dma")
                O("sync", lambda e, t2=t2, cs=cs: e.dma_start(out=lrkt[t2][:, :], in_=lrklay[cs]), "dma")
                O("sync", lambda e, t2=t2, cs=cs: e.dma_start(out=lrqt[t2][:, :], in_=lrqlay[cs]), "dma")
        for t2 in range(2):
            O("scalar", lambda e, t2=t2: e.activation(
                out=lrkt[t2], in_=lrkt[t2], func=AF.Ln, bias=BEP))
            O("scalar", lambda e, t2=t2: e.activation(
                out=lrqt[t2], in_=lrqt[t2], func=AF.Ln, scale=128.0, bias=BQ128))
        for t2 in range(2):
            src, dst = cgt[t2], cgt2[t2]
            s = 1
            while s < 128:
                O("vector", lambda e, src=src, dst=dst, s=s: e.tensor_tensor(
                    out=dst[:, s:128], in0=src[:, s:128], in1=src[:, 0:128 - s], op=ALU.add))
                O("vector", lambda e, src=src, dst=dst, s=s: e.tensor_copy(
                    out=dst[:, 0:s], in_=src[:, 0:s]))
                src, dst = dst, src
                s *= 2
            if src is not cgt[t2]:
                O("vector", lambda e, src=src, t2=t2: e.tensor_copy(out=cgt[t2], in_=src))

        av = [pool.alloc(128, 128, F32) for _ in range(2)]
        bv = [pool.alloc(128, 128, F32) for _ in range(2)]
        qv = [pool.alloc(128, 128, F32) for _ in range(2)]
        ncg = pool.alloc(128, 128, F32)
        sct = pool.alloc(128, 128, F32)
        scb = pool.alloc(128, 128, BF16)
        for t2 in range(2):
            O("vector", lambda e, t2=t2: e.scalar_tensor_tensor(
                out=av[t2], in0=lrkt[t2], scalar=-0.5, in1=cgt[t2], op0=ALU.mult, op1=ALU.add))
            O("vector", lambda e, t2=t2: e.tensor_tensor(
                out=av[t2], in0=av[t2], in1=lnbt[t2], op=ALU.add))
            O("vector", lambda e, t2=t2: e.tensor_scalar(
                out=ncg, in0=cgt[t2], scalar1=-1.0, scalar2=None, op0=ALU.mult))
            O("vector", lambda e, t2=t2: e.scalar_tensor_tensor(
                out=bv[t2], in0=lrkt[t2], scalar=-0.5, in1=ncg, op0=ALU.mult, op1=ALU.add))
            O("vector", lambda e, t2=t2: e.scalar_tensor_tensor(
                out=qv[t2], in0=lrqt[t2], scalar=-0.5, in1=cgt[t2], op0=ALU.mult, op1=ALU.add))
            O("scalar", lambda e, t2=t2: e.activation(out=sct, in_=av[t2], func=AF.Exp))
            O("vector", lambda e: e.tensor_scalar(
                out=scb, in0=sct, scalar1=-1.0, scalar2=None, op0=ALU.mult))
            O("sync", lambda e, t2=t2: e.dma_start(
                out=scl_hbm[0, t2 * 128:(t2 + 1) * 128, :], in_=scb), "dma")
            O("scalar", lambda e, t2=t2: e.activation(
                out=scb, in_=qv[t2], func=AF.Exp))
            O("sync", lambda e, t2=t2: e.dma_start(
                out=scl_hbm[1, t2 * 128:(t2 + 1) * 128, :], in_=scb), "dma")
            O("vector", lambda e, t2=t2: e.tensor_scalar(
                out=sct, in0=bv[t2], scalar1=cgt[t2][:, 127:128], scalar2=None, op0=ALU.add))
            O("scalar", lambda e: e.activation(out=scb, in_=sct, func=AF.Exp))
            O("sync", lambda e, t2=t2: e.dma_start(
                out=scl_hbm[2, t2 * 128:(t2 + 1) * 128, :], in_=scb), "dma")
            O("scalar", lambda e, t2=t2: e.activation(out=scb, in_=lnbt[t2], func=AF.Exp))
            O("sync", lambda e, t2=t2: e.dma_start(
                out=scl_hbm[3, t2 * 128:(t2 + 1) * 128, :], in_=scb), "dma")
            O("sync", lambda e, t2=t2: e.dma_start(
                out=evec_hbm[0, t2 * 128:(t2 + 1) * 128, :], in_=av[t2]), "dma")
            O("sync", lambda e, t2=t2: e.dma_start(
                out=evec_hbm[1, t2 * 128:(t2 + 1) * 128, :], in_=bv[t2]), "dma")
            O("sync", lambda e, t2=t2: e.dma_start(
                out=evec_hbm[2, t2 * 128:(t2 + 1) * 128, :], in_=qv[t2]), "dma")
            O("sync", lambda e, t2=t2: e.dma_start(
                out=gcx_hbm[t2 * 128:(t2 + 1) * 128, :], in_=cgt[t2][:, 127:128]), "dma")
        for i in range(4):
            O("sync", lambda e, i=i: e.dma_start(
                out=sclT[:, i * 256:(i + 1) * 256], in_=scl_hbm[i], transpose=True), "dma")
            O("scalar", lambda e, i=i: e.copy(
                out=sclF[:, i * 256:(i + 1) * 256], in_=sclT[:, i * 256:(i + 1) * 256]))
        gcrow = pool.alloc(1, 256, F32)
        O("sync", lambda e: e.dma_start(
            out=gcrow, in_=gcx_hbm.rearrange("p o -> o p")), "dma")
        O("scalar", lambda e: e.activation(out=gcrow, in_=gcrow, func=AF.Exp))
        O("tensor", lambda e: e.matmul(psS1[:, 0:256], lhsT=ONESF, rhs=gcrow, start=True, stop=True))
        O("scalar", lambda e: e.copy(out=gcb[:], in_=psS1[:, 0:256]))
        pool.reset(0)

        for p in range(8):
            tk = O("vector", lambda e, p=p: e.memset(St[p][:], 0.0))
            tk2 = O("vector", lambda e, p=p: e.memset(Sbt[p][:], 0.0))

        m2 = pool.mark()
        rows = pool.alloc(2, 8 * 1024, F32)
        O("vector", lambda e: e.memset(rows[:], 1.0))
        KFM = [pool.alloc(128, 512, BF16) for _ in range(2)]
        QFM = [pool.alloc(128, 512, BF16) for _ in range(2)]
        KTM = [pool.alloc(128, 512, BF16) for _ in range(2)]
        VTM = [pool.alloc(128, 1024, BF16) for _ in range(2)]
        ZTM = [pool.alloc(128, 1024, BF16) for _ in range(2)]
        Zpg = [[pool.alloc(128, 1024, BF16) for _ in range(7)] for _ in range(2)]
        Ppg = [[pool.alloc(128, 1024, BF16) for _ in range(6)] for _ in range(2)]
        ELg = pool.alloc(128, 1024, BF16)
        EQg = pool.alloc(128, 1024, BF16)
        nKKg = pool.alloc(128, 1024, BF16)
        Ptlg = [pool.alloc(128, 1024, BF16) for _ in range(2)]
        Xg = [pool.alloc(128, 1024, BF16) for _ in range(2)]
        bVtT = pool.alloc(128, 1024, BF16)
        qsbg = pool.alloc(128, 1024, BF16)
        Dbarg = pool.alloc(128, 1024, BF16)
        ObufT = pool.alloc(128, 1024, BF16)
        ggtT = pool.alloc(128, 1024, BF16)
        sqgT = pool.alloc(128, 1024, BF16)
        gstT = pool.alloc(128, 24, F32)

        bA = [psU[:, 0:512], psU[:, 512:1024]]
        bEL = psU[:, 1024:1536]
        bEQ = psU[:, 1536:2048]
        bZ = psV[:, 0:512]
        bP = psV[:, 512:1024]
        bX = [psV[:, 1024:1536], psV[:, 1536:2048]]

        ph2_entry = ch.all_tokens()
        for p in range(8):
            res(f"Sbt{p}").wrote(*ph2_entry)
            res(f"St0_{p}").wrote(*ph2_entry)

        def dep0(c):
            return ph2_entry if c <= 1 else ()

        def issue_loads(c):
            par = c % 2
            r = res(f"rows{par}")
            d = list(r.wdeps()) + list(dep0(c))
            toks = []
            for kind, src in ((0, 0), (1, 1), (2, 1), (3, 2)):
                row = 1 if kind < 2 else 0
                toks.append(xdma("gpsimd", lambda e, kind=kind, src=src, row=row, par=par, c=c: e.dma_start(
                    out=rows[row:row + 1, (par * 4 + kind) * 1024:(par * 4 + kind + 1) * 1024],
                    in_=evec_hbm[src, c * 8:(c + 1) * 8, :].rearrange("c p -> () (c p)")),
                    deps=d))
                d = []
            r.wrote(*toks)
            for name, tile in (("KFM", KFM), ("QFM", QFM), ("KTM", KTM), ("VTM", VTM), ("ZTM", ZTM)):
                rr = res(f"{name}{par}")
                d = list(rr.wdeps()) + list(dep0(c))
                toks = []
                for b2 in range(2):
                    off = b2 * T + c * 128
                    if name == "KTM":
                        toks.append(xdma("sync", lambda e, b2=b2, off=off, tile=tile, par=par: e.dma_start(
                            out=tile[par][:, b2 * 256:(b2 + 1) * 256],
                            in_=kb_hbm[:, off:off + 128], transpose=True), deps=d))
                    elif name == "VTM":
                        toks.append(xdma("sync", lambda e, b2=b2, off=off, tile=tile, par=par: e.dma_start(
                            out=tile[par][:, b2 * 512:(b2 + 1) * 512],
                            in_=vb_hbm[:, off:off + 128], transpose=True), deps=d))
                    elif name == "ZTM":
                        toks.append(xdma("sync", lambda e, b2=b2, off=off, tile=tile, par=par: e.dma_start(
                            out=tile[par][:, b2 * 512:(b2 + 1) * 512],
                            in_=zb_hbm[:, off:off + 128], transpose=True), deps=d))
                    else:
                        hb = kb_hbm if name == "KFM" else qb_hbm
                        for hh in range(2):
                            toks.append(xdma("sync", lambda e, b2=b2, hh=hh, off=off, tile=tile, par=par, hb=hb: e.dma_start(
                                out=tile[par][:, (b2 * 2 + hh) * 128:(b2 * 2 + hh + 1) * 128],
                                in_=hb[hh * 128:(hh + 1) * 128, off:off + 128]), deps=d))
                    d = []
                rr.wrote(*toks)

        def pidx(p):
            b2, vh = p // 4, p % 4
            return b2, vh, vh // 2, vh * 2 + b2

        def emit_A1(c):
            par = c % 2
            rowres = res(f"rows{par}")
            rowtok = rowres.rdeps()
            for bank, msk, rkL, rkR, outg, bname, gname in (
                    (bEL, mskL4[:], 0, 2, ELg, "bEL", "ELg"),
                    (bEQ, mskQ4[:], 1, 3, EQg, "bEQ", "EQg")):
                B = res(bname)
                for hgi in range(2):
                    d = B.wdeps() + rowtok + list(dep0(c))
                    t = xop("tensor", lambda e, bank=bank, msk=msk: e.matmul(
                        bank, lhsT=IDB, rhs=msk, start=True, stop=True), deps=d)
                    for vh in range(4):
                        p = hgi * 4 + vh
                        b2, _, _, pp = pidx(p)
                        lsl = (par * 4 + rkL) * 1024 + pp * 128
                        rsl = (par * 4 + rkR) * 1024 + pp * 128
                        t = xop("tensor", lambda e, bank=bank, vh=vh, lsl=lsl, rsl=rsl: e.matmul(
                            bank[:, vh * 128:(vh + 1) * 128],
                            lhsT=rows[0:2, lsl:lsl + 128], rhs=rows[0:2, rsl:rsl + 128],
                            start=False, stop=True))
                        rowres.read(t)
                    B.wrote(t)
                    rg = res(f"{gname}_h{hgi}")
                    ev = xop("scalar", lambda e, bank=bank, outg=outg, hgi=hgi: e.activation(
                        out=outg[:, hgi * 512:(hgi + 1) * 512], in_=bank, func=AF.Exp),
                        deps=B.rdeps() + rg.wdeps())
                    B.read(ev)
                    rg.wrote(ev)
            nktoks = []
            ptltoks = []
            for u in range(4):
                B = res(f"bA{u % 2}")
                bank = bA[u % 2]
                pA = 2 * u
                hgi = u // 2
                d = B.wdeps() + list(dep0(c))
                t = None
                for i2 in range(2):
                    p = pA + i2
                    b2, vh, kh, pp = pidx(p)
                    ks = (b2 * 2 + kh) * 128
                    t = xop("tensor", lambda e, bank=bank, i2=i2, ks=ks, par=par: e.matmul(
                        bank[:, i2 * 128:(i2 + 1) * 128],
                        lhsT=KFM[par][:, ks:ks + 128], rhs=KFM[par][:, ks:ks + 128],
                        start=True, stop=True), deps=d)
                    d = []
                    res(f"KFM{par}").read(t)
                    t = xop("tensor", lambda e, bank=bank, i2=i2, ks=ks, par=par: e.matmul(
                        bank[:, 256 + i2 * 128:256 + (i2 + 1) * 128],
                        lhsT=KFM[par][:, ks:ks + 128], rhs=QFM[par][:, ks:ks + 128],
                        start=True, stop=True))
                    res(f"KFM{par}").read(t)
                    res(f"QFM{par}").read(t)
                B.wrote(t)
                ev1 = xop("vector", lambda e, bank=bank, pA=pA: e.tensor_scalar(
                    out=nKKg[:, pA * 128:(pA + 2) * 128], in0=bank[:, 0:256],
                    scalar1=-1.0, scalar2=None, op0=ALU.mult),
                    deps=B.rdeps() + res("nKK").wdeps())
                ev2 = xop("vector", lambda e, bank=bank, pA=pA, par=par: e.tensor_tensor(
                    out=Ptlg[par][:, pA * 128:(pA + 2) * 128], in0=bank[:, 256:512],
                    in1=EQg[:, pA * 128:(pA + 2) * 128], op=ALU.mult),
                    deps=B.rdeps() + res(f"EQg_h{hgi}").rdeps() + res(f"Ptl{par}").wdeps())
                res(f"EQg_h{hgi}").read(ev2)
                B.read(ev1)
                B.read(ev2)
                nktoks.append(ev1)
                ptltoks.append(ev2)
            res("nKK").wrote(*nktoks)
            res(f"Ptl{par}").wrote(*ptltoks)
            for hgi in range(2):
                sl = slice(hgi * 512, (hgi + 1) * 512)
                t = xop("vector", lambda e, sl=sl, par=par: e.tensor_tensor(
                    out=Ppg[par][0][:, sl], in0=nKKg[:, sl], in1=ELg[:, sl], op=ALU.mult),
                    deps=res("nKK").rdeps() + res(f"ELg_h{hgi}").rdeps() + res(f"P{par}0h{hgi}").wdeps())
                res(f"ELg_h{hgi}").read(t)
                res("nKK").read(t)
                res(f"P{par}0h{hgi}").wrote(t)
            B = res("bEL")
            for hgi in range(2):
                d = B.wdeps() + res(f"P{par}0h{hgi}").rdeps()
                for vh in range(4):
                    p = hgi * 4 + vh
                    t = xop("tensor", lambda e, p=p, vh=vh, par=par: e.matmul(
                        bEL[:, vh * 128:(vh + 1) * 128],
                        lhsT=Ppg[par][0][:, p * 128:(p + 1) * 128], rhs=IDB,
                        start=True, stop=True), deps=d)
                    d = []
                    res(f"P{par}0h{hgi}").read(t)
                B.wrote(t)
                ev = xop("vector", lambda e, hgi=hgi, par=par: e.tensor_copy(
                    out=Zpg[par][0][:, hgi * 512:(hgi + 1) * 512], in_=bEL),
                    deps=B.rdeps() + res(f"Z{par}0h{hgi}").wdeps())
                B.read(ev)
                res(f"Z{par}0h{hgi}").wrote(ev)

        def emit_B_level(c, j):
            par = c % 2
            for hgi in range(2):
                dz = res("bZ").wdeps() + res(f"Z{par}{j - 1}h{hgi}").rdeps() + res(f"P{par}{j - 1}h{hgi}").rdeps()
                t = None
                for vh in range(4):
                    p = hgi * 4 + vh
                    sl = slice(p * 128, (p + 1) * 128)
                    t = xop("tensor", lambda e, vh=vh, sl=sl, j=j, par=par: e.matmul(
                        bZ[:, vh * 128:(vh + 1) * 128],
                        lhsT=Ppg[par][j - 1][:, sl], rhs=Zpg[par][j - 1][:, sl],
                        start=True, stop=True), deps=dz)
                    dz = ()
                    res(f"Z{par}{j - 1}h{hgi}").read(t)
                    res(f"P{par}{j - 1}h{hgi}").read(t)
                res("bZ").wrote(t)
                ev = xop("scalar", lambda e, hgi=hgi, j=j, par=par: e.copy(
                    out=Zpg[par][j][:, hgi * 512:(hgi + 1) * 512], in_=bZ),
                    deps=res("bZ").rdeps() + res(f"Z{par}{j}h{hgi}").wdeps())
                res("bZ").read(ev)
                res(f"Z{par}{j}h{hgi}").wrote(ev)
                if j < 6:
                    dp = res("bP").wdeps()
                    t = None
                    for vh in range(4):
                        p = hgi * 4 + vh
                        sl = slice(p * 128, (p + 1) * 128)
                        t = xop("tensor", lambda e, vh=vh, sl=sl, j=j, par=par: e.matmul(
                            bP[:, vh * 128:(vh + 1) * 128],
                            lhsT=Zpg[par][j - 1][:, sl], rhs=Ppg[par][j - 1][:, sl],
                            start=True, stop=True), deps=dp)
                        dp = ()
                        res(f"Z{par}{j - 1}h{hgi}").read(t)
                        res(f"P{par}{j - 1}h{hgi}").read(t)
                    res("bP").wrote(t)
                    ev = xop("vector", lambda e, hgi=hgi, j=j, par=par: e.tensor_copy(
                        out=Ppg[par][j][:, hgi * 512:(hgi + 1) * 512], in_=bP),
                        deps=res("bP").rdeps() + res(f"P{par}{j}h{hgi}").wdeps())
                    res("bP").read(ev)
                    res(f"P{par}{j}h{hgi}").wrote(ev)

        def emit_A2(c):
            par = c % 2
            rb = res("bVt")
            dW = rb.wdeps()
            bvt_toks = []
            for p in range(8):
                b2, vh, kh, pp = pidx(p)
                cc = c * 8 + pp
                t = xop("vector", lambda e, p=p, cc=cc, par=par: e.tensor_scalar(
                    out=bVtT[:, p * 128:(p + 1) * 128], in0=VTM[par][:, p * 128:(p + 1) * 128],
                    scalar1=bexT[:, cc:cc + 1], scalar2=None, op0=ALU.mult),
                    deps=res(f"VTM{par}").rdeps() + dW)
                dW = []
                res(f"VTM{par}").read(t)
                bvt_toks.append(t)
            rb.wrote(*bvt_toks)
            x0toks = [[], []]
            qstoks = []
            for u in range(4):
                B = res(f"bA{u % 2}")
                bank = bA[u % 2]
                pA = 2 * u
                d = B.wdeps() + list(dep0(c))
                t = None
                for i2 in range(2):
                    p = pA + i2
                    b2, vh, kh, pp = pidx(p)
                    ks = (b2 * 2 + kh) * 128
                    t = xop("tensor", lambda e, bank=bank, i2=i2, ks=ks, p=p, par=par: e.matmul(
                        bank[:, i2 * 128:(i2 + 1) * 128],
                        lhsT=KFM[par][:, ks:ks + 128], rhs=Sbt[p][:],
                        start=True, stop=True), deps=d + res(f"Sbt{p}").rdeps())
                    d = []
                    res(f"KFM{par}").read(t)
                    res(f"Sbt{p}").read(t)
                    t = xop("tensor", lambda e, bank=bank, i2=i2, ks=ks, p=p, par=par: e.matmul(
                        bank[:, 256 + i2 * 128:256 + (i2 + 1) * 128],
                        lhsT=QFM[par][:, ks:ks + 128], rhs=Sbt[p][:],
                        start=True, stop=True))
                    res(f"QFM{par}").read(t)
                    res(f"Sbt{p}").read(t)
                B.wrote(t)
                for i2 in range(2):
                    p = pA + i2
                    b2, vh, kh, pp = pidx(p)
                    cc = c * 8 + pp
                    hgi = p // 4
                    ev = xop("vector", lambda e, bank=bank, i2=i2, p=p, cc=cc: e.scalar_tensor_tensor(
                        out=Xg[0][:, p * 128:(p + 1) * 128], in0=bank[:, i2 * 128:(i2 + 1) * 128],
                        scalar=nrscT[:, cc:cc + 1], in1=bVtT[:, p * 128:(p + 1) * 128],
                        op0=ALU.mult, op1=ALU.add),
                        deps=B.rdeps() + [bvt_toks[p]] + res(f"Xg0h{hgi}").wdeps())
                    rb.read(ev)
                    B.read(ev)
                    x0toks[hgi].append(ev)
                    ev2 = xop("vector", lambda e, bank=bank, i2=i2, p=p, cc=cc: e.tensor_scalar(
                        out=qsbg[:, p * 128:(p + 1) * 128], in0=bank[:, 256 + i2 * 128:256 + (i2 + 1) * 128],
                        scalar1=oscT[:, cc:cc + 1], scalar2=None, op0=ALU.mult),
                        deps=B.rdeps() + res("qsb").wdeps())
                    B.read(ev2)
                    qstoks.append(ev2)
            res("Xg0h0").wrote(*x0toks[0])
            res("Xg0h1").wrote(*x0toks[1])
            res("qsb").wrote(*qstoks)

        def emit_C_level(c, j):
            par = c % 2
            xin_h = [res(f"Xg{j % 2}h0"), res(f"Xg{j % 2}h1")]
            for hgi in range(2):
                B = res(f"bX{hgi}")
                bank = bX[hgi]
                d = B.wdeps() + res(f"Z{par}{j}h{hgi}").rdeps() + xin_h[hgi].rdeps()
                t = None
                for vh in range(4):
                    p = hgi * 4 + vh
                    sl = slice(p * 128, (p + 1) * 128)
                    t = xop("tensor", lambda e, vh=vh, sl=sl, j=j, par=par, bank=bank: e.matmul(
                        bank[:, vh * 128:(vh + 1) * 128],
                        lhsT=Zpg[par][j][:, sl], rhs=Xg[j % 2][:, sl],
                        start=True, stop=False), deps=d)
                    d = []
                    t = xop("tensor", lambda e, vh=vh, sl=sl, j=j, bank=bank: e.matmul(
                        bank[:, vh * 128:(vh + 1) * 128],
                        lhsT=IDB, rhs=Xg[j % 2][:, sl],
                        start=False, stop=True))
                    res(f"Z{par}{j}h{hgi}").read(t)
                    xin_h[hgi].read(t)
                B.wrote(t)
                ev = xop("scalar", lambda e, hgi=hgi, j=j, bank=bank: e.copy(
                    out=Xg[(j + 1) % 2][:, hgi * 512:(hgi + 1) * 512], in_=bank),
                    deps=B.rdeps() + res(f"Xg{(j + 1) % 2}h{hgi}").wdeps())
                B.read(ev)
                res(f"Xg{(j + 1) % 2}h{hgi}").wrote(ev)

        def emit_D(c, out_toks):
            par = c % 2
            nxt = (c + 1) % 2
            cur = c % 2
            for p in range(8):
                b2, vh, kh, pp = pidx(p)
                cc = c * 8 + pp
                hgi = p // 4
                t = xop("vector", lambda e, p=p, cc=cc: e.tensor_scalar(
                    out=Dbarg[:, p * 128:(p + 1) * 128], in0=Xg[1][:, p * 128:(p + 1) * 128],
                    scalar1=dscT[:, cc:cc + 1], scalar2=None, op0=ALU.mult),
                    deps=res(f"Xg1h{hgi}").rdeps() + res(f"Db{p}").wdeps())
                res(f"Xg1h{hgi}").read(t)
                res(f"Db{p}").wrote(t)
            for hgi in range(2):
                B = res(f"bX{hgi}")
                bank = bX[hgi]
                d = B.wdeps() + res(f"Ptl{par}").rdeps() + res(f"Xg1h{hgi}").rdeps()
                t = None
                for vh in range(4):
                    p = hgi * 4 + vh
                    sl = slice(p * 128, (p + 1) * 128)
                    t = xop("tensor", lambda e, vh=vh, sl=sl, par=par, bank=bank: e.matmul(
                        bank[:, vh * 128:(vh + 1) * 128],
                        lhsT=Ptlg[par][:, sl], rhs=Xg[1][:, sl],
                        start=True, stop=True), deps=d)
                    d = []
                    res(f"Ptl{par}").read(t)
                    res(f"Xg1h{hgi}").read(t)
                B.wrote(t)
                ev = xop("vector", lambda e, hgi=hgi, bank=bank: e.tensor_tensor(
                    out=ObufT[:, hgi * 512:(hgi + 1) * 512], in0=bank,
                    in1=qsbg[:, hgi * 512:(hgi + 1) * 512], op=ALU.add),
                    deps=B.rdeps() + res("qsb").rdeps() + res(f"Obuf{hgi}").wdeps())
                B.read(ev)
                res("qsb").read(ev)
                res(f"Obuf{hgi}").wrote(ev)
                d = B.wdeps() + res(f"KTM{par}").rdeps()
                stoks = []
                for vh in range(4):
                    p = hgi * 4 + vh
                    b2, _, kh, pp = pidx(p)
                    cc = c * 8 + pp
                    ks = (b2 * 2 + kh) * 128
                    t = xop("tensor", lambda e, vh=vh, ks=ks, p=p, par=par, bank=bank: e.matmul(
                        bank[:, vh * 128:(vh + 1) * 128],
                        lhsT=KTM[par][:, ks:ks + 128], rhs=Dbarg[:, p * 128:(p + 1) * 128],
                        start=True, stop=True), deps=d + res(f"Db{p}").rdeps())
                    d = []
                    res(f"KTM{par}").read(t)
                    res(f"Db{p}").read(t)
                    stoks.append(t)
                B.wrote(stoks[-1])
                for vh in range(4):
                    p = hgi * 4 + vh
                    b2, _, _, pp = pidx(p)
                    cc = c * 8 + pp
                    ev = xop("vector", lambda e, vh=vh, p=p, cc=cc, nxt=nxt, cur=cur, bank=bank: e.scalar_tensor_tensor(
                        out=(St2[p] if nxt else St[p])[:], in0=(St2[p] if cur else St[p])[:],
                        scalar=gcb[:, cc:cc + 1], in1=bank[:, vh * 128:(vh + 1) * 128],
                        op0=ALU.mult, op1=ALU.add),
                        deps=[stoks[3]] + res(f"St{nxt}_{p}").wdeps() + res(f"St{cur}_{p}").rdeps())
                    B.read(ev)
                    res(f"St{nxt}_{p}").wrote(ev)
                    res(f"St{cur}_{p}").read(ev)
                    cp = xop("scalar", lambda e, p=p, nxt=nxt: e.copy(
                        out=Sbt[p][:], in_=(St2[p] if nxt else St[p])[:]),
                        deps=[ev] + res(f"Sbt{p}").wdeps())
                    res(f"St{nxt}_{p}").read(cp)
                    res(f"Sbt{p}").wrote(cp)
                sq = xop("scalar", lambda e, hgi=hgi: e.activation(
                    out=sqgT[:, hgi * 512:(hgi + 1) * 512],
                    in_=ObufT[:, hgi * 512:(hgi + 1) * 512], func=AF.Square),
                    deps=res(f"Obuf{hgi}").rdeps() + res(f"sqg{hgi}").wdeps())
                res(f"Obuf{hgi}").read(sq)
                res(f"sqg{hgi}").wrote(sq)
                g0 = hgi * 12
                rd = xop("vector", lambda e, hgi=hgi, g0=g0: e.tensor_reduce(
                    out=gstT[:, g0:g0 + 4].rearrange("p (h o) -> p h o", o=1),
                    in_=sqgT[:, hgi * 512:(hgi + 1) * 512].rearrange("p (h d) -> p h d", h=4),
                    axis=AX.X, op=ALU.add),
                    deps=[sq] + res(f"gst{hgi}").wdeps())
                res(f"sqg{hgi}").read(rd)
                ln = xop("scalar", lambda e, g0=g0: e.activation(
                    out=gstT[:, g0 + 4:g0 + 8], in_=gstT[:, g0:g0 + 4], func=AF.Ln,
                    scale=1.0 / 128, bias=BEP), deps=[rd])
                ex = xop("scalar", lambda e, g0=g0: e.activation(
                    out=gstT[:, g0 + 8:g0 + 12], in_=gstT[:, g0 + 4:g0 + 8], func=AF.Exp,
                    scale=-0.5), deps=[ln])
                mtoks = []
                for vh in range(4):
                    p = hgi * 4 + vh
                    mt = xop("vector", lambda e, p=p, vh=vh, hgi=hgi, g0=g0: e.tensor_scalar(
                        out=ggtT[:, p * 128:(p + 1) * 128], in0=ObufT[:, p * 128:(p + 1) * 128],
                        scalar1=gstT[:, g0 + 8 + vh:g0 + 9 + vh], scalar2=None, op0=ALU.mult),
                        deps=[ex] + res(f"Obuf{hgi}").rdeps() + res(f"ggt{hgi}").wdeps())
                    res(f"Obuf{hgi}").read(mt)
                    mtoks.append(mt)
                res(f"gst{hgi}").wrote(ex)
                res(f"gst{hgi}").read(mtoks[-1])
                gg = xop("vector", lambda e, hgi=hgi, par=par: e.tensor_tensor(
                    out=ggtT[:, hgi * 512:(hgi + 1) * 512], in0=ggtT[:, hgi * 512:(hgi + 1) * 512],
                    in1=ZTM[par][:, hgi * 512:(hgi + 1) * 512], op=ALU.mult),
                    deps=mtoks + res(f"ZTM{par}").rdeps())
                res(f"ZTM{par}").read(gg)
                res(f"ggt{hgi}").wrote(gg)
                off = hgi * T + c * 128
                dt = xdma("sync", lambda e, off=off, hgi=hgi: e.dma_start(
                    out=gated_tm[off:off + 128, :], in_=ggtT[:, hgi * 512:(hgi + 1) * 512]),
                    deps=[gg])
                res(f"ggt{hgi}").read(dt)
                out_toks.append(dt)

        gated_toks = []
        issue_loads(0)
        for k in range(NCH + 1):
            if k < NCH:
                emit_A1(k)
            for j in range(7):
                if k > 0 and j == 0:
                    emit_A2(k - 1)
                if j >= 1 and k < NCH:
                    emit_B_level(k, j)
                if k > 0:
                    emit_C_level(k - 1, j)
            if k > 0:
                emit_D(k - 1, gated_toks)
            if k + 1 < NCH:
                issue_loads(k + 1)
        ch.serial_last = ch.all_tokens()
        pool.reset(m2)

        if dbg >= 2:
            O("gpsimd", lambda e: e.dma_start(out=dbg_aps["d_gat"][:], in_=gated_tm[:]), "dma")

        e3 = ch.serial_last
        m3 = pool.mark()
        gfm = [pool.alloc(128, BT, BF16) for _ in range(4)]
        wobq = [pool.alloc(128, 4 * 512, BF16) for _ in range(2)]
        ptile = [pool.alloc(128, 512, BF16) for _ in range(2)]
        bW3 = [psV[:, 0:512], psV[:, 512:1024]]
        gtoks = []
        for vt in range(4):
            gtoks.append(xdma("sync", lambda e, vt=vt: e.dma_start(
                out=gfm[vt], in_=gated_tm[:, vt * 128:(vt + 1) * 128], transpose=True),
                deps=e3))
        wqtoks = [[], []]
        for dqp in range(2):
            wqtoks[dqp % 2] = [xdma("gpsimd", lambda e, dq=dqp, vt=vt: e.dma_start(
                out=wobq[dq % 2][:, vt * 512:(vt + 1) * 512],
                in_=wob_sl[vt * 128:(vt + 1) * 128, dq * 512:(dq + 1) * 512]),
                deps=list(e3)) for vt in range(4)]
        rs_toks = []
        for dq in range(4):
            store_toks = []
            for tt in range(64):
                bk = res(f"pW3{tt % 2}")
                bank = bW3[tt % 2]
                d = bk.wdeps() + wqtoks[dq % 2] + (gtoks if tt == 0 else [])
                t = None
                for vt in range(4):
                    t = xop("tensor", lambda e, tt=tt, vt=vt, dq=dq, bank=bank: e.matmul(
                        bank, lhsT=gfm[vt][:, tt * 128:(tt + 1) * 128],
                        rhs=wobq[dq % 2][:, vt * 512:(vt + 1) * 512],
                        start=(vt == 0), stop=(vt == 3)), deps=d)
                    d = []
                res(f"wq{dq % 2}").read(t)
                bk.wrote(t)
                pt = res(f"pt{tt % 2}")
                ev = xop("scalar", lambda e, tt=tt, bank=bank: e.copy(
                    out=ptile[tt % 2], in_=bank),
                    deps=bk.rdeps() + pt.wdeps())
                bk.read(ev)
                st = xdma(("sync", "scalar")[tt % 2], lambda e, dq=dq, tt=tt: e.dma_start(
                    out=part_q[dq][tt * 128:(tt + 1) * 128, :], in_=ptile[tt % 2]),
                    deps=[ev])
                pt.wrote(st)
                store_toks.append(st)
            if dq + 1 < 4:
                wqtoks[(dq + 1) % 2] = [xdma("gpsimd", lambda e, dq=dq + 1, vt=vt: e.dma_start(
                    out=wobq[dq % 2][:, vt * 512:(vt + 1) * 512],
                    in_=wob_sl[vt * 128:(vt + 1) * 128, dq * 512:(dq + 1) * 512]),
                    deps=res(f"wq{(dq + 1) % 2}").wdeps()) for vt in range(4)]
            rs_toks.append(ch.cc(lambda e, dq=dq: e.collective_compute(
                "ReduceScatter", ALU.add, replica_groups=RG,
                ins=[part_q[dq][:]], outs=[rs_q[dq][:]]), deps=store_toks))
        ch.serial_last = tuple(rs_toks) + ch.all_tokens()
        if dbg >= 3:
            for dq in range(4):
                O("gpsimd", lambda e, dq=dq: e.dma_start(
                    out=dbg_aps["d_rs"][:, dq * 512:(dq + 1) * 512], in_=rs_q[dq][:]), "dma")
            O("gpsimd", lambda e: e.dma_start(out=dbg_aps["d_pp"][:], in_=part_q[0][:]), "dma")
        pool.reset(m3)
        oab = pool.alloc(128, 8 * 2048, BF16)
        with ch.group():
            for tt in range(8):
                for dq in range(4):
                    O("sync", lambda e, tt=tt, dq=dq: e.dma_start(
                        out=oab[:, tt * 2048 + dq * 512:tt * 2048 + (dq + 1) * 512],
                        in_=rs_q[dq][tt * 128:(tt + 1) * 128, :]), "dma")
        oaf = pool.alloc(128, D, F32)
        xt3 = pool.alloc(128, D, F32)
        x2t = pool.alloc(128, D, F32)
        scr3 = pool.alloc(128, D, F32)
        st3 = pool.alloc(128, 8, F32)
        x2lb = pool.alloc(128, D, BF16)
        for tt in range(8):
            O("scalar", lambda e, tt=tt: e.copy(out=oaf, in_=oab[:, tt * 2048:(tt + 1) * 2048]))
            O("sync", lambda e, tt=tt: e.dma_start(out=xt3, in_=x_sh[tt * 128:(tt + 1) * 128, :]), "dma")
            ln_tile(oaf, None, scr3, st3, out_f32=x2t, res=xt3)
            O("sync", lambda e, tt=tt: e.dma_start(out=x2_hbm[tt * 128:(tt + 1) * 128, :], in_=x2t), "dma")
            ln_tile(x2t, x2lb, scr3, st3)
            O("sync", lambda e, tt=tt: e.dma_start(out=x2ln_hbm[tt * 128:(tt + 1) * 128, :], in_=x2lb), "dma")
        pool.reset(m3)

        x2lT = pool.alloc(128, 16 * 1024, BF16)
        with ch.group():
            for k in range(16):
                O("sync", lambda e, k=k: e.dma_start(
                    out=x2lT[:, k * 1024:(k + 1) * 1024],
                    in_=x2ln_hbm[:, k * 128:(k + 1) * 128], transpose=True), "dma")
        e4 = ch.all_tokens()
        ch.serial_last = e4
        m4pre = pool.mark()
        blk1 = [pool.alloc(128, 4096, BF16) for _ in range(2)]
        blk3 = [pool.alloc(128, 4096, BF16) for _ in range(2)]
        sg4 = [pool.alloc(128, 1024, BF16) for _ in range(2)]
        hid4 = [pool.alloc(128, 1024, BF16) for _ in range(2)]
        yt4 = [pool.alloc(128, 1024, BF16) for _ in range(2)]
        w1v = ag_w1t.rearrange("(c k p) m -> c p k m", p=128, k=16)
        w3v = ag_w3t.rearrange("(c k p) m -> c p k m", p=128, k=16)
        pWu = [psU[:, 0:1024], psU[:, 1024:2048]]
        pWv = [psV[:, 0:1024], psV[:, 1024:2048]]

        def load_blk(mp):
            pq = mp % 2
            c2 = mp // 4
            lsl = slice((mp % 4) * 256, (mp % 4 + 1) * 256)
            r1, r3 = res(f"blk1_{pq}"), res(f"blk3_{pq}")
            t1 = xdma("sync", lambda e, lsl=lsl, pq=pq, c2=c2: e.dma_start(
                out=blk1[pq].rearrange("p (k m) -> p k m", k=16), in_=w1v[c2][:, :, lsl]),
                deps=r1.wdeps() + (list(e4) if mp < 2 else []))
            t3 = xdma("scalar", lambda e, lsl=lsl, pq=pq, c2=c2: e.dma_start(
                out=blk3[pq].rearrange("p (k m) -> p k m", k=16), in_=w3v[c2][:, :, lsl]),
                deps=r3.wdeps() + (list(e4) if mp < 2 else []))
            r1.wrote(t1)
            r3.wrote(t3)

        load_blk(0)
        load_blk(1)
        ytoks = []
        for mp in range(32):
            pq = mp % 2
            for mh in range(2):
                m = 2 * mp + mh
                q = m % 2
                moff = mh * 128
                rU, rV = res(f"pWu{q}"), res(f"pWv{q}")
                d = rU.wdeps() + res(f"blk1_{pq}").rdeps() + (list(e4) if m < 2 else [])
                t = None
                for k in range(16):
                    for h in range(2):
                        t = xop("tensor", lambda e, k=k, h=h, moff=moff, pq=pq, q=q: e.matmul(
                            pWu[q][:, h * 512:(h + 1) * 512],
                            lhsT=blk1[pq][:, k * 256 + moff: k * 256 + moff + 128],
                            rhs=x2lT[:, k * 1024 + h * 512: k * 1024 + (h + 1) * 512],
                            start=(k == 0), stop=(k == 15)), deps=d)
                        d = []
                res(f"blk1_{pq}").read(t)
                rU.wrote(t)
                d = rV.wdeps() + res(f"blk3_{pq}").rdeps()
                for k in range(16):
                    for h in range(2):
                        t = xop("tensor", lambda e, k=k, h=h, moff=moff, pq=pq, q=q: e.matmul(
                            pWv[q][:, h * 512:(h + 1) * 512],
                            lhsT=blk3[pq][:, k * 256 + moff: k * 256 + moff + 128],
                            rhs=x2lT[:, k * 1024 + h * 512: k * 1024 + (h + 1) * 512],
                            start=(k == 0), stop=(k == 15)), deps=d)
                        d = []
                res(f"blk3_{pq}").read(t)
                rV.wrote(t)
                sl = xop("scalar", lambda e, q=q: e.activation(
                    out=sg4[q], in_=pWu[q], func=AF.Silu),
                    deps=rU.rdeps() + res(f"sg{q}").wdeps())
                rU.read(sl)
                res(f"sg{q}").wrote(sl)
                ml = xop("vector", lambda e, q=q: e.tensor_tensor(
                    out=hid4[q], in0=sg4[q], in1=pWv[q], op=ALU.mult),
                    deps=[sl] + rV.rdeps() + res(f"hid{q}").wdeps())
                rV.read(ml)
                res(f"sg{q}").read(ml)
                res(f"hid{q}").wrote(ml)
                d = rU.wdeps()
                for h in range(2):
                    t = xop("tensor", lambda e, h=h, q=q: e.matmul(
                        pWu[q][:, h * 512:(h + 1) * 512], lhsT=h128t[:],
                        rhs=hid4[q][:, h * 512:(h + 1) * 512], start=True, stop=True),
                        deps=d + [ml])
                    d = []
                res(f"hid{q}").read(t)
                rU.wrote(t)
                ev = xop("scalar", lambda e, q=q: e.activation(
                    out=yt4[q], in_=pWu[q], func=AF.Copy, scale=1.0 / 128),
                    deps=rU.rdeps() + res(f"yt{q}").wdeps())
                rU.read(ev)
                st = xdma("sync", lambda e, m=m, q=q: e.dma_start(out=ybuf_hbm[m], in_=yt4[q]),
                          deps=[ev])
                res(f"yt{q}").wrote(st)
                ytoks.append(st)
            if mp + 2 < 32:
                load_blk(mp + 2)
        if dbg >= 3:
            ch.serial_last = ch.all_tokens()
            for mi in range(8):
                O("gpsimd", lambda e, mi=mi: e.dma_start(
                    out=dbg_aps["d_y1"][mi * 128:(mi + 1) * 128, :], in_=ybuf_hbm[mi]), "dma")
        pool.reset(m3)

        yr2 = [pool.alloc(64, 2048, BF16) for _ in range(2)]
        h2t2 = [pool.alloc(64, 2048, BF16) for _ in range(2)]
        h2_all = []
        for jp in range(64):
            q = jp % 2
            rl = res(f"yr{q}")
            tl = xdma("sync", lambda e, jp=jp, q=q: e.dma_start(
                out=yr2[q].rearrange("a (j p) -> a j p", j=2),
                in_=ybuf_hbm[:, 2 * jp:2 * jp + 2, :]),
                deps=rl.wdeps() + (ytoks if jp < 2 else []))
            rl.wrote(tl)
            bk = res(f"pH64{q}")
            pH = psU if q == 0 else psV
            d = bk.wdeps() + [tl]
            t = None
            for h in range(4):
                t = xop("tensor", lambda e, h=h, q=q, pH=pH: e.matmul(
                    pH[0:64, h * 512:(h + 1) * 512], lhsT=h64t[:],
                    rhs=yr2[q][:, h * 512:(h + 1) * 512], start=True, stop=True), deps=d)
                d = []
            rl.read(t)
            bk.wrote(t)
            ev = xop("scalar", lambda e, q=q, pH=pH: e.activation(
                out=h2t2[q], in_=pH[0:64, :], func=AF.Copy, scale=float(np.sqrt(2.0))),
                deps=bk.rdeps() + res(f"h2t{q}").wdeps())
            bk.read(ev)
            sts = []
            for jj in range(2):
                j = 2 * jp + jj
                sts.append(xdma("sync", lambda e, j=j, jj=jj, q=q: e.dma_start(
                    out=h2_hbm[:, j % 2, :, j // 2, :].rearrange("tt a tp -> a tt tp"),
                    in_=h2t2[q][:, jj * 1024:(jj + 1) * 1024].rearrange("a (tt tp) -> a tt tp", tt=8)),
                    deps=[ev]))
            res(f"h2t{q}").wrote(*sts)
            h2_all.extend(sts)
        h2toks = h2_all
        if dbg >= 3:
            ch.serial_last = ch.all_tokens()
            O("gpsimd", lambda e: e.dma_start(
                out=dbg_aps["d_h2"][:], in_=h2_hbm[0].rearrange("h a m p -> (h a) (m p)")), "dma")
        pool.reset(m3)

        ffob = pool.alloc(128, 8 * 2048, BF16)
        m5 = pool.mark()
        w2q = [pool.alloc(128, 32 * 512, BF16) for _ in range(3)]
        h2blk = [pool.alloc(128, 8192, BF16) for _ in range(2)]
        w2v = ag_w2t.rearrange("(a b) d -> b a d", a=64)
        bW2 = [psV[:, 0:512], psV[:, 512:1024], psV[:, 1024:1536], psV[:, 1536:2048]]

        def load_w2h(dq, kh):
            qq = (dq * 2 + kh) % 3
            r = res(f"w2q{qq}")
            toks = [xdma("gpsimd", lambda e, dq=dq, k=k, kh=kh, qq=qq: e.dma_start(
                out=w2q[qq][:, (k - kh * 32) * 512:(k - kh * 32 + 1) * 512],
                in_=w2v[2 * k:2 * k + 2, :, dq * 512:(dq + 1) * 512]),
                deps=(r.wdeps() + list(h2toks)) if k == kh * 32 else [])
                for k in range(kh * 32, kh * 32 + 32)]
            r.wrote(*toks)

        load_w2h(0, 0)
        load_w2h(0, 1)
        w2ev = []
        for dq in range(4):
            if dq + 1 < 4:
                load_w2h(dq + 1, 0)
            q0 = (dq * 2) % 3
            q1 = (dq * 2 + 1) % 3
            for tt in range(8):
                qb = res(f"h2b{tt % 2}")
                tl = xdma("sync", lambda e, tt=tt: e.dma_start(
                    out=h2blk[tt % 2], in_=h2_hbm[tt].rearrange("h a m p -> (h a) (m p)")),
                    deps=qb.wdeps() + (h2toks if dq == 0 and tt < 2 else []))
                qb.wrote(tl)
                bk = res(f"pW2{tt % 4}")
                bank = bW2[tt % 4]
                d = bk.wdeps() + [tl] + res(f"w2q{q0}").rdeps() + res(f"w2q{q1}").rdeps()
                t = None
                for k in range(64):
                    qq = q0 if k < 32 else q1
                    t = xop("tensor", lambda e, k=k, qq=qq, tt=tt, bank=bank: e.matmul(
                        bank, lhsT=h2blk[tt % 2][:, k * 128:(k + 1) * 128],
                        rhs=w2q[qq][:, (k % 32) * 512:(k % 32 + 1) * 512],
                        start=(k == 0), stop=(k == 63)), deps=d)
                    d = []
                qb.read(t)
                res(f"w2q{q0}").read(t)
                res(f"w2q{q1}").read(t)
                bk.wrote(t)
                ev = xop("scalar", lambda e, dq=dq, tt=tt, bank=bank: e.copy(
                    out=ffob[:, tt * 2048 + dq * 512: tt * 2048 + (dq + 1) * 512], in_=bank),
                    deps=bk.rdeps())
                bk.read(ev)
                w2ev.append(ev)
            if dq + 1 < 4:
                load_w2h(dq + 1, 1)
        ch.serial_last = tuple(w2ev) + ch.all_tokens()
        if dbg >= 3:
            O("gpsimd", lambda e: e.dma_start(out=dbg_aps["d_ffo"][:], in_=ffob), "dma")
        pool.reset(m5)
        ff_f = pool.alloc(128, D, F32)
        x2t4 = pool.alloc(128, D, F32)
        outt = pool.alloc(128, D, F32)
        scr4 = pool.alloc(128, D, F32)
        st4 = pool.alloc(128, 8, F32)
        for tt in range(8):
            O("scalar", lambda e, tt=tt: e.copy(out=ff_f, in_=ffob[:, tt * 2048:(tt + 1) * 2048]))
            O("sync", lambda e, tt=tt: e.dma_start(out=x2t4, in_=x2_hbm[tt * 128:(tt + 1) * 128, :]), "dma")
            ln_tile(ff_f, None, scr4, st4, out_f32=outt, res=x2t4)
            O("sync", lambda e, tt=tt: e.dma_start(out=out_sh[tt * 128:(tt + 1) * 128, :], in_=outt), "dma")

        ch.emit(block, drain=ch.all_tokens())

    return nc, dbg_aps


def make_inputs(core, inp):
    c = core
    qc = slice(256 * c, 256 * (c + 1))
    kc = slice(2048 + 256 * c, 2048 + 256 * (c + 1))
    vc = slice(4096 + 512 * c, 4096 + 512 * (c + 1))
    wqkv = np.asarray(inp["L0_attn_w_qkv"], dtype=np.float32)
    conv = np.asarray(inp["L0_attn_conv_w"], dtype=np.float32)
    wq = np.concatenate([wqkv[:, qc], wqkv[:, kc], wqkv[:, vc]], axis=1)
    cw = np.concatenate([conv[:, qc], conv[:, kc], conv[:, vc]], axis=1)
    x = np.asarray(inp["L0_attn_x"], dtype=np.float32).reshape(BT, D)
    return {
        "x_sh": np.ascontiguousarray(x[c * TSH:(c + 1) * TSH]),
        "wq_sl": np.ascontiguousarray(wq),
        "wz_sl": np.ascontiguousarray(np.asarray(inp["L0_attn_w_z"], dtype=np.float32)[:, 512 * c:512 * (c + 1)]),
        "wba_sl": np.ascontiguousarray(np.concatenate(
            [np.asarray(inp["L0_attn_w_b"], dtype=np.float32)[:, 4 * c:4 * (c + 1)],
             np.asarray(inp["L0_attn_w_a"], dtype=np.float32)[:, 4 * c:4 * (c + 1)]], axis=1)),
        "cwT_sl": np.ascontiguousarray(cw.T),
        "dta8": np.ascontiguousarray(np.concatenate(
            [np.zeros(4, np.float32), np.asarray(inp["L0_attn_dt_bias"], dtype=np.float32)[4 * c:4 * (c + 1)]])[:, None]),
        "alog8": np.ascontiguousarray(np.concatenate(
            [np.zeros(4, np.float32), np.asarray(inp["L0_attn_A_log"], dtype=np.float32)[4 * c:4 * (c + 1)]])[:, None]),
        "normw": np.ascontiguousarray(np.asarray(inp["L0_attn_norm_w"], dtype=np.float32)[:, None]),
        "wo_sl": np.ascontiguousarray(np.asarray(inp["L0_attn_w_out"], dtype=np.float32)[512 * c:512 * (c + 1), :]),
        "w1_sl": np.ascontiguousarray(np.asarray(inp["L0_ffn_W1"], dtype=np.float32)[1024 * c:1024 * (c + 1), :]),
        "w3_sl": np.ascontiguousarray(np.asarray(inp["L0_ffn_W3"], dtype=np.float32)[1024 * c:1024 * (c + 1), :]),
        "w2_sl": np.ascontiguousarray(np.asarray(inp["L0_ffn_W2"], dtype=np.float32)[:, 1024 * c:1024 * (c + 1)]),
    }


def kernel(**inp):
    nc, _ = build(dbg=0)
    in_maps = [make_inputs(c, inp) for c in range(8)]
    res = run_bass_kernel_spmd(nc, in_maps, core_ids=list(range(8)))
    out = np.concatenate([res.results[c]["out_sh"] for c in range(8)], axis=0)
    return out.reshape(B, T, D).astype(np.float32)
